# version 17
# baseline (speedup 1.0000x reference)
import math
import numpy as np

import concourse.bass as bass
import concourse.mybir as mybir
from concourse.bass_utils import run_bass_kernel_spmd

F32 = mybir.dt.float32
BF16 = mybir.dt.bfloat16
FP8 = mybir.dt.float8e4
AF = mybir.ActivationFunctionType
ALU = mybir.AluOpType
AX = mybir.AxisListType

D = 1024
NCH = 8
DK = 64
DH = 128
DL = 256
E = 16
BIG = 1.0e30
EPS_LN = 1e-5
C_DECAY = 0.01 * 0.01
SC_L0 = 1.0 / 8.0
SC_FFN = 1.0 / math.sqrt(128.0)
SC_L1 = 1.0 / (math.sqrt(1024.0) + 1e-8)
POOL = 15360


class Cfg:
    def __init__(self, T=2048, H=16, n_seq=2, n_cores=8, debug=False):
        self.T = T
        self.TT = T // 128
        self.H = H
        self.n_seq = n_seq
        self.n_cores = n_cores
        self.debug = debug
        self.use_cc = n_cores > 1
        self.detect = False
        assert T % 512 == 0 and H % 4 == 0


def rope_perm(H):
    perm = []
    for g in range(H // 4):
        for par in (0, 1):
            for h in range(4 * g, 4 * g + 4):
                for f in range(32):
                    perm.append(64 * h + 2 * f + par)
    return np.array(perm)


def host_constants(cfg):
    T = cfg.T
    c = {}
    c["ident"] = np.eye(128, dtype=np.float32)
    i = np.arange(128)
    c["causmask"] = np.where(i[None, :] > i[:, None], -BIG, 0.0).astype(np.float32)
    c["causmT"] = np.ascontiguousarray(c["causmask"].T)
    c["dcorr"] = np.exp(-2.0 * C_DECAY *
                        np.maximum(i[None, :] - i[:, None], 0)).astype(np.float32)
    c["dcorrT"] = np.ascontiguousarray(c["dcorr"].T)
    powers = np.arange(0, DK, 2, dtype=np.float32) / np.float32(DK)
    freqs = (np.float32(1.0) / (np.float32(10000.0) ** powers)).astype(np.float32)
    t = np.arange(T, dtype=np.float32)
    ang = np.outer(t, freqs).astype(np.float32)
    c["ctab"] = np.ascontiguousarray(np.tile(np.cos(ang).astype(np.float32).T,
                                             (4, 1)))
    c["stab"] = np.ascontiguousarray(np.tile(np.sin(ang).astype(np.float32).T,
                                             (4, 1)))
    pos = np.arange(T, dtype=np.float64).reshape(cfg.TT, 128).T
    c["eplus"] = np.ascontiguousarray(np.exp(C_DECAY * pos).astype(np.float32))
    c["eminus"] = np.ascontiguousarray(np.exp(-C_DECAY * pos).astype(np.float32))
    c["escA"] = np.ascontiguousarray((SC_L1 * np.exp(-C_DECAY * pos))
                                     .astype(np.float32))
    c["escB"] = np.ascontiguousarray((SC_L1 * np.exp(C_DECAY * pos))
                                     .astype(np.float32))
    c["e2plus"] = np.ascontiguousarray(np.exp(2.0 * C_DECAY * pos)
                                       .astype(np.float32))
    return c


def host_weights(cfg, w):
    H = cfg.H
    perm = rope_perm(H)
    o = {}
    for nm in ("wq", "wk"):
        a = np.ascontiguousarray(np.asarray(w[f"L0_attn_{nm}"]).T[:, perm])
        o[f"l0_{nm}g"] = np.ascontiguousarray(
            a.reshape(D, H // 4, 256).transpose(1, 0, 2))
    a = np.ascontiguousarray(np.asarray(w["L0_attn_wv"]).T)
    o["l0_wvp"] = np.ascontiguousarray(a.reshape(D, H // 2, 128).transpose(1, 0, 2))
    a = np.ascontiguousarray(np.asarray(w["L0_attn_wo"]).T)
    o["l0_wop"] = np.ascontiguousarray(a.reshape(H // 2, 128, D))
    o["l0_snT"] = np.ascontiguousarray(np.asarray(w["L0_ffn_sim_matrix"]).T)
    o["l0_gates"] = np.ascontiguousarray(np.asarray(w["L0_ffn_gates"])
                                         .reshape(1, E))
    o["l0_qkvp"] = np.ascontiguousarray(np.concatenate(
        [np.asarray(w["L0_ffn_q_proj"]), np.asarray(w["L0_ffn_k_proj"]),
         np.asarray(w["L0_ffn_v_proj"])], axis=2))
    o["l0_op"] = np.ascontiguousarray(w["L0_ffn_o_proj"])
    a = np.ascontiguousarray(w["L1_attn_Wq"])
    o["l1_wqg"] = np.ascontiguousarray(a.reshape(D, H // 2, 128).transpose(1, 0, 2))
    o["l1_wk"] = np.ascontiguousarray(w["L1_attn_Wk"])
    o["l1_wv"] = np.ascontiguousarray(w["L1_attn_Wv"])
    o["l1_wdown"] = np.ascontiguousarray(w["L1_attn_W_down_kv"])
    for nm, key in (("wupk", "L1_attn_W_up_k"), ("wupv", "L1_attn_W_up_v")):
        a = np.ascontiguousarray(w[key])
        o[f"l1_{nm}g"] = np.ascontiguousarray(
            a.reshape(DL, H // 2, 128).transpose(1, 0, 2))
    o["l1_wo"] = np.ascontiguousarray(w["L1_attn_Wo"])
    o["l1_wt"] = np.ascontiguousarray(np.asarray(w["L1_ffn_weight"]).T)
    o["l1_bias"] = np.ascontiguousarray(np.asarray(w["L1_ffn_bias"])
                                        .reshape(1, D))
    return o


class P:
    ENGS = ("sync", "tensor", "vector", "scalar", "gpsimd")

    def __init__(self, nc):
        self.nc = nc
        self.q = {e: [] for e in P.ENGS}
        self.sems = {}
        self.cnt = {}
        self.n_instr = 0

    def sem(self, name):
        if name not in self.sems:
            self.sems[name] = self.nc.alloc_semaphore(name)
            self.cnt[name] = 0
        return self.sems[name]

    def on(self, eng, fn):
        self.q[eng].append(fn)
        self.n_instr += 1

    def wait(self, eng, name, val):
        if val is None or val <= 0:
            return
        s = self.sem(name)
        self.on(eng, lambda e, s=s, v=val: e.wait_ge(s, v))

    def emit(self, eng, fn, sem=None, n=1):
        if sem is None:
            self.on(eng, fn)
            return None
        s = self.sem(sem)
        self.cnt[sem] += n
        tot = self.cnt[sem]
        self.on(eng, lambda e, fn=fn, s=s, n=n: fn(e).then_inc(s, n))
        return tot

    def chain(self, eng, fn, sem="v"):
        t = self.emit(eng, fn, sem)
        self.wait(eng, sem, t)
        return t

    def dma(self, eng, out, in_, sem="dW"):
        return self.emit(eng, lambda e, o=out, i=in_: e.dma_start(out=o, in_=i),
                         sem, 16)

    def barrier(self, tag=""):
        s = self.sem("bar")
        self.cnt["bar"] += 5
        tgt = self.cnt["bar"]
        for eng in P.ENGS:
            self.on(eng, lambda e, s=s: e.drain().then_inc(s, 1))
            self.on(eng, lambda e, s=s, t=tgt: e.wait_ge(s, t))

    def build(self):
        with self.nc.Block() as blk:
            for eng in P.ENGS:
                def body(e, eng=eng):
                    for fn in self.q[eng]:
                        fn(e)
                getattr(blk, eng)(body)


def build_program(cfg):
    T, TT, H = cfg.T, cfg.TT, cfg.H
    NG, NP = H // 4, H // 2
    SPAN = 512
    nc = bass.Bass("TRN2", target_bir_lowering=False,
                   detect_race_conditions=cfg.detect)

    dr = {}
    def din(name, shape):
        dr[name] = nc.dram_tensor(name, list(shape), F32, kind="ExternalInput").ap()
        return dr[name]

    x0 = din("x0", (cfg.n_seq, T, D))
    out = nc.dram_tensor("out", [cfg.n_seq, T, D], F32, kind="ExternalOutput").ap()
    for nm, shape in (
        ("ident", (128, 128)), ("causmask", (128, 128)), ("dcorr", (128, 128)),
        ("dcorrT", (128, 128)), ("e2plus", (128, TT)),
        ("causmT", (128, 128)),
        ("ctab", (128, T)), ("stab", (128, T)),
        ("eplus", (128, TT)), ("eminus", (128, TT)),
        ("escA", (128, TT)), ("escB", (128, TT)),
        ("l0_wqg", (NG, D, 256)), ("l0_wkg", (NG, D, 256)),
        ("l0_wvp", (NP, D, 128)), ("l0_wop", (NP, 128, D)),
        ("l0_snT", (16, D)), ("l0_gates", (1, E)),
        ("l0_qkvp", (E, D, 3 * DH)), ("l0_op", (E, DH, D)),
        ("l1_wqg", (NP, D, 128)), ("l1_wk", (D, D)), ("l1_wv", (D, D)),
        ("l1_wdown", (D, DL)), ("l1_wupkg", (NP, DL, 128)),
        ("l1_wupvg", (NP, DL, 128)), ("l1_wo", (D, D)),
        ("l1_wt", (D, D)), ("l1_bias", (1, D)),
    ):
        din(nm, shape)
    spill_xn = nc.dram_tensor("spill_xn", [128, NCH * T], F32).ap()
    spill_acc = nc.dram_tensor("spill_acc", [128, TT * D], F32).ap()
    cc_in = nc.dram_tensor("cc_in", [1, 8], F32)
    cc_out = nc.dram_tensor("cc_out", [1, 8], F32)

    dbg = {}
    if cfg.debug:
        for nm, shape in (
            ("d_xn0", (128, NCH * T)), ("d_w16", (128, E * TT)),
            ("d_accq", (128, TT * DH)), ("d_o", (128, TT * DH)),
            ("d_acc_ffn", (128, TT * D)), ("d_acc_l0", (128, TT * D)),
            ("d_latT", (128, 2 * T)), ("d_amax", (1, 8)),
            ("d_acc_at1", (128, TT * D)), ("d_xn1", (128, NCH * T)),
            ("d_pool0", (128, 513 * TT + 130)),
        ):
            dbg[nm] = nc.dram_tensor(nm, list(shape), F32,
                                     kind="ExternalOutput").ap()

    def sbt(name, shape, dt=F32):
        return nc.alloc_sbuf_tensor(name, list(shape), dt).ap()

    xn = sbt("xn", (128, NCH * T))
    acc = sbt("acc", (128, TT * D))
    pool = sbt("pool", (128, POOL))
    ident = sbt("identS", (128, 128))
    causm = sbt("causmS", (128, 128))
    dcorr = sbt("dcorrS", (128, 128))
    dcorrT = sbt("dcorrTS", (128, 128))
    causmT = sbt("causmTS", (128, 128))
    e2plus = sbt("e2plusS", (128, TT))
    eplus = sbt("eplusS", (128, TT))
    eminus = sbt("eminusS", (128, TT))
    escA = sbt("escAS", (128, TT))
    escB = sbt("escBS", (128, TT))
    w16 = sbt("w16S", (128, E * TT))
    rninv = sbt("rninvS", (128, TT))
    gmax = sbt("gmaxS", (128, 1))
    smallA = sbt("smallAS", (128, 64))
    smallB = sbt("smallBS", (128, 64))
    smallC = sbt("smallCS", (128, 512))
    onesr = sbt("onesrS", (1, 128))
    sg16 = sbt("sg16S", (128, 16))
    neg16 = sbt("neg16S", (128, 16))
    snN = sbt("snNS", (128, 16 * NCH))
    g1 = sbt("g1S", (1, 16))
    amax1 = sbt("amax1S", (1, 8))
    smallD = sbt("smallDS", (128, 512))
    onec = sbt("onecS", (128, 1))
    w8 = sbt("w8S", (128, NCH * D), FP8)
    bft = sbt("bftS", (128, 512), BF16)
    q8t = sbt("q8tS", (128, 256), FP8)
    psum = nc.alloc_psum_tensor("psA", [128, 4096], F32).ap()

    p = P(nc)

    def pl(off, n):
        return pool[:, off:off + n]

    def mm(out_ap, lhsT, rhs, start, stop, sem=None):
        return p.emit("tensor",
                      lambda e: e.matmul(out_ap, lhsT, rhs, start=start, stop=stop),
                      sem)

    def tr(out_ap, in_ap, idn=None, sem=None):
        i_ = ident if idn is None else idn
        return p.emit("tensor", lambda e: e.transpose(out_ap, in_ap, i_), sem)

    def vcp(out_ap, in_ap, sem=None, eng="vector"):
        return p.emit(eng, lambda e: e.tensor_copy(out=out_ap, in_=in_ap), sem)

    def ts(out_ap, in0, s1, s2, op0, op1=None, sem=None, eng="vector"):
        return p.emit(eng, lambda e: e.tensor_scalar(
            out=out_ap, in0=in0, scalar1=s1, scalar2=s2, op0=op0,
            op1=(ALU.bypass if op1 is None else op1)), sem)

    def tsc(out_ap, in0, s1, s2, op0, op1=None, eng="vector"):
        return p.chain(eng, lambda e: e.tensor_scalar(
            out=out_ap, in0=in0, scalar1=s1, scalar2=s2, op0=op0,
            op1=(ALU.bypass if op1 is None else op1)))

    def tt2c(out_ap, in0, in1, op, eng="vector"):
        return p.chain(eng, lambda e: e.tensor_tensor(
            out=out_ap, in0=in0, in1=in1, op=op))

    def rsumc(out_ap, in_ap):
        return p.chain("vector", lambda e: e.reduce_sum(out_ap, in_ap,
                                                        axis=AX.X))

    def rmaxc(out_ap, in_ap, negate=False, absval=False):
        kw = {}
        if negate:
            kw["negate"] = True
        if absval:
            kw["apply_absolute_value"] = True
        return p.chain("vector", lambda e: e.reduce_max(out_ap, in_ap,
                                                        axis=AX.X, **kw))

    def recipc(out_ap, in_ap):
        return p.chain("vector", lambda e: e.reciprocal(out_ap, in_ap))

    def sttc(out_ap, in0, scalar, in1, op0, op1):
        return p.chain("vector", lambda e: e.scalar_tensor_tensor(
            out=out_ap, in0=in0, scalar=scalar, in1=in1, op0=op0, op1=op1))

    def vcpc(out_ap, in_ap, eng="vector"):
        return p.chain(eng, lambda e: e.tensor_copy(out=out_ap, in_=in_ap))

    def tt2(out_ap, in0, in1, op, sem=None, eng="vector"):
        return p.emit(eng, lambda e: e.tensor_tensor(
            out=out_ap, in0=in0, in1=in1, op=op), sem)

    def stt(out_ap, in0, scalar, in1, op0, op1, sem=None):
        return p.emit("vector", lambda e: e.scalar_tensor_tensor(
            out=out_ap, in0=in0, scalar=scalar, in1=in1, op0=op0, op1=op1), sem)

    def act(out_ap, in_ap, func, bias=0.0, scale=1.0, accum=None, sem=None):
        kw = {"accum_out": accum} if accum is not None else {}
        return p.emit("scalar", lambda e: e.activation(
            out=out_ap, in_=in_ap, func=func, bias=bias, scale=scale, **kw), sem)

    def rmax(out_ap, in_ap, negate=False, absval=False, sem=None, ):
        kw = {}
        if negate:
            kw["negate"] = True
        if absval:
            kw["apply_absolute_value"] = True
        return p.emit("vector", lambda e: e.reduce_max(out_ap, in_ap, axis=AX.X,
                                                       **kw), sem)

    def rsum(out_ap, in_ap, sem=None):
        return p.emit("vector", lambda e: e.reduce_sum(out_ap, in_ap, axis=AX.X),
                      sem)

    def recip(out_ap, in_ap, sem=None):
        return p.emit("vector", lambda e: e.reciprocal(out_ap, in_ap), sem)

    def sel(out_ap, mask, on_t, on_f, sem=None):
        return p.emit("vector", lambda e: e.select(
            out=out_ap, mask=mask, on_true=on_t, on_false=on_f), sem)

    def memset(ap_, v, eng="vector", sem=None):
        return p.emit(eng, lambda e: e.memset(ap_, v), sem)

    def xnc(c):
        return xn[:, T * c:T * (c + 1)]

    PVB = psum[:, 3584:3648]
    PVF = psum[:, 3584:3712]

    def pipe2(jobs):
        prev = None
        for f, b in jobs:
            f()
            if prev is not None:
                prev()
            prev = b
        if prev is not None:
            prev()

    def pair_bound(srcs, sq_off, scale_negs, outcols):
        sq = pl(sq_off, 512)
        memset(smallB[:1, 30:30 + len(srcs)], 0.0, sem="v")
        p.wait("scalar", "v", p.cnt["v"])
        for j2, so in enumerate(srcs):
            for ch in range(T // 512):
                sb = act(sq, pl(so + 512 * ch, 512), AF.Square, sem="s")
                p.wait("tensor", "s", sb)
                nmm = mm(psum[:1, 3072:3584], onec, sq, True, True, sem="mm")
                p.wait("vector", "mm", nmm)
                rmaxc(smallA[:1, 16:17], psum[:1, 3072:3584])
                vx = tt2c(smallB[:1, 30 + j2:31 + j2],
                          smallB[:1, 30 + j2:31 + j2],
                          smallA[:1, 16:17], ALU.max)
                p.wait("scalar", "v", vx)
                p.wait("tensor", "v", vx)
        s4 = None
        for j2 in range(len(srcs)):
            s4 = act(smallA[:1, 17 + j2:18 + j2], smallB[:1, 30 + j2:31 + j2],
                     AF.Sqrt, sem="s")
        p.wait("vector", "s", s4)
        for scale_neg, outc in zip(scale_negs, outcols):
            if len(srcs) >= 2:
                tt2c(smallA[:1, 20:21], smallA[:1, 17:18], smallA[:1, 18:19],
                     ALU.mult)
            else:
                tt2c(smallA[:1, 20:21], smallA[:1, 17:18], smallA[:1, 19:20],
                     ALU.mult)
            v5 = tsc(smallA[:1, 21:22], smallA[:1, 20:21], scale_neg, None,
                     ALU.mult)
            p.wait("tensor", "v", v5)
            bb = mm(psum[:, 3584:3585], onesr, smallA[:1, 21:22], True, True,
                    sem="mm")
            p.wait("vector", "mm", bb)
            vb = vcp(smallB[:, outc:outc + 1], psum[:, 3584:3585], sem="v")
            p.wait("scalar", "v", vb)
            p.wait("tensor", "v", vb)

    for nm, dst in (("ident", ident), ("causmask", causm), ("dcorr", dcorr),
                    ("dcorrT", dcorrT), ("e2plus", e2plus),
                    ("causmT", causmT),
                    ("eplus", eplus), ("eminus", eminus),
                    ("escA", escA), ("escB", escB), ("l0_gates", g1)):
        p.dma("sync", dst, dr[nm], "dC")
    snT = pool[:16, 0:D]
    p.dma("sync", snT, dr["l0_snT"], "dC")
    vC = p.dma("sync", acc, x0[0].rearrange("(t p) d -> p t d", p=128), "dC")
    p.wait("vector", "dC", vC)
    memset(neg16, -BIG)
    memset(onesr, 1.0)
    memset(onec, 1.0)
    memset(gmax, 0.0)
    p.wait("scalar", "dC", vC)
    act(g1, g1, AF.Sigmoid)
    act(psum[:16, 512:512 + D], snT, AF.Square, accum=smallA[:16, 0:1])
    p.barrier()
    mm(psum[:, 0:16], onesr, g1, True, True)
    act(smallA[:16, 1:2], smallA[:16, 0:1], AF.Sqrt)
    p.barrier()
    vcp(sg16, psum[:, 0:16])
    ts(smallA[:16, 2:3], smallA[:16, 1:2], 1e-12, None, ALU.max)
    p.barrier()
    recip(smallA[:16, 3:4], smallA[:16, 2:3])
    p.barrier()
    ts(snT, snT, smallA[:16, 3:4], None, ALU.mult)
    p.barrier()
    for c in range(NCH):
        tr(psum[:, 1024 + 16 * c:1024 + 16 * (c + 1)],
           snT[:, 128 * c:128 * (c + 1)], ident[:16, :16])
    p.barrier()
    vcp(snN, psum[:, 1024:1024 + 16 * NCH])
    p.barrier()

    def phase_ln(layer, track_amax, dbg_tap=None):
        SQ = psum[:, 2048:2048 + D]
        TR = psum[:, 3072:3072 + 1024]
        xt = pl(POOL - 1024, 1024)
        for t in range(TT):
            a = acc[:, D * t:D * (t + 1)]
            rs, nm_, ssq = smallA[:, 0:1], smallA[:, 1:2], smallA[:, 2:3]
            rstd = smallA[:, 5:6]
            rsumc(rs, a)
            v2 = tsc(nm_, rs, -1.0 / D, None, ALU.mult)
            p.wait("scalar", "v", v2)
            s1 = act(SQ, a, AF.Square, bias=nm_, accum=ssq, sem="s")
            p.wait("vector", "s", s1)
            v3 = tsc(smallA[:, 3:4], ssq, 1.0 / D, EPS_LN, ALU.mult, ALU.add)
            p.wait("scalar", "v", v3)
            s2 = act(smallA[:, 4:5], smallA[:, 3:4], AF.Sqrt, sem="s")
            if layer == 0:
                s2 = act(smallA[:, 8:9], ssq, AF.Sqrt, sem="s")
            p.wait("vector", "s", s2)
            recipc(rstd, smallA[:, 4:5])
            if layer == 0:
                tt2c(smallA[:, 6:7], smallA[:, 8:9], rstd, ALU.mult)
                tsc(smallA[:, 7:8], smallA[:, 6:7], 1e-12, None, ALU.max)
                recipc(rninv[:, t:t + 1], smallA[:, 7:8])
            if track_amax:
                rmaxc(smallB[:, 0:1], SQ)
                tt2c(smallB[:, 1:2], rstd, rstd, ALU.mult)
                tt2c(smallB[:, 2:3], smallB[:, 0:1], smallB[:, 1:2], ALU.mult)
                tt2c(gmax, gmax, smallB[:, 2:3], ALU.max)
            v4 = ts(xt, a, nm_, rstd, ALU.add, ALU.mult, sem="v")
            p.wait("tensor", "v", v4)
            tms = None
            for c in range(NCH):
                tms = tr(TR[:, 128 * c:128 * (c + 1)], xt[:, 128 * c:128 * (c + 1)],
                         sem="mm" if c == NCH - 1 else None)
            p.wait("vector", "mm", tms)
            src = psum[:, 3072:3072 + 1024].rearrange("p (c r) -> p c r", r=128)
            dst = xn.rearrange("p (c tk) -> p c tk", c=NCH)[:, :,
                                                            128 * t:128 * (t + 1)]
            v5 = vcp(dst, src, sem="v")
            p.wait("tensor", "v", v5)
        p.barrier()
        if dbg_tap is not None:
            dd = p.dma("sync", dbg_tap, xn, "dD")
            p.wait("sync", "dD", dd)

    def phase_l0ffn(seq):
        ACCQ, ACCK = 0, TT * DH
        WQKV = 3 * TT * DH
        WSZ = 3 * NCH * DH
        KT = 2 * TT * DH
        QTf = KT + T
        OBUF = QTf + T
        EXPB = OBUF + TT * DH
        ACCV = WQKV + 2 * WSZ
        GS = WQKV
        vz0 = memset(pool[:, ACCV:ACCV + 129 * TT].rearrange(
            "p (b c) -> p b c", c=129)[:, :, 0:1], 1.0, sem="v")
        p.wait("scalar", "v", vz0)
        p.wait("tensor", "v", vz0)
        for t in range(TT):
            lg = psum[:16, 0:128]
            mt = None
            for c in range(NCH):
                mt = mm(lg, snN[:, 16 * c:16 * (c + 1)],
                        xnc(c)[:, 128 * t:128 * (t + 1)], c == 0, c == NCH - 1,
                        sem="mm" if c == NCH - 1 else None)
            p.wait("vector", "mm", mt)
            v1 = vcp(smallC[:16, 0:128], lg, sem="v")
            p.wait("tensor", "v", v1)
            t1 = tr(psum[:, 512:512 + 16], smallC[:16, 0:128], ident[:16, :16],
                    sem="mm")
            p.wait("vector", "mm", t1)
            lt = pl(GS, 16)
            g_ = pl(GS + 16, 16)
            hc = pl(GS + 32, 16)
            sl = pl(GS + 48, 16)
            e16 = pl(GS + 64, 16)
            fb = pl(GS + 80, 16)
            l2 = pl(GS + 96, 16)
            tsc(lt, psum[:, 512:528], rninv[:, t:t + 1], None, ALU.mult)
            tt2c(lt, lt, sg16, ALU.subtract)
            tsc(g_, lt, 0.0, None, ALU.max)
            tsc(hc, lt, 0.0, None, ALU.is_gt)
            rsumc(smallB[:, 3:4], hc)
            tsc(smallB[:, 4:5], smallB[:, 3:4], 0.5, None, ALU.is_lt)
            rmaxc(smallB[:, 5:6], lt)
            tsc(fb, lt, smallB[:, 5:6], None, ALU.is_equal)
            sttc(l2, fb, -BIG, lt, ALU.mult, ALU.add)
            rmaxc(smallB[:, 6:7], l2)
            tsc(fb, lt, smallB[:, 6:7], None, ALU.is_ge)
            sttc(sl, fb, smallB[:, 4:5], hc, ALU.mult, ALU.add)
            tt2c(l2, g_, sl, ALU.mult)
            tsc(fb, sl, 1.0, BIG, ALU.subtract, ALU.mult)
            tt2c(e16, l2, fb, ALU.add)
            v2 = rmaxc(smallB[:, 7:8], e16, negate=True)
            p.wait("scalar", "v", v2)
            s1 = act(e16, e16, AF.Exp, bias=smallB[:, 7:8], accum=smallB[:, 8:9],
                     sem="s")
            p.wait("vector", "s", s1)
            recipc(smallB[:, 9:10], smallB[:, 8:9])
            vlast = ts(w16[:, E * t:E * (t + 1)], e16, smallB[:, 9:10], None,
                       ALU.mult, sem="v")
            p.wait("tensor", "v", vlast)
        p.barrier()
        if cfg.debug and seq == 0:
            dd = p.dma("sync", dbg["d_w16"], w16, "dD")
            p.wait("sync", "dD", dd)

        pqkvO = (psum[:, 0:384], psum[:, 512:896])
        mm_done = {}
        eevd = {}
        epend = []
        for ei in range(E):
            buf = WQKV + (ei % 2) * WSZ
            sname = "dWa" if ei % 2 == 0 else "dWb"
            if ei >= 2:
                p.wait("sync", "mm", mm_done[ei - 2])
            ldt = p.dma("sync", pl(buf, WSZ),
                        dr["l0_qkvp"][ei].rearrange("(c p) h -> p c h", p=128),
                        sname)
            p.wait("tensor", sname, ldt)
            for t in range(TT):
                b = (ei * TT + t) % 2
                pqkv = pqkvO[b]
                p.wait("tensor", "ev", eevd.get(b))
                last = None
                for c in range(NCH):
                    lhs = xnc(c)[:, 128 * t:128 * (t + 1)]
                    last = mm(pqkv, lhs, pl(buf + 3 * DH * c, 3 * DH),
                              c == 0, c == NCH - 1,
                              sem="mm" if c == NCH - 1 else None)
                mm_done[ei] = last

                def evac(ei=ei, t=t, pqkv=pqkv, last=last, b=b):
                    p.wait("vector", "mm", last)
                    wcol = w16[:, E * t + ei:E * t + ei + 1]
                    for j, accb in enumerate((ACCQ, ACCK, ACCV)):
                        ps_ = pqkv[:, 128 * j:128 * (j + 1)]
                        if j == 2:
                            dst = pl(accb + 129 * t + 1, DH)
                        else:
                            dst = pl(accb + DH * t, DH)
                        if ei == 0:
                            eevd[b] = ts(dst, ps_, wcol, None, ALU.mult,
                                         sem="ev" if j == 2 else None)
                        else:
                            eevd[b] = stt(dst, ps_, wcol, dst, ALU.mult,
                                          ALU.add, sem="ev" if j == 2 else None)
                epend.append(evac)
                if len(epend) >= 2:
                    epend.pop(0)()
        for f in epend:
            f()
        p.barrier()
        if cfg.debug and seq == 0:
            dd = p.dma("sync", dbg["d_accq"], pl(ACCQ, TT * DH), "dD")
            p.wait("sync", "dD", dd)

        for srcb, dstb in ((ACCK, KT), (ACCQ, QTf)):
            tpend = []
            tev_last = None
            for t8 in range((TT + 7) // 8):
                nblk = min(8, TT - 8 * t8)
                base = 2048 + 1024 * (t8 % 2)
                tl = None
                for i in range(nblk):
                    kx = 8 * t8 + i
                    tl = tr(psum[:, base + 128 * i:base + 128 * (i + 1)],
                            pl(srcb + DH * kx, DH),
                            sem="mm" if i == nblk - 1 else None)

                def evac(t8=t8, nblk=nblk, base=base, tl=tl, dstb=dstb):
                    p.wait("vector", "mm", tl)
                    return vcp(pl(dstb + 1024 * t8, 128 * nblk),
                               psum[:, base:base + 128 * nblk], sem="ev")
                tpend.append(evac)
                if len(tpend) >= 2:
                    tev_last = tpend.pop(0)()
            for f in tpend:
                tev_last = f()
            p.wait("tensor", "ev", tev_last)
        p.barrier()
        SQB = psum[:, 3072:3200]
        memset(smallB[:, 24:26], 0.0, sem="v")
        p.wait("scalar", "v", p.cnt["v"])
        for j2, srcb in enumerate((ACCQ, ACCK)):
            for t in range(TT):
                sb = act(SQB, pl(srcb + DH * t, DH), AF.Square,
                         accum=smallB[:, 28:29], sem="s")
                p.wait("vector", "s", sb)
                tt2c(smallB[:, 24 + j2:25 + j2], smallB[:, 24 + j2:25 + j2],
                     smallB[:, 28:29], ALU.max)
                p.wait("scalar", "v", p.cnt["v"])
        vq = p.cnt["v"]
        p.wait("tensor", "v", vq)
        t1 = tr(psum[:1, 3200:3328], smallB[:, 24:25], sem="mm")
        t2 = tr(psum[:1, 3328:3456], smallB[:, 25:26], sem="mm")
        p.wait("vector", "mm", t2)
        vcpc(smallC[:1, 0:128], psum[:1, 3200:3328])
        rmaxc(smallA[:1, 16:17], smallC[:1, 0:128])
        vcpc(smallC[:1, 0:128], psum[:1, 3328:3456])
        v3 = rmaxc(smallA[:1, 17:18], smallC[:1, 0:128])
        p.wait("scalar", "v", v3)
        act(smallA[:1, 18:19], smallA[:1, 16:17], AF.Sqrt)
        s4 = act(smallA[:1, 19:20], smallA[:1, 17:18], AF.Sqrt, sem="s")
        p.wait("vector", "s", s4)
        tt2c(smallA[:1, 20:21], smallA[:1, 18:19], smallA[:1, 19:20], ALU.mult)
        v5 = tsc(smallA[:1, 21:22], smallA[:1, 20:21], -SC_FFN, None, ALU.mult)
        p.wait("tensor", "v", v5)
        bb = mm(psum[:, 3456:3457], onesr, smallA[:1, 21:22], True, True,
                sem="mm")
        p.wait("vector", "mm", bb)
        vb = vcp(smallB[:, 27:28], psum[:, 3456:3457], sem="v")
        p.wait("scalar", "v", vb)
        nbnd = smallB[:, 27:28]
        p.barrier()

        sexp = {0: None, 1: None, 2: None}
        pvdone = {0: None, 1: None, 2: None}
        scb = (0, 512, 3072)
        gevd = {}
        gpend = []
        pvlast = {}
        for qtg in range(TT // 4):
            for f in gpend:
                f()
            gpend = []
            pends = []
            for kc in range(TT):
                par = (qtg * TT + kc) % 3
                SCh = psum[:, scb[par]:scb[par] + 512]
                p.wait("tensor", "s", sexp[par])
                smm = mm(SCh, pl(KT + 128 * kc, 128),
                         pl(QTf + 512 * qtg, 512), True, True, sem="mm")
                p.wait("scalar", "mm", smm)
                p.wait("scalar", "mm", pvdone[par])
                s1 = act(pl(EXPB + 512 * par, 512), SCh, AF.Exp, bias=nbnd,
                         scale=SC_FFN, sem="s")
                sexp[par] = s1

                def pend_pv(kc=kc, par=par, s1=s1):
                    p.wait("tensor", "s", s1)
                    for qi in range(4):
                        pvb = psum[:, 1024 + 512 * qi:1024 + 512 * qi + 129]
                        if kc == 0:
                            p.wait("tensor", "ev", gevd.get(qi))
                        pvlast[qi] = mm(
                            pvb, pool[:, EXPB + 512 * par + 128 * qi:
                                      EXPB + 512 * par + 128 * (qi + 1)],
                            pl(ACCV + 129 * kc, 129), kc == 0, kc == TT - 1,
                            sem="mm" if qi == 3 else None)
                    pvdone[par] = pvlast[3]
                pends.append(pend_pv)
                if len(pends) >= 2:
                    pends.pop(0)()
            for f in pends:
                f()
            for qi in range(4):
                def evac(qi=qi, qt=4 * qtg + qi, last=pvlast[3],
                         pvb=psum[:, 1024 + 512 * qi:1024 + 512 * qi + 129]):
                    p.wait("vector", "mm", last)
                    recipc(smallA[:, 8 + qi:9 + qi], pvb[:, 0:1])
                    gevd[qi] = ts(pl(OBUF + DH * qt, DH), pvb[:, 1:129],
                                  smallA[:, 8 + qi:9 + qi], None, ALU.mult,
                                  sem="ev")
                gpend.append(evac)
        for f in gpend:
            f()
        for qi in range(4):
            p.wait("tensor", "ev", gevd.get(qi))
        p.barrier()
        if cfg.debug and seq == 0:
            dd = p.dma("sync", dbg["d_o"], pl(OBUF, TT * DH), "dD")
            p.wait("sync", "dD", dd)

        ZET = pl(EXPB, 256)
        OPW = 0
        OPSZ = 2 * D
        ztmp = pl(EXPB + 256, 128)
        mmo_done = {}
        opb = [0]
        opevd = {}
        oppend = []
        for es in range(E // 2):
            ob = OPW
            if es >= 1:
                p.wait("sync", "mm", mmo_done[es - 1])
            ld2 = p.dma("sync", pl(ob, OPSZ),
                        dr["l0_op"][2 * es:2 * es + 2].rearrange("e p j -> p e j"),
                        "dOa")
            p.wait("tensor", "dOa", ld2)
            for t in range(TT):
                for k in range(2):
                    ei = 2 * es + k
                    wcol = w16[:, E * t + ei:E * t + ei + 1]
                    vz = ts(ztmp, pl(OBUF + DH * t, DH), wcol, None, ALU.mult,
                            sem="v")
                    p.wait("tensor", "v", vz)
                    tz = tr(psum[:, 2048 + 128 * k:2048 + 128 * (k + 1)], ztmp,
                            sem="mm")
                    p.wait("vector", "mm", tz)
                    ez = vcp(ZET[:, 128 * k:128 * (k + 1)],
                             psum[:, 2048 + 128 * k:2048 + 128 * (k + 1)],
                             sem="ev")
                    p.wait("tensor", "ev", ez)
                for ncol in range(2):
                    b = opb[0] % 2
                    opb[0] += 1
                    po = psum[:, 512 * b:512 * (b + 1)]
                    p.wait("tensor", "ev", opevd.get(b))
                    last = None
                    for k in range(2):
                        last = mm(po, ZET[:, 128 * k:128 * (k + 1)],
                                  pl(ob + D * k + 512 * ncol, 512),
                                  k == 0, k == 1, sem="mm" if k == 1 else None)
                    if t == TT - 1 and ncol == 1:
                        mmo_done[es] = last

                    def evac(t=t, ncol=ncol, po=po, last=last, b=b):
                        p.wait("vector", "mm", last)
                        dst = acc[:, D * t + 512 * ncol:
                                  D * t + 512 * (ncol + 1)]
                        opevd[b] = tt2(dst, po, dst, ALU.add, sem="ev")
                    oppend.append(evac)
                    if len(oppend) >= 2:
                        oppend.pop(0)()
        for f in oppend:
            f()
        p.barrier()
        if cfg.debug and seq == 0:
            dd = p.dma("sync", dbg["d_acc_ffn"], acc, "dD")
            p.wait("sync", "dD", dd)

    def phase_l0attn(seq):
        QF, KF = 0, 4096
        AT2 = 8192
        VP = 10240
        PB = 12304
        WSL = 13312
        EXPB0 = WSL
        TBC, TBS = 12288, 12800
        S0, S1_, S2, S3 = 8192, 8704, 9216, 9728
        free_wsl = None
        for g in range(NG):
            for wsrc, dstoff in ((dr["l0_wqg"], QF), (dr["l0_wkg"], KF)):
                p.wait("sync", "mm", free_wsl)
                lw = p.dma("sync", pl(WSL, 2048),
                           wsrc[g].rearrange("(c p) m -> p c m", p=128), "dW")
                p.wait("tensor", "dW", lw)
                for sp in range(0, T, SPAN):
                    p.dma("sync", pl(TBC, SPAN), dr["ctab"][:, sp:sp + SPAN], "dT")
                    lt_ = p.dma("sync", pl(TBS, SPAN), dr["stab"][:, sp:sp + SPAN],
                                "dT")
                    for mc in range(2):
                        last = None
                        for j in range(0, SPAN, 512):
                            for c in range(NCH):
                                last = mm(psum[:, 512 * mc:512 * mc + 512],
                                          pl(WSL + 256 * c + 128 * mc, 128),
                                          xnc(c)[:, sp + j:sp + j + 512],
                                          c == 0, c == NCH - 1,
                                          sem="mm" if c == NCH - 1 else None)
                            free_wsl = last
                            p.wait("vector", "mm", last)
                            dst = pl((S0 if mc == 0 else S1_) + j, 512)
                            ev = vcp(dst, psum[:, 512 * mc:512 * mc + 512],
                                     sem="ev")
                            p.wait("tensor", "ev", ev)
                    p.wait("vector", "dT", lt_)
                    p.wait("vector", "ev", p.cnt["ev"])
                    PE_, PO_ = pl(S0, SPAN), pl(S1_, SPAN)
                    tbc, tbs = pl(TBC, SPAN), pl(TBS, SPAN)
                    tt2(pl(S2, SPAN), PE_, tbc, ALU.mult)
                    tt2c(pl(S3, SPAN), PO_, tbs, ALU.mult)
                    tt2c(pl(S2, SPAN), pl(S2, SPAN), pl(S3, SPAN), ALU.subtract)
                    tt2(pl(S3, SPAN), PE_, tbs, ALU.mult)
                    tt2c(PO_, PO_, tbc, ALU.mult)
                    vr = p.chain("vector", lambda e, a_=pl(S3, SPAN), b_=PO_:
                                 e.tensor_tensor(out=b_, in0=a_, in1=b_,
                                                 op=ALU.add))
                    p.wait("sync", "v", vr)
                    rr = None
                    for hp in range(4):
                        pair_off = dstoff + 2048 * (hp // 2)
                        row = 64 * (hp % 2)
                        for half, srcoff in ((0, S2), (32, S1_)):
                            srcap = pool[32 * hp:32 * hp + 32,
                                         srcoff:srcoff + SPAN]
                            dstap = pool[row + half:row + half + 32,
                                         pair_off + sp:pair_off + sp + SPAN]
                            rr = p.dma("sync", dstap, srcap, "dR")
                    p.wait("vector", "dR", rr)
                    p.wait("tensor", "dR", rr)
            for pr2 in range(2):
                pr = 2 * g + pr2
                p.wait("sync", "mm", free_wsl)
                lv = p.dma("sync", pl(WSL, 1024),
                           dr["l0_wvp"][pr].rearrange("(c p) m -> p c m", p=128),
                           "dW")
                p.wait("tensor", "dW", lv)
                p.wait("vector", "mm", free_wsl)
                vzp = memset(pool[:, VP:VP + 129 * TT].rearrange(
                    "p (b c) -> p b c", c=129)[:, :, 64:65], 1.0, sem="v")
                p.wait("scalar", "v", vzp)
                p.wait("tensor", "v", vzp)
                pvvO = (psum[:, 1024:1152], psum[:, 1536:1664])
                vevd = {}
                vpend = []
                for t in range(TT):
                    b = t % 2
                    pvv = pvvO[b]
                    p.wait("tensor", "ev", vevd.get(b))
                    last = None
                    for c in range(NCH):
                        last = mm(pvv, xnc(c)[:, 128 * t:128 * (t + 1)],
                                  pl(WSL + 128 * c, 128), c == 0, c == NCH - 1,
                                  sem="mm" if c == NCH - 1 else None)
                    free_wsl = last

                    def evac(t=t, pvv=pvv, last=last, b=b):
                        p.wait("vector", "mm", last)
                        vcp(pl(VP + 129 * t, 64), pvv[:, 0:64])
                        vevd[b] = vcp(pl(VP + 129 * t + 65, 64),
                                      pvv[:, 64:128], sem="ev")
                    vpend.append(evac)
                    if len(vpend) >= 2:
                        vpend.pop(0)()
                for f in vpend:
                    f()
                p.wait("tensor", "ev", vevd.get(0))
                p.wait("tensor", "ev", vevd.get(1))
                pair_bound((QF + 2048 * pr2, KF + 2048 * pr2), PB,
                           (-SC_L0,), (27,))
                nbnd = smallB[:, 27:28]
                p.wait("tensor", "s", p.cnt["s"])
                gevd = {}
                for hh in range(2):
                    qoff = QF + 2048 * pr2
                    koff = KF + 2048 * pr2
                    qrow = 64 * hh
                    vz_off = 64 * hh
                    vo_off = 0 if hh == 0 else 1
                    sexp = {0: None, 1: None, 2: None}
                    pvdone = {0: None, 1: None, 2: None}
                    scb0 = (0, 512, 3072)
                    exb0 = (EXPB0, EXPB0 + 512, PB)
                    parc = [0]
                    gpend = []
                    for qtg in range(TT // 4):
                        for f in gpend:
                            f()
                        gpend = []
                        pvstop = {}
                        pends = []
                        for kc in range(4 * qtg + 4):
                            qi0 = max(0, kc - 4 * qtg)
                            par = parc[0] % 3
                            parc[0] += 1
                            w = 128 * (4 - qi0)
                            SCh = psum[:, scb0[par]:scb0[par] + w]
                            p.wait("tensor", "s", sexp[par])
                            smm = mm(SCh,
                                     pool[qrow:qrow + 64,
                                          koff + 128 * kc:koff + 128 * (kc + 1)],
                                     pool[qrow:qrow + 64,
                                          qoff + 512 * qtg + 128 * qi0:
                                          qoff + 512 * (qtg + 1)],
                                     True, True, sem="mm")
                            if kc >= 4 * qtg:
                                p.wait("vector", "mm", smm)
                                vdg = tt2c(SCh[:, 0:128], SCh[:, 0:128],
                                           causmT, ALU.add)
                                p.wait("scalar", "v", vdg)
                            p.wait("scalar", "mm", smm)
                            p.wait("scalar", "mm", pvdone[par])
                            s1 = act(pl(exb0[par], w), SCh, AF.Exp,
                                     bias=nbnd, scale=SC_L0, sem="s")
                            sexp[par] = s1

                            def pend_pv(kc=kc, par=par, s1=s1, qi0=qi0,
                                        qtg=qtg, vz_off=vz_off):
                                p.wait("tensor", "s", s1)
                                pvl = None
                                for qi in range(qi0, 4):
                                    qt = 4 * qtg + qi
                                    pvb = psum[:, 1024 + 512 * qi:
                                               1024 + 512 * qi + 65]
                                    if kc == 0:
                                        p.wait("tensor", "ev", gevd.get(qi))
                                    stop = kc == qt
                                    pvl = mm(
                                        pvb,
                                        pool[:, exb0[par] +
                                             128 * (qi - qi0):
                                             exb0[par] +
                                             128 * (qi - qi0) + 128],
                                        pl(VP + 129 * kc + vz_off, 65),
                                        kc == 0, stop,
                                        sem="mm" if (stop or qi == 3) else None)
                                    if stop:
                                        pvstop[qi] = pvl
                                pvdone[par] = pvl
                            pends.append(pend_pv)
                            if len(pends) >= 2:
                                pends.pop(0)()
                        for f in pends:
                            f()
                        for qi in range(4):
                            def evac(qi=qi, qt=4 * qtg + qi, hh=hh,
                                     last=pvstop[qi], vo_off=vo_off,
                                     pvb=psum[:, 1024 + 512 * qi:
                                              1024 + 512 * qi + 65]):
                                zc = pvb[:, 64:65] if hh == 0 else pvb[:, 0:1]
                                p.wait("vector", "mm", last)
                                recipc(smallA[:, 8 + qi:9 + qi], zc)
                                gevd[qi] = ts(
                                    pl(AT2 + 128 * qt + 64 * hh, 64),
                                    pvb[:, vo_off:vo_off + 64],
                                    smallA[:, 8 + qi:9 + qi], None, ALU.mult,
                                    sem="ev")
                            gpend.append(evac)
                    for f in gpend:
                        f()
                for qi in range(4):
                    p.wait("tensor", "ev", gevd.get(qi))
                p.wait("sync", "mm", free_wsl)
                lwo = p.dma("sync", pl(WSL + 1024, 1024), dr["l0_wop"][pr], "dW")
                ATT = VP
                tpend = []
                tev_last = None
                for t8 in range((TT + 7) // 8):
                    nblk = min(8, TT - 8 * t8)
                    base = 2048 + 1024 * (t8 % 2)
                    tl = None
                    for i in range(nblk):
                        kx = 8 * t8 + i
                        tl = tr(psum[:, base + 128 * i:base + 128 * (i + 1)],
                                pl(AT2 + 128 * kx, 128),
                                sem="mm" if i == nblk - 1 else None)

                    def evac(t8=t8, nblk=nblk, base=base, tl=tl):
                        p.wait("vector", "mm", tl)
                        return vcp(pl(ATT + 1024 * t8, 128 * nblk),
                                   psum[:, base:base + 128 * nblk], sem="ev")
                    tpend.append(evac)
                    if len(tpend) >= 2:
                        tev_last = tpend.pop(0)()
                for f in tpend:
                    tev_last = f()
                p.wait("tensor", "ev", tev_last)
                p.wait("tensor", "dW", lwo)
                evd = {}
                pend = []
                for i, (t, ncol) in enumerate(
                        (t, ncol) for t in range(TT) for ncol in range(2)):
                    b = i % 2
                    po = psum[:, 512 * b:512 * (b + 1)]
                    p.wait("tensor", "ev", evd.get(b))
                    last = mm(po, pl(ATT + 128 * t, 128),
                              pl(WSL + 1024 + 512 * ncol, 512), True, True,
                              sem="mm")
                    free_wsl = last

                    def evac(t=t, ncol=ncol, po=po, last=last, b=b):
                        p.wait("vector", "mm", last)
                        dst = acc[:, D * t + 512 * ncol:
                                  D * t + 512 * (ncol + 1)]
                        evd[b] = tt2(dst, po, dst, ALU.add, sem="ev")
                    pend.append(evac)
                    if len(pend) >= 2:
                        pend.pop(0)()
                for f in pend:
                    f()
                p.wait("tensor", "ev", evd.get(0))
                p.wait("tensor", "ev", evd.get(1))
        p.barrier()
        if cfg.debug and seq == 0:
            dd = p.dma("sync", dbg["d_acc_l0"], acc, "dD")
            p.wait("sync", "dD", dd)
            p.dma("sync", dbg["d_pool0"][:, 0:128 * TT],
                  pool[:, 8192:8192 + 128 * TT], "dD")
            p.dma("sync", dbg["d_pool0"][:, 128 * TT:257 * TT],
                  pool[:, 10240:10240 + 129 * TT], "dD")
            p.dma("sync", dbg["d_pool0"][:, 257 * TT:385 * TT],
                  pool[:, 2048:2048 + 128 * TT], "dD")
            p.dma("sync", dbg["d_pool0"][:, 385 * TT:513 * TT],
                  pool[:, 6144:6144 + 128 * TT], "dD")
            dp = p.dma("sync", dbg["d_pool0"][:, 513 * TT:513 * TT + 130],
                       pool[:, 13312:13312 + 130], "dD")
            p.wait("sync", "dD", dp)

    def phase_l1attn(seq):
        LAT = 0
        WKVH = 4096
        KVTG = 8192
        WDN = 10240
        latT = pl(LAT, 2 * T)
        ldn = p.dma("sync", pl(WDN, 2048),
                    dr["l1_wdown"].rearrange("(c p) j -> p c j", p=128), "dW")
        p.wait("tensor", "dW", ldn)
        NTG = T // 512
        kv_mm_done = None
        for hkv in range(2):
            cols = slice(512 * hkv, 512 * (hkv + 1))
            p.wait("gpsimd", "mm", kv_mm_done)
            l1 = p.dma("gpsimd", pl(WKVH, 4096),
                       dr["l1_wk"][:, cols].rearrange("(c p) j -> p c j", p=128),
                       "dG")
            p.wait("gpsimd", "dG", l1)
            l2_ = p.emit("gpsimd", lambda e, cols=cols: e.dma_start(
                out=pl(WKVH, 4096),
                in_=dr["l1_wv"][:, cols].rearrange("(c p) j -> p c j", p=128),
                accum_op=ALU.add), "dG", 16)
            p.wait("tensor", "dG", l2_)
            kevd = {}
            levd = {}
            for tg in range(NTG):
                kpend = []
                for kvi in range(4):
                    kvc = 4 * hkv + kvi
                    b = kvi % 2
                    pkv = psum[:, 512 * b:512 * (b + 1)]
                    p.wait("tensor", "ev", kevd.get(b))
                    last = None
                    for c in range(NCH):
                        last = mm(pkv, pl(WKVH + 512 * c + 128 * kvi, 128),
                                  xnc(c)[:, 512 * tg:512 * (tg + 1)],
                                  c == 0, c == NCH - 1,
                                  sem="mm" if c == NCH - 1 else None)
                    kv_mm_done = last

                    def evac(kvi=kvi, pkv=pkv, last=last, b=b):
                        p.wait("vector", "mm", last)
                        kevd[b] = vcp(pl(KVTG + 512 * kvi, 512), pkv, sem="ev")
                    kpend.append(evac)
                    if len(kpend) >= 2:
                        kpend.pop(0)()
                for f in kpend:
                    f()
                p.wait("tensor", "ev", kevd.get(0))
                p.wait("tensor", "ev", kevd.get(1))
                lpend = []
                for mc2 in range(2):
                    b = mc2 % 2
                    plat = psum[:, 1024 + 512 * b:1024 + 512 * (b + 1)]
                    p.wait("tensor", "ev", levd.get(b))
                    last = None
                    for kvi in range(4):
                        kvc = 4 * hkv + kvi
                        last = mm(plat, pl(WDN + 256 * kvc + 128 * mc2, 128),
                                  pl(KVTG + 512 * kvi, 512), kvi == 0, kvi == 3,
                                  sem="mm" if kvi == 3 else None)

                    def evac(tg=tg, mc2=mc2, plat=plat, last=last, b=b,
                             hkv=hkv):
                        p.wait("vector", "mm", last)
                        dstl = latT[:, T * mc2 + 512 * tg:
                                    T * mc2 + 512 * (tg + 1)]
                        if hkv == 0:
                            levd[b] = vcp(dstl, plat, sem="ev")
                        else:
                            levd[b] = tt2(dstl, plat, dstl, ALU.add, sem="ev")
                    lpend.append(evac)
                    if len(lpend) >= 2:
                        lpend.pop(0)()
                for f in lpend:
                    f()
            p.wait("tensor", "ev", levd.get(0))
            p.wait("tensor", "ev", levd.get(1))
        p.barrier()
        if cfg.debug and seq == 0:
            dd = p.dma("sync", dbg["d_latT"], latT, "dD")
            p.wait("sync", "dD", dd)

        Q2, KA, KB = 4096, 6144, 8192
        V2 = 10240
        AT2 = 11280
        WQS = 14336
        EXPB1 = WQS
        WUK = smallC[:, 0:256]
        WUV = smallC[:, 256:512]
        vzo = memset(pool[:, V2:V2 + 65 * TT].rearrange(
            "p (b c) -> p b c", c=65)[:, :, 0:1], 1.0, sem="v")
        p.wait("scalar", "v", vzo)
        p.wait("tensor", "v", vzo)
        free_wq = None
        for pr in range(NP):
            p.wait("sync", "mm", free_wq)
            lq = p.dma("sync", pl(WQS, 1024),
                       dr["l1_wqg"][pr].rearrange("(c p) m -> p c m", p=128), "dW")
            p.dma("sync", WUK,
                  dr["l1_wupkg"][pr].rearrange("(c p) m -> p c m", p=128), "dW")
            lv = p.dma("sync", WUV,
                       dr["l1_wupvg"][pr].rearrange("(c p) m -> p c m", p=128),
                       "dW")
            p.wait("tensor", "dW", lv)
            vzq = memset(smallB[:, 30:31], 0.0, sem="v")
            p.wait("scalar", "v", vzq)
            qtail = {}
            qpend2 = []
            for t in range(TT):
                b = t % 2
                ptq = psum[:, 512 * b:512 * b + 128]
                qo = 13328 + 128 * b
                trb = 2048 + 512 * b
                p.wait("tensor", "ev", qtail.get(b))
                last = None
                for c in range(NCH):
                    last = mm(ptq, xnc(c)[:, 128 * t:128 * (t + 1)],
                              pl(WQS + 128 * c, 128), c == 0, c == NCH - 1,
                              sem="mm" if c == NCH - 1 else None)
                free_wq = last

                def qchain(t=t, b=b, ptq=ptq, qo=qo, trb=trb, last=last):
                    acol = 36 if b == 0 else 37
                    p.wait("vector", "mm", last)
                    vsq = ts(pl(qo, 128), ptq, escA[:, t:t + 1], None,
                             ALU.mult, sem="v")
                    p.wait("scalar", "v", vsq)
                    sq = act(psum[:, 3072 + 128 * b:3200 + 128 * b],
                             pl(qo, 128), AF.Square,
                             accum=smallA[:, acol:acol + 1], sem="s")
                    p.wait("vector", "s", sq)
                    tt2c(smallB[:, 30:31], smallB[:, 30:31],
                         smallA[:, acol:acol + 1], ALU.max)
                    p.wait("tensor", "v", vsq)
                    tq = tr(psum[:, trb:trb + 128], pl(qo, 128), sem="mm")
                    p.wait("vector", "mm", tq)
                    qtail[b] = vcp(pl(Q2 + 128 * t, 128),
                                   psum[:, trb:trb + 128], sem="ev")
                qpend2.append(qchain)
                if len(qpend2) >= 2:
                    qpend2.pop(0)()
            for f in qpend2:
                f()
            p.wait("tensor", "v", p.cnt["v"])
            tqa = tr(psum[:1, 3328:3456], smallB[:, 30:31], sem="mm")
            p.wait("vector", "mm", tqa)
            vcpc(smallD[:1, 0:128], psum[:1, 3328:3456])
            v9 = rmaxc(smallA[:1, 23:24], smallD[:1, 0:128])
            p.wait("scalar", "v", v9)
            s9 = act(smallA[:1, 38:39], smallA[:1, 23:24], AF.Sqrt, sem="s")
            p.wait("vector", "s", s9)
            vz = memset(smallB[:, 28:30], 0.0, sem="v")
            p.wait("scalar", "v", vz)
            ktail = {}
            kpend2 = []
            for t in range(TT):
                b = t % 2
                pk_ = psum[:, 1024 + 512 * b:1024 + 512 * b + 128]
                ko = 11264 + 256 * b
                trb = 2048 + 512 * b
                p.wait("tensor", "ev", ktail.get(b))
                mm(pk_, latT[:, 128 * t:128 * (t + 1)], WUK[:, 0:128],
                   True, False)
                lastk = mm(pk_, latT[:, T + 128 * t:T + 128 * (t + 1)],
                           WUK[:, 128:256], False, True, sem="mm")

                def chain(t=t, b=b, pk_=pk_, ko=ko, trb=trb, lastk=lastk):
                    acol = 28 if b == 0 else 32
                    p.wait("vector", "mm", lastk)
                    ts(pl(ko, 128), pk_, eplus[:, t:t + 1], None, ALU.mult)
                    vb = ts(pl(ko + 128, 128), pk_, eminus[:, t:t + 1], None,
                            ALU.mult, sem="v")
                    p.wait("scalar", "v", vb)
                    act(psum[:, 3072 + 256 * b:3200 + 256 * b], pl(ko, 128),
                        AF.Square, accum=smallA[:, acol:acol + 1])
                    sqb = act(psum[:, 3200 + 256 * b:3328 + 256 * b],
                              pl(ko + 128, 128), AF.Square,
                              accum=smallA[:, acol + 1:acol + 2], sem="s")
                    p.wait("vector", "s", sqb)
                    tt2c(smallB[:, 28:29], smallB[:, 28:29],
                         smallA[:, acol:acol + 1], ALU.max)
                    tt2c(smallB[:, 29:30], smallB[:, 29:30],
                         smallA[:, acol + 1:acol + 2], ALU.max)
                    p.wait("tensor", "v", vb)
                    tr(psum[:, trb:trb + 128], pl(ko, 128))
                    tB = tr(psum[:, trb + 128:trb + 256], pl(ko + 128, 128),
                            sem="mm")
                    p.wait("vector", "mm", tB)
                    vcp(pl(KA + 128 * t, 128), psum[:, trb:trb + 128])
                    ktail[b] = vcp(pl(KB + 128 * t, 128),
                                   psum[:, trb + 128:trb + 256], sem="ev")
                kpend2.append(chain)
                if len(kpend2) >= 2:
                    kpend2.pop(0)()
            for f in kpend2:
                f()
            p.wait("tensor", "v", p.cnt["v"])
            tka = tr(psum[:1, 3328:3456], smallB[:, 28:29], sem="mm")
            tkb = tr(psum[:1, 3456:3584], smallB[:, 29:30], sem="mm")
            p.wait("vector", "mm", tkb)
            vcpc(smallC[:1, 0:128], psum[:1, 3328:3456])
            rmaxc(smallA[:1, 24:25], smallC[:1, 0:128])
            vcpc(smallC[:1, 0:128], psum[:1, 3456:3584])
            v6 = rmaxc(smallA[:1, 25:26], smallC[:1, 0:128])
            p.wait("scalar", "v", v6)
            act(smallA[:1, 26:27], smallA[:1, 24:25], AF.Sqrt)
            s7 = act(smallA[:1, 27:28], smallA[:1, 25:26], AF.Sqrt, sem="s")
            p.wait("vector", "s", s7)
            tsc(smallA[:1, 27:28], smallA[:1, 27:28],
                float(np.exp(2.0 * C_DECAY * (T - 1))), None, ALU.mult)
            tt2c(smallA[:1, 19:20], smallA[:1, 26:27], smallA[:1, 27:28],
                 ALU.max)
            tt2c(smallA[:1, 20:21], smallA[:1, 38:39], smallA[:1, 19:20],
                 ALU.mult)
            v8 = tsc(smallA[:1, 21:22], smallA[:1, 20:21], -1.0, None,
                     ALU.mult)
            p.wait("tensor", "v", v8)
            bb8 = mm(psum[:, 3584:3585], onesr, smallA[:1, 21:22], True, True,
                     sem="mm")
            p.wait("vector", "mm", bb8)
            vb8 = vcp(smallB[:, 27:28], psum[:, 3584:3585], sem="v")
            p.wait("scalar", "v", vb8)
            nbnd = smallB[:, 27:28]
            gevd = {}
            for hh in range(2):
                p.wait("tensor", "s", p.cnt["s"])
                vevd2 = {}
                vpend2 = []
                for t in range(TT):
                    b = t % 2
                    pv_ = psum[:, 512 * b:512 * b + 64]
                    p.wait("tensor", "ev", vevd2.get(b))
                    if hh == 0 and t < 4:
                        p.wait("tensor", "ev", gevd.get(t))
                    mm(pv_, latT[:, 128 * t:128 * (t + 1)],
                       WUV[:, 64 * hh:64 * hh + 64], True, False)
                    lastv = mm(pv_, latT[:, T + 128 * t:T + 128 * (t + 1)],
                               WUV[:, 128 + 64 * hh:128 + 64 * hh + 64],
                               False, True, sem="mm")

                    def vevac(t=t, pv_=pv_, lastv=lastv, b=b):
                        p.wait("vector", "mm", lastv)
                        vevd2[b] = vcp(pl(V2 + 65 * t + 1, 64), pv_, sem="ev")
                    vpend2.append(vevac)
                    if len(vpend2) >= 2:
                        vpend2.pop(0)()
                for f in vpend2:
                    f()
                p.wait("tensor", "ev", vevd2.get(0))
                p.wait("tensor", "ev", vevd2.get(1))
                qrow = 64 * hh
                sexp = {0: None, 1: None}
                pvdone = {0: None, 1: None}
                parc = [0]
                gpend = []
                for qtg in range(TT // 4):
                    for f in gpend:
                        f()
                    gpend = []
                    p.wait("tensor", "v", p.cnt["v"])
                    tq2 = None
                    for qi in range(4):
                        qt = 4 * qtg + qi
                        tq2 = tr(psum[:1, 3072 + 128 * qi:
                                      3072 + 128 * (qi + 1)],
                                 e2plus[:, qt:qt + 1],
                                 sem="mm" if qi == 3 else None)
                    p.wait("vector", "mm", tq2)
                    ve2 = vcp(smallD[:1, 0:512], psum[:1, 3072:3584], sem="v")
                    p.wait("tensor", "v", ve2)
                    e2l = None
                    for qi in range(4):
                        e2l = mm(psum[:, 3584 + 128 * qi:3584 + 128 * (qi + 1)],
                                 onesr, smallD[:1, 128 * qi:128 * (qi + 1)],
                                 True, True, sem="mm" if qi == 3 else None)
                    p.wait("vector", "mm", e2l)
                    ve2b = vcpc(pl(13584, 512), psum[:, 3584:4096])
                    p.wait("tensor", "v", ve2b)
                    pvlast = None
                    pend_pv = None
                    for kc in range(TT):
                        nB = max(0, min(kc - 4 * qtg, 4))
                        pieces = []
                        if nB:
                            pieces.append(("B", 0, nB))
                        if nB < 4:
                            pieces.append(("A", nB, 4))
                        for kind, qi0, qi1 in pieces:
                            par = parc[0] % 2
                            parc[0] += 1
                            w = 128 * (qi1 - qi0)
                            SCh = psum[:, 512 * par:512 * par + w]
                            ko = KA if kind == "A" else KB
                            p.wait("tensor", "s", sexp[par])
                            smm = mm(SCh,
                                     pool[qrow:qrow + 64,
                                          ko + 128 * kc:ko + 128 * (kc + 1)],
                                     pool[qrow:qrow + 64,
                                          Q2 + 512 * qtg + 128 * qi0:
                                          Q2 + 512 * qtg + 128 * qi1],
                                     True, True, sem="mm")
                            vfix = None
                            if kind == "B":
                                p.wait("vector", "mm", smm)
                                for qi in range(qi0, qi1):
                                    off = 128 * (qi - qi0)
                                    vfix = tt2c(
                                        SCh[:, off:off + 128],
                                        SCh[:, off:off + 128],
                                        pl(13584 + 128 * qi, 128), ALU.mult)
                            elif 4 * qtg <= kc < 4 * qtg + 4:
                                qi_d = kc - 4 * qtg
                                off = 128 * (qi_d - qi0)
                                p.wait("vector", "mm", smm)
                                vfix = tt2c(SCh[:, off:off + 128],
                                            SCh[:, off:off + 128], dcorrT,
                                            ALU.mult)
                            if vfix is not None:
                                p.wait("scalar", "v", vfix)
                            p.wait("scalar", "mm", smm)
                            p.wait("scalar", "mm", pvdone[par])
                            s1 = act(pl(EXPB1 + 512 * par, w), SCh, AF.Exp,
                                     bias=nbnd, sem="s")
                            sexp[par] = s1
                            if pend_pv is not None:
                                pend_pv()

                            def pend_pv(kc=kc, par=par, s1=s1, qi0=qi0,
                                        qi1=qi1):
                                nonlocal pvlast
                                p.wait("tensor", "s", s1)
                                for qi in range(qi0, qi1):
                                    pvb = psum[:, 1024 + 512 * qi:
                                               1024 + 512 * qi + 65]
                                    if kc == 0:
                                        p.wait("tensor", "ev", gevd.get(qi))
                                    pvlast = mm(
                                        pvb,
                                        pool[:, EXPB1 + 512 * par +
                                             128 * (qi - qi0):
                                             EXPB1 + 512 * par +
                                             128 * (qi - qi0) + 128],
                                        pl(V2 + 65 * kc, 65), kc == 0,
                                        kc == TT - 1,
                                        sem="mm" if qi == qi1 - 1 else None)
                                pvdone[par] = pvlast
                    pend_pv()
                    for qi in range(4):
                        def evac(qi=qi, qt=4 * qtg + qi, last=pvlast, hh=hh,
                                 pvb=psum[:, 1024 + 512 * qi:
                                          1024 + 512 * qi + 65]):
                            p.wait("vector", "mm", last)
                            recipc(smallA[:, 8 + qi:9 + qi], pvb[:, 0:1])
                            gevd[qi] = ts(pl(AT2 + 128 * qt + 64 * hh, 64),
                                          pvb[:, 1:65],
                                          smallA[:, 8 + qi:9 + qi], None,
                                          ALU.mult, sem="ev")
                        gpend.append(evac)
                for f in gpend:
                    f()
                for qi in range(4):
                    p.wait("tensor", "ev", gevd.get(qi))
            p.wait("sync", "mm", p.cnt["mm"])
            lwo = p.dma("sync", pl(WQS, 1024),
                        dr["l1_wo"][128 * pr:128 * (pr + 1), :], "dW")
            ATT = Q2
            tpend = []
            tev_last = None
            for t8 in range((TT + 7) // 8):
                nblk = min(8, TT - 8 * t8)
                base = 2048 + 1024 * (t8 % 2)
                tl = None
                for i in range(nblk):
                    kx = 8 * t8 + i
                    tl = tr(psum[:, base + 128 * i:base + 128 * (i + 1)],
                            pl(AT2 + 128 * kx, 128),
                            sem="mm" if i == nblk - 1 else None)

                def evac(t8=t8, nblk=nblk, base=base, tl=tl):
                    p.wait("vector", "mm", tl)
                    return vcp(pl(ATT + 1024 * t8, 128 * nblk),
                               psum[:, base:base + 128 * nblk], sem="ev")
                tpend.append(evac)
                if len(tpend) >= 2:
                    tev_last = tpend.pop(0)()
            for f in tpend:
                tev_last = f()
            p.wait("tensor", "ev", tev_last)
            p.wait("tensor", "dW", lwo)
            evd = {}
            pend = []
            for i, (t, ncol) in enumerate(
                    (t, ncol) for t in range(TT) for ncol in range(2)):
                b = i % 2
                po = psum[:, 512 * b:512 * (b + 1)]
                p.wait("tensor", "ev", evd.get(b))
                last = mm(po, pl(ATT + 128 * t, 128),
                          pl(WQS + 512 * ncol, 512), True, True, sem="mm")
                free_wq = last

                def evac(t=t, ncol=ncol, po=po, last=last, b=b):
                    p.wait("vector", "mm", last)
                    dst = acc[:, D * t + 512 * ncol:D * t + 512 * (ncol + 1)]
                    evd[b] = tt2(dst, po, dst, ALU.add, sem="ev")
                pend.append(evac)
                if len(pend) >= 2:
                    pend.pop(0)()
            for f in pend:
                f()
            p.wait("tensor", "ev", evd.get(0))
            p.wait("tensor", "ev", evd.get(1))
        p.barrier()
        if cfg.debug and seq == 0:
            dd = p.dma("sync", dbg["d_acc_at1"], acc, "dD")
            p.wait("sync", "dD", dd)

    def phase_amax_cc():
        t1 = tr(psum[:1, 3072:3200], gmax, sem="mm")
        p.wait("vector", "mm", t1)
        v1 = vcp(smallC[:1, 0:128], psum[:1, 3072:3200], sem="v")
        p.wait("vector", "v", v1)
        rmaxc(smallA[:1, 10:11], smallC[:1, 0:128])
        v2 = memset(amax1, 0.0, sem="v")
        p.wait("scalar", "v", v2)
        s1 = act(amax1[:1, 0:1], smallA[:1, 10:11], AF.Sqrt, sem="s")
        p.wait("gpsimd", "s", s1)
        p.wait("gpsimd", "v", v2)
        p.wait("sync", "s", s1)
        p.wait("sync", "v", v2)
        if cfg.use_cc:
            g1_ = p.dma("gpsimd", cc_in[:, :], amax1, "dG")
            p.wait("gpsimd", "dG", g1_)
            cc = p.emit("gpsimd", lambda e: e.collective_compute(
                "AllReduce", ALU.max,
                replica_groups=[list(range(cfg.n_cores))],
                ins=[cc_in.ap().opt()], outs=[cc_out.ap().opt()]), "dCC", 1)
            p.wait("gpsimd", "dCC", cc)
            g2_ = p.dma("gpsimd", amax1, cc_out[:, :], "dG")
            p.wait("vector", "dG", g2_)
        if cfg.debug:
            dd = p.dma("sync", dbg["d_amax"], amax1, "dD")
            p.wait("sync", "dD", dd)

    def emit_scale_from_amax(a_ap, s_col, r_col, tag):
        i0 = {"x": 16, "w": 24}[tag]
        ts(smallA[:1, i0:i0 + 1], a_ap, 1e-12, None, ALU.max)
        p.barrier()
        recip(smallA[:1, i0 + 1:i0 + 2], smallA[:1, i0:i0 + 1])
        p.barrier()
        ts(smallA[:1, i0 + 2:i0 + 3], smallA[:1, i0 + 1:i0 + 2], 448.0, None,
           ALU.mult)
        p.barrier()
        ts(smallA[:1, i0 + 3:i0 + 4], smallA[:1, i0 + 2:i0 + 3], 448.0, None,
           ALU.min)
        p.barrier()
        recip(smallA[:1, i0 + 4:i0 + 5], smallA[:1, i0 + 3:i0 + 4])
        ts(smallA[:1, i0 + 5:i0 + 6], smallA[:1, i0 + 3:i0 + 4], 0.5, None,
           ALU.mult)
        p.barrier()
        m1 = mm(psum[:, 3584:3585], onesr, smallA[:1, i0 + 5:i0 + 6],
                True, True)
        m2 = mm(psum[:, 3586:3587], onesr, smallA[:1, i0 + 4:i0 + 5],
                True, True, sem="mm")
        p.wait("vector", "mm", m2)
        vcp(s_col, psum[:, 3584:3585])
        vcp(r_col, psum[:, 3586:3587])
        p.barrier()

    BIASB = 0
    WT = 1024

    def phase_wquant():
        lbr = p.dma("sync", pool[:1, WT:WT + D], dr["l1_bias"], "dW")
        p.wait("tensor", "dW", lbr)
        mm(psum[:, 0:512], onesr, pool[:1, WT:WT + 512], True, True)
        m2 = mm(psum[:, 512:1024], onesr, pool[:1, WT + 512:WT + D], True, True,
                sem="mm")
        p.wait("vector", "mm", m2)
        ev = vcp(pl(BIASB, D), psum[:, 0:1024], sem="ev")
        p.wait("sync", "ev", ev)
        memset(smallA[:, 12:13], 0.0)
        for c in range(NCH):
            lw = p.dma("sync", pl(WT, D), dr["l1_wt"][128 * c:128 * (c + 1), :],
                       "dW")
            p.wait("vector", "dW", lw)
            rmaxc(smallA[:, 13:14], pl(WT, D), absval=True)
            va = tt2c(smallA[:, 12:13], smallA[:, 12:13], smallA[:, 13:14],
                      ALU.max)
            p.wait("sync", "v", va)
        p.barrier()
        t1 = tr(psum[:1, 3072:3200], smallA[:, 12:13], sem="mm")
        p.wait("vector", "mm", t1)
        v1 = vcp(smallC[:1, 0:128], psum[:1, 3072:3200], sem="v")
        p.wait("vector", "v", v1)
        rmaxc(smallA[:1, 30:31], smallC[:1, 0:128])
        p.barrier()
        emit_scale_from_amax(smallA[:1, 30:31], smallB[:, 16:17],
                             smallB[:, 17:18], "w")
        for c in range(NCH):
            lw = p.dma("sync", pl(WT, D), dr["l1_wt"][128 * c:128 * (c + 1), :],
                       "dW")
            p.wait("vector", "dW", lw)
            tsc(pl(WT, D), pl(WT, D), smallB[:, 16:17], -224.0, ALU.mult,
                ALU.max)
            vq = ts(w8[:, D * c:D * (c + 1)], pl(WT, D), 224.0, None, ALU.min,
                    sem="v")
            p.wait("sync", "v", vq)
            p.wait("vector", "v", vq)
        p.barrier()

    def phase_l1ffn(seq):
        qfree = {0: None, 1: None}
        fevd = {}
        fpend = []
        for t in range(TT):
            bb = t % 2
            po0 = psum[:, 1024 * bb:1024 * bb + 512]
            po1 = psum[:, 1024 * bb + 512:1024 * bb + 1024]
            p.wait("tensor", "ev", fevd.get(bb))
            last = None
            for c in range(NCH):
                slot = c % 2
                p.wait("vector", "mm", qfree[slot])
                tsc(smallC[:, 0:128], xnc(c)[:, 128 * t:128 * (t + 1)],
                    smallB[:, 18:19], -224.0, ALU.mult, ALU.max)
                vq = tsc(q8t[:, 128 * slot:128 * (slot + 1)], smallC[:, 0:128],
                         224.0, None, ALU.min)
                p.wait("tensor", "v", vq)
                mm(po0, q8t[:, 128 * slot:128 * (slot + 1)],
                   w8[:, D * c:D * c + 512], c == 0, c == NCH - 1)
                last = mm(po1, q8t[:, 128 * slot:128 * (slot + 1)],
                          w8[:, D * c + 512:D * (c + 1)], c == 0, c == NCH - 1,
                          sem="mm")
                qfree[slot] = last

            def evac(t=t, po0=po0, po1=po1, last=last, bb=bb):
                p.wait("vector", "mm", last)
                for ncol in range(2):
                    po = po0 if ncol == 0 else po1
                    sttc(bft, po, smallB[:, 19:20],
                         pl(BIASB + 512 * ncol, 512), ALU.mult, ALU.add)
                    dst = acc[:, D * t + 512 * ncol:D * t + 512 * (ncol + 1)]
                    ea = tt2(dst, dst, bft, ALU.add, sem="ev")
                    p.wait("vector", "ev", ea)
                    fevd[bb] = ea
            fpend.append(evac)
            if len(fpend) >= 2:
                fpend.pop(0)()
        for f in fpend:
            f()
        p.barrier()

    phase_ln(0, False, dbg.get("d_xn0"))
    phase_l0ffn(0)
    phase_l0attn(0)
    phase_ln(1, True, dbg.get("d_xn1"))
    phase_l1attn(0)
    s1 = p.dma("sync", spill_xn, xn, "dS")
    s2 = p.dma("sync", spill_acc, acc, "dS")
    p.wait("sync", "dS", s2)
    l3 = p.dma("sync", acc, x0[1].rearrange("(t p) d -> p t d", p=128), "dX")
    p.wait("vector", "dX", l3)
    p.barrier()
    phase_ln(0, False)
    phase_l0ffn(1)
    phase_l0attn(1)
    phase_ln(1, True)
    phase_amax_cc()
    phase_l1attn(1)
    phase_wquant()
    p.barrier()
    emit_scale_from_amax(amax1[:1, 0:1], smallB[:, 18:19], smallB[:, 20:21], "x")
    tt2c(smallB[:, 19:20], smallB[:, 20:21], smallB[:, 17:18], ALU.mult)
    tsc(smallB[:, 19:20], smallB[:, 19:20], 4.0, None, ALU.mult)
    p.barrier()
    phase_l1ffn(1)
    st_b = p.dma("sync", out[1].rearrange("(t p) d -> p t d", p=128), acc, "dS")
    p.wait("sync", "dS", st_b)
    p.dma("sync", xn, spill_xn, "dX")
    r2 = p.dma("sync", acc, spill_acc, "dX")
    p.wait("vector", "dX", r2)
    p.wait("tensor", "dX", r2)
    p.barrier()
    phase_l1ffn(0)
    st_a = p.dma("sync", out[0].rearrange("(t p) d -> p t d", p=128), acc, "dS")
    p.wait("sync", "dS", st_a)
    p.barrier()
    p.build()
    return nc, p


def kernel(**inputs):
    cfg = Cfg()
    cons = host_constants(cfg)
    wts = host_weights(cfg, inputs)
    x = np.ascontiguousarray(np.asarray(inputs["L0_attn_x"], dtype=np.float32))
    nc, p = build_program(cfg)
    base = dict(cons)
    base.update(wts)
    in_maps = []
    for k in range(cfg.n_cores):
        m = dict(base)
        m["x0"] = np.ascontiguousarray(x[2 * k:2 * k + 2])
        in_maps.append(m)
    res = run_bass_kernel_spmd(nc, in_maps, core_ids=list(range(cfg.n_cores)))
    return np.concatenate([r["out"] for r in res.results], axis=0)


# version 18
# speedup vs baseline: 1.2982x; 1.2982x over previous
import math
import numpy as np

import concourse.bass as bass
import concourse.mybir as mybir
from concourse.bass_utils import run_bass_kernel_spmd

F32 = mybir.dt.float32
BF16 = mybir.dt.bfloat16
FP8 = mybir.dt.float8e4
AF = mybir.ActivationFunctionType
ALU = mybir.AluOpType
AX = mybir.AxisListType

D = 1024
NCH = 8
DK = 64
DH = 128
DL = 256
E = 16
BIG = 1.0e30
EPS_LN = 1e-5
C_DECAY = 0.01 * 0.01
SC_L0 = 1.0 / 8.0
SC_FFN = 1.0 / math.sqrt(128.0)
SC_L1 = 1.0 / (math.sqrt(1024.0) + 1e-8)
POOL = 15376


class Cfg:
    def __init__(self, T=2048, H=16, n_seq=2, n_cores=8, debug=False):
        self.T = T
        self.TT = T // 128
        self.H = H
        self.n_seq = n_seq
        self.n_cores = n_cores
        self.debug = debug
        self.use_cc = n_cores > 1
        self.detect = False
        assert T % 512 == 0 and H % 4 == 0


def rope_perm(H):
    perm = []
    for g in range(H // 4):
        for par in (0, 1):
            for h in range(4 * g, 4 * g + 4):
                for f in range(32):
                    perm.append(64 * h + 2 * f + par)
    return np.array(perm)


def host_constants(cfg):
    T = cfg.T
    c = {}
    c["ident"] = np.eye(128, dtype=np.float32)
    i = np.arange(128)
    c["causmask"] = np.where(i[None, :] > i[:, None], -BIG, 0.0).astype(np.float32)
    c["causmT"] = np.ascontiguousarray(c["causmask"].T)
    c["dcorr"] = np.exp(-2.0 * C_DECAY *
                        np.maximum(i[None, :] - i[:, None], 0)).astype(np.float32)
    c["dcorrT"] = np.ascontiguousarray(c["dcorr"].T)
    powers = np.arange(0, DK, 2, dtype=np.float32) / np.float32(DK)
    freqs = (np.float32(1.0) / (np.float32(10000.0) ** powers)).astype(np.float32)
    t = np.arange(T, dtype=np.float32)
    ang = np.outer(t, freqs).astype(np.float32)
    c["ctab"] = np.ascontiguousarray(np.tile(np.cos(ang).astype(np.float32).T,
                                             (4, 1)))
    c["stab"] = np.ascontiguousarray(np.tile(np.sin(ang).astype(np.float32).T,
                                             (4, 1)))
    pos = np.arange(T, dtype=np.float64).reshape(cfg.TT, 128).T
    c["eplus"] = np.ascontiguousarray(np.exp(C_DECAY * pos).astype(np.float32))
    c["eminus"] = np.ascontiguousarray(np.exp(-C_DECAY * pos).astype(np.float32))
    c["escA"] = np.ascontiguousarray((SC_L1 * np.exp(-C_DECAY * pos))
                                     .astype(np.float32))
    c["escB"] = np.ascontiguousarray((SC_L1 * np.exp(C_DECAY * pos))
                                     .astype(np.float32))
    c["e2plus"] = np.ascontiguousarray(np.exp(2.0 * C_DECAY * pos)
                                       .astype(np.float32))
    return c


def host_weights(cfg, w):
    H = cfg.H
    perm = rope_perm(H)
    o = {}
    for nm in ("wq", "wk"):
        a = np.ascontiguousarray(np.asarray(w[f"L0_attn_{nm}"]).T[:, perm])
        o[f"l0_{nm}g"] = np.ascontiguousarray(
            a.reshape(D, H // 4, 256).transpose(1, 0, 2))
    a = np.ascontiguousarray(np.asarray(w["L0_attn_wv"]).T)
    o["l0_wvp"] = np.ascontiguousarray(a.reshape(D, H // 2, 128).transpose(1, 0, 2))
    a = np.ascontiguousarray(np.asarray(w["L0_attn_wo"]).T)
    o["l0_wop"] = np.ascontiguousarray(a.reshape(H // 2, 128, D))
    o["l0_snT"] = np.ascontiguousarray(np.asarray(w["L0_ffn_sim_matrix"]).T)
    o["l0_gates"] = np.ascontiguousarray(np.asarray(w["L0_ffn_gates"])
                                         .reshape(1, E))
    o["l0_qkvp"] = np.ascontiguousarray(np.concatenate(
        [np.asarray(w["L0_ffn_q_proj"]), np.asarray(w["L0_ffn_k_proj"]),
         np.asarray(w["L0_ffn_v_proj"])], axis=2))
    o["l0_op"] = np.ascontiguousarray(w["L0_ffn_o_proj"])
    a = np.ascontiguousarray(w["L1_attn_Wq"])
    o["l1_wqg"] = np.ascontiguousarray(a.reshape(D, H // 2, 128).transpose(1, 0, 2))
    o["l1_wk"] = np.ascontiguousarray(w["L1_attn_Wk"])
    o["l1_wv"] = np.ascontiguousarray(w["L1_attn_Wv"])
    o["l1_wdown"] = np.ascontiguousarray(w["L1_attn_W_down_kv"])
    for nm, key in (("wupk", "L1_attn_W_up_k"), ("wupv", "L1_attn_W_up_v")):
        a = np.ascontiguousarray(w[key])
        o[f"l1_{nm}g"] = np.ascontiguousarray(
            a.reshape(DL, H // 2, 128).transpose(1, 0, 2))
    o["l1_wo"] = np.ascontiguousarray(w["L1_attn_Wo"])
    o["l1_wt"] = np.ascontiguousarray(np.asarray(w["L1_ffn_weight"]).T)
    o["l1_bias"] = np.ascontiguousarray(np.asarray(w["L1_ffn_bias"])
                                        .reshape(1, D))
    return o


class P:
    ENGS = ("sync", "tensor", "vector", "scalar", "gpsimd")

    def __init__(self, nc):
        self.nc = nc
        self.q = {e: [] for e in P.ENGS}
        self.sems = {}
        self.cnt = {}
        self.n_instr = 0

    def sem(self, name):
        if name not in self.sems:
            self.sems[name] = self.nc.alloc_semaphore(name)
            self.cnt[name] = 0
        return self.sems[name]

    def on(self, eng, fn):
        self.q[eng].append(fn)
        self.n_instr += 1

    def wait(self, eng, name, val):
        if val is None or val <= 0:
            return
        s = self.sem(name)
        self.on(eng, lambda e, s=s, v=val: e.wait_ge(s, v))

    def emit(self, eng, fn, sem=None, n=1):
        if sem is None:
            self.on(eng, fn)
            return None
        s = self.sem(sem)
        self.cnt[sem] += n
        tot = self.cnt[sem]
        self.on(eng, lambda e, fn=fn, s=s, n=n: fn(e).then_inc(s, n))
        return tot

    def chain(self, eng, fn, sem="v"):
        t = self.emit(eng, fn, sem)
        self.wait(eng, sem, t)
        return t

    def dma(self, eng, out, in_, sem="dW"):
        return self.emit(eng, lambda e, o=out, i=in_: e.dma_start(out=o, in_=i),
                         sem, 16)

    def barrier(self, tag=""):
        s = self.sem("bar")
        self.cnt["bar"] += 5
        tgt = self.cnt["bar"]
        for eng in P.ENGS:
            self.on(eng, lambda e, s=s: e.drain().then_inc(s, 1))
            self.on(eng, lambda e, s=s, t=tgt: e.wait_ge(s, t))

    def build(self):
        with self.nc.Block() as blk:
            for eng in P.ENGS:
                def body(e, eng=eng):
                    for fn in self.q[eng]:
                        fn(e)
                getattr(blk, eng)(body)


def build_program(cfg):
    T, TT, H = cfg.T, cfg.TT, cfg.H
    NG, NP = H // 4, H // 2
    SPAN = 512
    nc = bass.Bass("TRN2", target_bir_lowering=False,
                   detect_race_conditions=cfg.detect)

    dr = {}
    def din(name, shape):
        dr[name] = nc.dram_tensor(name, list(shape), F32, kind="ExternalInput").ap()
        return dr[name]

    x0 = din("x0", (cfg.n_seq, T, D))
    out = nc.dram_tensor("out", [cfg.n_seq, T, D], F32, kind="ExternalOutput").ap()
    for nm, shape in (
        ("ident", (128, 128)), ("causmask", (128, 128)), ("dcorr", (128, 128)),
        ("dcorrT", (128, 128)), ("e2plus", (128, TT)),
        ("causmT", (128, 128)),
        ("ctab", (128, T)), ("stab", (128, T)),
        ("eplus", (128, TT)), ("eminus", (128, TT)),
        ("escA", (128, TT)), ("escB", (128, TT)),
        ("l0_wqg", (NG, D, 256)), ("l0_wkg", (NG, D, 256)),
        ("l0_wvp", (NP, D, 128)), ("l0_wop", (NP, 128, D)),
        ("l0_snT", (16, D)), ("l0_gates", (1, E)),
        ("l0_qkvp", (E, D, 3 * DH)), ("l0_op", (E, DH, D)),
        ("l1_wqg", (NP, D, 128)), ("l1_wk", (D, D)), ("l1_wv", (D, D)),
        ("l1_wdown", (D, DL)), ("l1_wupkg", (NP, DL, 128)),
        ("l1_wupvg", (NP, DL, 128)), ("l1_wo", (D, D)),
        ("l1_wt", (D, D)), ("l1_bias", (1, D)),
    ):
        din(nm, shape)
    spill_xn = nc.dram_tensor("spill_xn", [128, NCH * T], F32).ap()
    spill_lat = nc.dram_tensor("spill_lat", [128, 2 * T], F32).ap()
    spill_acc = nc.dram_tensor("spill_acc", [128, TT * D], F32).ap()
    cc_in = nc.dram_tensor("cc_in", [1, 8], F32)
    cc_out = nc.dram_tensor("cc_out", [1, 8], F32)

    dbg = {}
    if cfg.debug:
        for nm, shape in (
            ("d_xn0", (128, NCH * T)), ("d_w16", (128, E * TT)),
            ("d_accq", (128, TT * DH)), ("d_o", (128, TT * DH)),
            ("d_acc_ffn", (128, TT * D)), ("d_acc_l0", (128, TT * D)),
            ("d_latT", (128, 2 * T)), ("d_amax", (1, 8)),
            ("d_acc_at1", (128, TT * D)), ("d_xn1", (128, NCH * T)),
            ("d_pool0", (128, 513 * TT + 130)),
        ):
            dbg[nm] = nc.dram_tensor(nm, list(shape), F32,
                                     kind="ExternalOutput").ap()

    def sbt(name, shape, dt=F32):
        return nc.alloc_sbuf_tensor(name, list(shape), dt).ap()

    xn = sbt("xn", (128, NCH * T))
    acc = sbt("acc", (128, TT * D))
    pool = sbt("pool", (128, POOL))
    ident = sbt("identS", (128, 128))
    causm = sbt("causmS", (128, 128))
    dcorr = sbt("dcorrS", (128, 128))
    dcorrT = sbt("dcorrTS", (128, 128))
    causmT = sbt("causmTS", (128, 128))
    e2plus = sbt("e2plusS", (128, TT))
    eplus = sbt("eplusS", (128, TT))
    eminus = sbt("eminusS", (128, TT))
    escA = sbt("escAS", (128, TT))
    escB = sbt("escBS", (128, TT))
    w16 = sbt("w16S", (128, E * TT))
    rninv = sbt("rninvS", (128, TT))
    gmax = sbt("gmaxS", (128, 1))
    smallA = sbt("smallAS", (128, 64))
    smallB = sbt("smallBS", (128, 64))
    smallC = sbt("smallCS", (128, 512))
    onesr = sbt("onesrS", (1, 128))
    sg16 = sbt("sg16S", (128, 16))
    neg16 = sbt("neg16S", (128, 16))
    snN = sbt("snNS", (128, 16 * NCH))
    g1 = sbt("g1S", (1, 16))
    amax1 = sbt("amax1S", (1, 8))
    smallD = sbt("smallDS", (128, 512))
    onec = sbt("onecS", (128, 1))
    w8 = sbt("w8S", (128, NCH * D), FP8)
    bft = sbt("bftS", (128, 512), BF16)
    q8t = sbt("q8tS", (128, 256), FP8)
    psum = nc.alloc_psum_tensor("psA", [128, 4096], F32).ap()

    p = P(nc)

    def pl(off, n):
        return pool[:, off:off + n]

    def mm(out_ap, lhsT, rhs, start, stop, sem=None):
        return p.emit("tensor",
                      lambda e: e.matmul(out_ap, lhsT, rhs, start=start, stop=stop),
                      sem)

    def tr(out_ap, in_ap, idn=None, sem=None):
        i_ = ident if idn is None else idn
        return p.emit("tensor", lambda e: e.transpose(out_ap, in_ap, i_), sem)

    def vcp(out_ap, in_ap, sem=None, eng="vector"):
        return p.emit(eng, lambda e: e.tensor_copy(out=out_ap, in_=in_ap), sem)

    def ts(out_ap, in0, s1, s2, op0, op1=None, sem=None, eng="vector"):
        return p.emit(eng, lambda e: e.tensor_scalar(
            out=out_ap, in0=in0, scalar1=s1, scalar2=s2, op0=op0,
            op1=(ALU.bypass if op1 is None else op1)), sem)

    def tsc(out_ap, in0, s1, s2, op0, op1=None, eng="vector"):
        return p.chain(eng, lambda e: e.tensor_scalar(
            out=out_ap, in0=in0, scalar1=s1, scalar2=s2, op0=op0,
            op1=(ALU.bypass if op1 is None else op1)))

    def tt2c(out_ap, in0, in1, op, eng="vector"):
        return p.chain(eng, lambda e: e.tensor_tensor(
            out=out_ap, in0=in0, in1=in1, op=op))

    def rsumc(out_ap, in_ap):
        return p.chain("vector", lambda e: e.reduce_sum(out_ap, in_ap,
                                                        axis=AX.X))

    def rmaxc(out_ap, in_ap, negate=False, absval=False):
        kw = {}
        if negate:
            kw["negate"] = True
        if absval:
            kw["apply_absolute_value"] = True
        return p.chain("vector", lambda e: e.reduce_max(out_ap, in_ap,
                                                        axis=AX.X, **kw))

    def recipc(out_ap, in_ap):
        return p.chain("vector", lambda e: e.reciprocal(out_ap, in_ap))

    def sttc(out_ap, in0, scalar, in1, op0, op1):
        return p.chain("vector", lambda e: e.scalar_tensor_tensor(
            out=out_ap, in0=in0, scalar=scalar, in1=in1, op0=op0, op1=op1))

    def vcpc(out_ap, in_ap, eng="vector"):
        return p.chain(eng, lambda e: e.tensor_copy(out=out_ap, in_=in_ap))

    def tt2(out_ap, in0, in1, op, sem=None, eng="vector"):
        return p.emit(eng, lambda e: e.tensor_tensor(
            out=out_ap, in0=in0, in1=in1, op=op), sem)

    def stt(out_ap, in0, scalar, in1, op0, op1, sem=None):
        return p.emit("vector", lambda e: e.scalar_tensor_tensor(
            out=out_ap, in0=in0, scalar=scalar, in1=in1, op0=op0, op1=op1), sem)

    def act(out_ap, in_ap, func, bias=0.0, scale=1.0, accum=None, sem=None):
        kw = {"accum_out": accum} if accum is not None else {}
        return p.emit("scalar", lambda e: e.activation(
            out=out_ap, in_=in_ap, func=func, bias=bias, scale=scale, **kw), sem)

    def rmax(out_ap, in_ap, negate=False, absval=False, sem=None, ):
        kw = {}
        if negate:
            kw["negate"] = True
        if absval:
            kw["apply_absolute_value"] = True
        return p.emit("vector", lambda e: e.reduce_max(out_ap, in_ap, axis=AX.X,
                                                       **kw), sem)

    def rsum(out_ap, in_ap, sem=None):
        return p.emit("vector", lambda e: e.reduce_sum(out_ap, in_ap, axis=AX.X),
                      sem)

    def recip(out_ap, in_ap, sem=None):
        return p.emit("vector", lambda e: e.reciprocal(out_ap, in_ap), sem)

    def sel(out_ap, mask, on_t, on_f, sem=None):
        return p.emit("vector", lambda e: e.select(
            out=out_ap, mask=mask, on_true=on_t, on_false=on_f), sem)

    def memset(ap_, v, eng="vector", sem=None):
        return p.emit(eng, lambda e: e.memset(ap_, v), sem)

    def xnc(c):
        return xn[:, T * c:T * (c + 1)]

    PVB = psum[:, 3584:3648]
    PVF = psum[:, 3584:3712]

    def pipe2(jobs):
        prev = None
        for f, b in jobs:
            f()
            if prev is not None:
                prev()
            prev = b
        if prev is not None:
            prev()

    def pair_bound(srcs, sq_off, scale_negs, outcols):
        sqO = (pl(sq_off, 512), pl(sq_off + 512, 512))
        pbO = (psum[:1, 3072:3584], psum[:1, 3584:4096])
        memset(smallB[:1, 30:30 + len(srcs)], 0.0, sem="v")
        p.wait("scalar", "v", p.cnt["v"])
        jobs = [(j2, so, ch) for j2, so in enumerate(srcs)
                for ch in range(T // 512)]
        nmmd = {0: None, 1: None}
        rdone = {0: None, 1: None}
        pend = None
        for i, (j2, so, ch) in enumerate(jobs):
            par = i % 2
            p.wait("scalar", "mm", nmmd[par])
            sb = act(sqO[par], pl(so + 512 * ch, 512), AF.Square, sem="s")
            if pend is not None:
                pend()

            def pend(j2=j2, par=par, sb=sb):
                p.wait("tensor", "s", sb)
                p.wait("tensor", "v", rdone[par])
                nmm = mm(pbO[par], onec, sqO[par], True, True, sem="mm")
                nmmd[par] = nmm
                p.wait("vector", "mm", nmm)
                rmaxc(smallA[:1, 16 + par:17 + par], pbO[par])
                rdone[par] = tt2c(smallB[:1, 30 + j2:31 + j2],
                                  smallB[:1, 30 + j2:31 + j2],
                                  smallA[:1, 16 + par:17 + par], ALU.max)
        pend()
        p.wait("scalar", "v", p.cnt["v"])
        s4 = None
        for j2 in range(len(srcs)):
            s4 = act(smallA[:1, 17 + j2:18 + j2], smallB[:1, 30 + j2:31 + j2],
                     AF.Sqrt, sem="s")
        p.wait("vector", "s", s4)
        for scale_neg, outc in zip(scale_negs, outcols):
            if len(srcs) >= 2:
                tt2c(smallA[:1, 20:21], smallA[:1, 17:18], smallA[:1, 18:19],
                     ALU.mult)
            else:
                tt2c(smallA[:1, 20:21], smallA[:1, 17:18], smallA[:1, 19:20],
                     ALU.mult)
            v5 = tsc(smallA[:1, 21:22], smallA[:1, 20:21], scale_neg, None,
                     ALU.mult)
            p.wait("tensor", "v", v5)
            bb = mm(psum[:, 3584:3585], onesr, smallA[:1, 21:22], True, True,
                    sem="mm")
            p.wait("vector", "mm", bb)
            vb = vcp(smallB[:, outc:outc + 1], psum[:, 3584:3585], sem="v")
            p.wait("scalar", "v", vb)
            p.wait("tensor", "v", vb)

    for nm, dst in (("ident", ident), ("causmask", causm), ("dcorr", dcorr),
                    ("dcorrT", dcorrT), ("e2plus", e2plus),
                    ("causmT", causmT),
                    ("eplus", eplus), ("eminus", eminus),
                    ("escA", escA), ("escB", escB), ("l0_gates", g1)):
        p.dma("sync", dst, dr[nm], "dC")
    snT = pool[:16, 0:D]
    p.dma("sync", snT, dr["l0_snT"], "dC")
    vC = p.dma("sync", acc, x0[0].rearrange("(t p) d -> p t d", p=128), "dC")
    p.wait("vector", "dC", vC)
    memset(neg16, -BIG)
    memset(onesr, 1.0)
    memset(onec, 1.0)
    memset(gmax, 0.0)
    p.wait("scalar", "dC", vC)
    act(g1, g1, AF.Sigmoid)
    act(psum[:16, 512:512 + D], snT, AF.Square, accum=smallA[:16, 0:1])
    p.barrier()
    mm(psum[:, 0:16], onesr, g1, True, True)
    act(smallA[:16, 1:2], smallA[:16, 0:1], AF.Sqrt)
    p.barrier()
    vcp(sg16, psum[:, 0:16])
    ts(smallA[:16, 2:3], smallA[:16, 1:2], 1e-12, None, ALU.max)
    p.barrier()
    recip(smallA[:16, 3:4], smallA[:16, 2:3])
    p.barrier()
    ts(snT, snT, smallA[:16, 3:4], None, ALU.mult)
    p.barrier()
    for c in range(NCH):
        tr(psum[:, 1024 + 16 * c:1024 + 16 * (c + 1)],
           snT[:, 128 * c:128 * (c + 1)], ident[:16, :16])
    p.barrier()
    vcp(snN, psum[:, 1024:1024 + 16 * NCH])
    p.barrier()

    def phase_ln(layer, track_amax, dbg_tap=None):
        SQ = psum[:, 2048:2048 + D]
        TR = psum[:, 3072:3072 + 1024]
        xt = pl(POOL - 1024, 1024)
        for t in range(TT):
            a = acc[:, D * t:D * (t + 1)]
            rs, nm_, ssq = smallA[:, 0:1], smallA[:, 1:2], smallA[:, 2:3]
            rstd = smallA[:, 5:6]
            rsumc(rs, a)
            v2 = tsc(nm_, rs, -1.0 / D, None, ALU.mult)
            p.wait("scalar", "v", v2)
            s1 = act(SQ, a, AF.Square, bias=nm_, accum=ssq, sem="s")
            p.wait("vector", "s", s1)
            v3 = tsc(smallA[:, 3:4], ssq, 1.0 / D, EPS_LN, ALU.mult, ALU.add)
            p.wait("scalar", "v", v3)
            s2 = act(smallA[:, 4:5], smallA[:, 3:4], AF.Sqrt, sem="s")
            if layer == 0:
                s2 = act(smallA[:, 8:9], ssq, AF.Sqrt, sem="s")
            p.wait("vector", "s", s2)
            recipc(rstd, smallA[:, 4:5])
            if layer == 0:
                tt2c(smallA[:, 6:7], smallA[:, 8:9], rstd, ALU.mult)
                tsc(smallA[:, 7:8], smallA[:, 6:7], 1e-12, None, ALU.max)
                recipc(rninv[:, t:t + 1], smallA[:, 7:8])
            if track_amax:
                rmaxc(smallB[:, 0:1], SQ)
                tt2c(smallB[:, 1:2], rstd, rstd, ALU.mult)
                tt2c(smallB[:, 2:3], smallB[:, 0:1], smallB[:, 1:2], ALU.mult)
                tt2c(gmax, gmax, smallB[:, 2:3], ALU.max)
            v4 = ts(xt, a, nm_, rstd, ALU.add, ALU.mult, sem="v")
            p.wait("tensor", "v", v4)
            tms = None
            for c in range(NCH):
                tms = tr(TR[:, 128 * c:128 * (c + 1)], xt[:, 128 * c:128 * (c + 1)],
                         sem="mm" if c == NCH - 1 else None)
            p.wait("vector", "mm", tms)
            src = psum[:, 3072:3072 + 1024].rearrange("p (c r) -> p c r", r=128)
            dst = xn.rearrange("p (c tk) -> p c tk", c=NCH)[:, :,
                                                            128 * t:128 * (t + 1)]
            v5 = vcp(dst, src, sem="v")
            p.wait("tensor", "v", v5)
        p.barrier()
        if dbg_tap is not None:
            dd = p.dma("sync", dbg_tap, xn, "dD")
            p.wait("sync", "dD", dd)

    def phase_l0ffn(seq):
        ACCQ, ACCK = 0, TT * DH
        WQKV = 3 * TT * DH
        WSZ = 3 * NCH * DH
        KT = 2 * TT * DH
        QTf = KT + T
        OBUF = QTf + T
        EXPB = OBUF + TT * DH
        ACCV = WQKV + 2 * WSZ
        GS = WQKV
        vz0 = memset(pool[:, ACCV:ACCV + 129 * TT].rearrange(
            "p (b c) -> p b c", c=129)[:, :, 0:1], 1.0, sem="v")
        p.wait("scalar", "v", vz0)
        p.wait("tensor", "v", vz0)
        for t in range(TT):
            lg = psum[:16, 0:128]
            mt = None
            for c in range(NCH):
                mt = mm(lg, snN[:, 16 * c:16 * (c + 1)],
                        xnc(c)[:, 128 * t:128 * (t + 1)], c == 0, c == NCH - 1,
                        sem="mm" if c == NCH - 1 else None)
            p.wait("vector", "mm", mt)
            v1 = vcp(smallC[:16, 0:128], lg, sem="v")
            p.wait("tensor", "v", v1)
            t1 = tr(psum[:, 512:512 + 16], smallC[:16, 0:128], ident[:16, :16],
                    sem="mm")
            p.wait("vector", "mm", t1)
            lt = pl(GS, 16)
            g_ = pl(GS + 16, 16)
            hc = pl(GS + 32, 16)
            sl = pl(GS + 48, 16)
            e16 = pl(GS + 64, 16)
            fb = pl(GS + 80, 16)
            l2 = pl(GS + 96, 16)
            tsc(lt, psum[:, 512:528], rninv[:, t:t + 1], None, ALU.mult)
            tt2c(lt, lt, sg16, ALU.subtract)
            tsc(g_, lt, 0.0, None, ALU.max)
            tsc(hc, lt, 0.0, None, ALU.is_gt)
            rsumc(smallB[:, 3:4], hc)
            tsc(smallB[:, 4:5], smallB[:, 3:4], 0.5, None, ALU.is_lt)
            rmaxc(smallB[:, 5:6], lt)
            tsc(fb, lt, smallB[:, 5:6], None, ALU.is_equal)
            sttc(l2, fb, -BIG, lt, ALU.mult, ALU.add)
            rmaxc(smallB[:, 6:7], l2)
            tsc(fb, lt, smallB[:, 6:7], None, ALU.is_ge)
            sttc(sl, fb, smallB[:, 4:5], hc, ALU.mult, ALU.add)
            tt2c(l2, g_, sl, ALU.mult)
            tsc(fb, sl, 1.0, BIG, ALU.subtract, ALU.mult)
            tt2c(e16, l2, fb, ALU.add)
            v2 = rmaxc(smallB[:, 7:8], e16, negate=True)
            p.wait("scalar", "v", v2)
            s1 = act(e16, e16, AF.Exp, bias=smallB[:, 7:8], accum=smallB[:, 8:9],
                     sem="s")
            p.wait("vector", "s", s1)
            recipc(smallB[:, 9:10], smallB[:, 8:9])
            vlast = ts(w16[:, E * t:E * (t + 1)], e16, smallB[:, 9:10], None,
                       ALU.mult, sem="v")
            p.wait("tensor", "v", vlast)
        p.barrier()
        if cfg.debug and seq == 0:
            dd = p.dma("sync", dbg["d_w16"], w16, "dD")
            p.wait("sync", "dD", dd)

        pqkvO = (psum[:, 0:384], psum[:, 512:896])
        mm_done = {}
        eevd = {}
        epend = []
        for ei in range(E):
            buf = WQKV + (ei % 2) * WSZ
            sname = "dWa" if ei % 2 == 0 else "dWb"
            if ei >= 2:
                p.wait("sync", "mm", mm_done[ei - 2])
            ldt = p.dma("sync", pl(buf, WSZ),
                        dr["l0_qkvp"][ei].rearrange("(c p) h -> p c h", p=128),
                        sname)
            p.wait("tensor", sname, ldt)
            for t in range(TT):
                b = (ei * TT + t) % 2
                pqkv = pqkvO[b]
                p.wait("tensor", "ev", eevd.get(b))
                last = None
                for c in range(NCH):
                    lhs = xnc(c)[:, 128 * t:128 * (t + 1)]
                    last = mm(pqkv, lhs, pl(buf + 3 * DH * c, 3 * DH),
                              c == 0, c == NCH - 1,
                              sem="mm" if c == NCH - 1 else None)
                mm_done[ei] = last

                def evac(ei=ei, t=t, pqkv=pqkv, last=last, b=b):
                    p.wait("vector", "mm", last)
                    wcol = w16[:, E * t + ei:E * t + ei + 1]
                    for j, accb in enumerate((ACCQ, ACCK, ACCV)):
                        ps_ = pqkv[:, 128 * j:128 * (j + 1)]
                        if j == 2:
                            dst = pl(accb + 129 * t + 1, DH)
                        else:
                            dst = pl(accb + DH * t, DH)
                        if ei == 0:
                            eevd[b] = ts(dst, ps_, wcol, None, ALU.mult,
                                         sem="ev" if j == 2 else None)
                        else:
                            eevd[b] = stt(dst, ps_, wcol, dst, ALU.mult,
                                          ALU.add, sem="ev" if j == 2 else None)
                epend.append(evac)
                if len(epend) >= 2:
                    epend.pop(0)()
        for f in epend:
            f()
        p.barrier()
        if cfg.debug and seq == 0:
            dd = p.dma("sync", dbg["d_accq"], pl(ACCQ, TT * DH), "dD")
            p.wait("sync", "dD", dd)

        for srcb, dstb in ((ACCK, KT), (ACCQ, QTf)):
            tpend = []
            tev_last = None
            for t8 in range((TT + 7) // 8):
                nblk = min(8, TT - 8 * t8)
                base = 2048 + 1024 * (t8 % 2)
                tl = None
                for i in range(nblk):
                    kx = 8 * t8 + i
                    tl = tr(psum[:, base + 128 * i:base + 128 * (i + 1)],
                            pl(srcb + DH * kx, DH),
                            sem="mm" if i == nblk - 1 else None)

                def evac(t8=t8, nblk=nblk, base=base, tl=tl, dstb=dstb):
                    p.wait("vector", "mm", tl)
                    return vcp(pl(dstb + 1024 * t8, 128 * nblk),
                               psum[:, base:base + 128 * nblk], sem="ev")
                tpend.append(evac)
                if len(tpend) >= 2:
                    tev_last = tpend.pop(0)()
            for f in tpend:
                tev_last = f()
            p.wait("tensor", "ev", tev_last)
        p.barrier()
        SQB = psum[:, 3072:3200]
        memset(smallB[:, 24:26], 0.0, sem="v")
        p.wait("scalar", "v", p.cnt["v"])
        for j2, srcb in enumerate((ACCQ, ACCK)):
            for t in range(TT):
                sb = act(SQB, pl(srcb + DH * t, DH), AF.Square,
                         accum=smallB[:, 28:29], sem="s")
                p.wait("vector", "s", sb)
                tt2c(smallB[:, 24 + j2:25 + j2], smallB[:, 24 + j2:25 + j2],
                     smallB[:, 28:29], ALU.max)
                p.wait("scalar", "v", p.cnt["v"])
        vq = p.cnt["v"]
        p.wait("tensor", "v", vq)
        t1 = tr(psum[:1, 3200:3328], smallB[:, 24:25], sem="mm")
        t2 = tr(psum[:1, 3328:3456], smallB[:, 25:26], sem="mm")
        p.wait("vector", "mm", t2)
        vcpc(smallC[:1, 0:128], psum[:1, 3200:3328])
        rmaxc(smallA[:1, 16:17], smallC[:1, 0:128])
        vcpc(smallC[:1, 0:128], psum[:1, 3328:3456])
        v3 = rmaxc(smallA[:1, 17:18], smallC[:1, 0:128])
        p.wait("scalar", "v", v3)
        act(smallA[:1, 18:19], smallA[:1, 16:17], AF.Sqrt)
        s4 = act(smallA[:1, 19:20], smallA[:1, 17:18], AF.Sqrt, sem="s")
        p.wait("vector", "s", s4)
        tt2c(smallA[:1, 20:21], smallA[:1, 18:19], smallA[:1, 19:20], ALU.mult)
        v5 = tsc(smallA[:1, 21:22], smallA[:1, 20:21], -SC_FFN, None, ALU.mult)
        p.wait("tensor", "v", v5)
        bb = mm(psum[:, 3456:3457], onesr, smallA[:1, 21:22], True, True,
                sem="mm")
        p.wait("vector", "mm", bb)
        vb = vcp(smallB[:, 27:28], psum[:, 3456:3457], sem="v")
        p.wait("scalar", "v", vb)
        nbnd = smallB[:, 27:28]
        p.barrier()

        sexp = {0: None, 1: None, 2: None}
        pvdone = {0: None, 1: None, 2: None}
        scb = (0, 512, 3072)
        gevd = {}
        gpend = []
        pvlast = {}
        for qtg in range(TT // 4):
            for f in gpend:
                f()
            gpend = []
            pends = []
            for kc in range(TT):
                par = (qtg * TT + kc) % 3
                SCh = psum[:, scb[par]:scb[par] + 512]
                p.wait("tensor", "s", sexp[par])
                smm = mm(SCh, pl(KT + 128 * kc, 128),
                         pl(QTf + 512 * qtg, 512), True, True, sem="mm")
                p.wait("scalar", "mm", smm)
                p.wait("scalar", "mm", pvdone[par])
                s1 = act(pl(EXPB + 512 * par, 512), SCh, AF.Exp, bias=nbnd,
                         scale=SC_FFN, sem="s")
                sexp[par] = s1

                def pend_pv(kc=kc, par=par, s1=s1):
                    p.wait("tensor", "s", s1)
                    for qi in range(4):
                        pvb = psum[:, 1024 + 512 * qi:1024 + 512 * qi + 129]
                        if kc == 0:
                            p.wait("tensor", "ev", gevd.get(qi))
                        pvlast[qi] = mm(
                            pvb, pool[:, EXPB + 512 * par + 128 * qi:
                                      EXPB + 512 * par + 128 * (qi + 1)],
                            pl(ACCV + 129 * kc, 129), kc == 0, kc == TT - 1,
                            sem="mm" if qi == 3 else None)
                    pvdone[par] = pvlast[3]
                pends.append(pend_pv)
                if len(pends) >= 2:
                    pends.pop(0)()
            for f in pends:
                f()
            for qi in range(4):
                def evac(qi=qi, qt=4 * qtg + qi, last=pvlast[3],
                         pvb=psum[:, 1024 + 512 * qi:1024 + 512 * qi + 129]):
                    p.wait("vector", "mm", last)
                    recipc(smallA[:, 8 + qi:9 + qi], pvb[:, 0:1])
                    gevd[qi] = ts(pl(OBUF + DH * qt, DH), pvb[:, 1:129],
                                  smallA[:, 8 + qi:9 + qi], None, ALU.mult,
                                  sem="ev")
                gpend.append(evac)
        for f in gpend:
            f()
        for qi in range(4):
            p.wait("tensor", "ev", gevd.get(qi))
        p.barrier()
        if cfg.debug and seq == 0:
            dd = p.dma("sync", dbg["d_o"], pl(OBUF, TT * DH), "dD")
            p.wait("sync", "dD", dd)

        ZET = pl(EXPB, 256)
        OPW = 0
        OPSZ = 2 * D
        ztmp = pl(EXPB + 256, 128)
        mmo_done = {}
        opb = [0]
        opevd = {}
        oppend = []
        dblw = OBUF >= 2 * OPSZ
        for es in range(E // 2):
            ob = OPW + OPSZ * (es % 2) if dblw else OPW
            sname = "dOa" if (es % 2 == 0 or not dblw) else "dOb"
            gate = es - 2 if dblw else es - 1
            if gate >= 0:
                p.wait("sync", "mm", mmo_done[gate])
            ld2 = p.dma("sync", pl(ob, OPSZ),
                        dr["l0_op"][2 * es:2 * es + 2].rearrange("e p j -> p e j"),
                        sname)
            p.wait("tensor", sname, ld2)
            for t in range(TT):
                for k in range(2):
                    ei = 2 * es + k
                    wcol = w16[:, E * t + ei:E * t + ei + 1]
                    vz = ts(ztmp, pl(OBUF + DH * t, DH), wcol, None, ALU.mult,
                            sem="v")
                    p.wait("tensor", "v", vz)
                    tz = tr(psum[:, 2048 + 128 * k:2048 + 128 * (k + 1)], ztmp,
                            sem="mm")
                    p.wait("vector", "mm", tz)
                    ez = vcp(ZET[:, 128 * k:128 * (k + 1)],
                             psum[:, 2048 + 128 * k:2048 + 128 * (k + 1)],
                             sem="ev")
                    p.wait("tensor", "ev", ez)
                for ncol in range(2):
                    b = opb[0] % 2
                    opb[0] += 1
                    po = psum[:, 512 * b:512 * (b + 1)]
                    p.wait("tensor", "ev", opevd.get(b))
                    last = None
                    for k in range(2):
                        last = mm(po, ZET[:, 128 * k:128 * (k + 1)],
                                  pl(ob + D * k + 512 * ncol, 512),
                                  k == 0, k == 1, sem="mm" if k == 1 else None)
                    if t == TT - 1 and ncol == 1:
                        mmo_done[es] = last

                    def evac(t=t, ncol=ncol, po=po, last=last, b=b):
                        p.wait("vector", "mm", last)
                        dst = acc[:, D * t + 512 * ncol:
                                  D * t + 512 * (ncol + 1)]
                        opevd[b] = tt2(dst, po, dst, ALU.add, sem="ev")
                    oppend.append(evac)
                    if len(oppend) >= 2:
                        oppend.pop(0)()
        for f in oppend:
            f()
        p.barrier()
        if cfg.debug and seq == 0:
            dd = p.dma("sync", dbg["d_acc_ffn"], acc, "dD")
            p.wait("sync", "dD", dd)

    def phase_l0attn(seq):
        QF, KF = 0, 4096
        AT2 = 8192
        VP = 10240
        PB = 12304
        WSL = 13312
        EXPB0 = WSL
        TBC, TBS = 12288, 12800
        S0, S1_, S2, S3 = 8192, 8704, 9216, 9728
        free_wsl = None
        for g in range(NG):
            for wsrc, dstoff in ((dr["l0_wqg"], QF), (dr["l0_wkg"], KF)):
                p.wait("sync", "mm", free_wsl)
                lw = p.dma("sync", pl(WSL, 2048),
                           wsrc[g].rearrange("(c p) m -> p c m", p=128), "dW")
                p.wait("tensor", "dW", lw)
                qkevd = {}
                for sp in range(0, T, SPAN):
                    p.dma("sync", pl(TBC, SPAN), dr["ctab"][:, sp:sp + SPAN], "dT")
                    lt_ = p.dma("sync", pl(TBS, SPAN), dr["stab"][:, sp:sp + SPAN],
                                "dT")
                    qkpend = []
                    for mc in range(2):
                        last = None
                        p.wait("tensor", "ev", qkevd.get(mc))
                        for j in range(0, SPAN, 512):
                            for c in range(NCH):
                                last = mm(psum[:, 512 * mc:512 * mc + 512],
                                          pl(WSL + 256 * c + 128 * mc, 128),
                                          xnc(c)[:, sp + j:sp + j + 512],
                                          c == 0, c == NCH - 1,
                                          sem="mm" if c == NCH - 1 else None)
                            free_wsl = last

                        def qkevac(mc=mc, last=last):
                            p.wait("vector", "mm", last)
                            dst = pl((S0 if mc == 0 else S1_), 512)
                            qkevd[mc] = vcp(
                                dst, psum[:, 512 * mc:512 * mc + 512],
                                sem="ev")
                        qkpend.append(qkevac)
                        if len(qkpend) >= 2:
                            qkpend.pop(0)()
                    for f in qkpend:
                        f()
                    p.wait("vector", "dT", lt_)
                    p.wait("vector", "ev", p.cnt["ev"])
                    PE_, PO_ = pl(S0, SPAN), pl(S1_, SPAN)
                    tbc, tbs = pl(TBC, SPAN), pl(TBS, SPAN)
                    tt2(pl(S2, SPAN), PE_, tbc, ALU.mult)
                    tt2c(pl(S3, SPAN), PO_, tbs, ALU.mult)
                    tt2c(pl(S2, SPAN), pl(S2, SPAN), pl(S3, SPAN), ALU.subtract)
                    tt2(pl(S3, SPAN), PE_, tbs, ALU.mult)
                    tt2c(PO_, PO_, tbc, ALU.mult)
                    vr = p.chain("vector", lambda e, a_=pl(S3, SPAN), b_=PO_:
                                 e.tensor_tensor(out=b_, in0=a_, in1=b_,
                                                 op=ALU.add))
                    p.wait("sync", "v", vr)
                    rr = None
                    for hp in range(4):
                        pair_off = dstoff + 2048 * (hp // 2)
                        row = 64 * (hp % 2)
                        for half, srcoff in ((0, S2), (32, S1_)):
                            srcap = pool[32 * hp:32 * hp + 32,
                                         srcoff:srcoff + SPAN]
                            dstap = pool[row + half:row + half + 32,
                                         pair_off + sp:pair_off + sp + SPAN]
                            rr = p.dma("sync", dstap, srcap, "dR")
                    p.wait("vector", "dR", rr)
                    p.wait("tensor", "dR", rr)
            for pr2 in range(2):
                pr = 2 * g + pr2
                p.wait("sync", "mm", free_wsl)
                lv = p.dma("sync", pl(WSL, 1024),
                           dr["l0_wvp"][pr].rearrange("(c p) m -> p c m", p=128),
                           "dW")
                p.wait("tensor", "dW", lv)
                p.wait("vector", "mm", free_wsl)
                vzp = memset(pool[:, VP:VP + 129 * TT].rearrange(
                    "p (b c) -> p b c", c=129)[:, :, 64:65], 1.0, sem="v")
                p.wait("scalar", "v", vzp)
                p.wait("tensor", "v", vzp)
                pvvO = (psum[:, 1024:1152], psum[:, 1536:1664])
                vevd = {}
                vpend = []
                for t in range(TT):
                    b = t % 2
                    pvv = pvvO[b]
                    p.wait("tensor", "ev", vevd.get(b))
                    last = None
                    for c in range(NCH):
                        last = mm(pvv, xnc(c)[:, 128 * t:128 * (t + 1)],
                                  pl(WSL + 128 * c, 128), c == 0, c == NCH - 1,
                                  sem="mm" if c == NCH - 1 else None)
                    free_wsl = last

                    def evac(t=t, pvv=pvv, last=last, b=b):
                        p.wait("vector", "mm", last)
                        vcp(pl(VP + 129 * t, 64), pvv[:, 0:64])
                        vevd[b] = vcp(pl(VP + 129 * t + 65, 64),
                                      pvv[:, 64:128], sem="ev")
                    vpend.append(evac)
                    if len(vpend) >= 2:
                        vpend.pop(0)()
                for f in vpend:
                    f()
                p.wait("tensor", "ev", vevd.get(0))
                p.wait("tensor", "ev", vevd.get(1))
                pair_bound((QF + 2048 * pr2, KF + 2048 * pr2), PB,
                           (-SC_L0,), (27,))
                nbnd = smallB[:, 27:28]
                p.wait("tensor", "s", p.cnt["s"])
                gevd = {}
                qoff = QF + 2048 * pr2
                koff = KF + 2048 * pr2
                sexp = {}
                pvdone = {}
                scb0 = {0: (0, 512), 1: (3072, 3584)}
                exb0 = {0: (EXPB0, EXPB0 + 512), 1: (14336, 14848)}
                parc = [0]
                gpend = []
                for qtg in range(TT // 4):
                    for f in gpend:
                        f()
                    gpend = []
                    pvstop = {}
                    pends = []
                    for kc in range(4 * qtg + 4):
                        qi0 = max(0, kc - 4 * qtg)
                        par = parc[0] % 2
                        parc[0] += 1
                        w = 128 * (4 - qi0)
                        smms = {}
                        for hh in range(2):
                            qrow = 64 * hh
                            SCh = psum[:, scb0[hh][par]:scb0[hh][par] + w]
                            p.wait("tensor", "s", sexp.get((hh, par)))
                            smms[hh] = mm(
                                SCh,
                                pool[qrow:qrow + 64,
                                     koff + 128 * kc:koff + 128 * (kc + 1)],
                                pool[qrow:qrow + 64,
                                     qoff + 512 * qtg + 128 * qi0:
                                     qoff + 512 * (qtg + 1)],
                                True, True, sem="mm")
                        s1s = {}
                        for hh in range(2):
                            SCh = psum[:, scb0[hh][par]:scb0[hh][par] + w]
                            if kc >= 4 * qtg:
                                p.wait("vector", "mm", smms[hh])
                                vdg = tt2c(SCh[:, 0:128], SCh[:, 0:128],
                                           causmT, ALU.add)
                                p.wait("scalar", "v", vdg)
                            p.wait("scalar", "mm", smms[hh])
                            p.wait("scalar", "mm", pvdone.get((hh, par)))
                            s1s[hh] = act(pl(exb0[hh][par], w), SCh, AF.Exp,
                                          bias=nbnd, scale=SC_L0, sem="s")
                            sexp[(hh, par)] = s1s[hh]

                        def pend_pv(kc=kc, par=par, s1s=dict(s1s), qi0=qi0,
                                    qtg=qtg):
                            p.wait("tensor", "s", s1s[1])
                            for qi in range(qi0, 4):
                                qt = 4 * qtg + qi
                                if kc == 0:
                                    p.wait("tensor", "ev", gevd.get(qi))
                                stop = kc == qt
                                pvl = None
                                for hh in range(2):
                                    pvb = psum[:, 1024 + 512 * qi + 128 * hh:
                                               1024 + 512 * qi + 128 * hh + 65]
                                    pvl = mm(
                                        pvb,
                                        pool[:, exb0[hh][par] +
                                             128 * (qi - qi0):
                                             exb0[hh][par] +
                                             128 * (qi - qi0) + 128],
                                        pl(VP + 129 * kc + 64 * hh, 65),
                                        kc == 0 and hh == 0,
                                        stop and hh == 1,
                                        sem="mm" if (stop or qi == 3)
                                        and hh == 1 else None)
                                if stop:
                                    pvstop[qi] = pvl
                            pvdone[(0, par)] = pvl
                            pvdone[(1, par)] = pvl
                        pends.append(pend_pv)
                        if len(pends) >= 2:
                            pends.pop(0)()
                    for f in pends:
                        f()
                    for qi in range(4):
                        def evac(qi=qi, qt=4 * qtg + qi, last=pvstop[qi]):
                            p.wait("vector", "mm", last)
                            for hh in range(2):
                                pvb = psum[:, 1024 + 512 * qi + 128 * hh:
                                           1024 + 512 * qi + 128 * hh + 65]
                                zc = pvb[:, 64:65] if hh == 0 else pvb[:, 0:1]
                                vo = 0 if hh == 0 else 1
                                rc_ = 8 + qi if hh == 0 else 40 + qi
                                recipc(smallA[:, rc_:rc_ + 1], zc)
                                gevd[qi] = ts(
                                    pl(AT2 + 128 * qt + 64 * hh, 64),
                                    pvb[:, vo:vo + 64],
                                    smallA[:, rc_:rc_ + 1], None, ALU.mult,
                                    sem="ev")
                        gpend.append(evac)
                for f in gpend:
                    f()
                for qi in range(4):
                    p.wait("tensor", "ev", gevd.get(qi))
                p.wait("sync", "mm", p.cnt["mm"])
                lwo = p.dma("sync", pl(WSL + 1024, 1024), dr["l0_wop"][pr], "dW")
                ATT = VP
                tpend = []
                tev_last = None
                for t8 in range((TT + 7) // 8):
                    nblk = min(8, TT - 8 * t8)
                    base = 2048 + 1024 * (t8 % 2)
                    tl = None
                    for i in range(nblk):
                        kx = 8 * t8 + i
                        tl = tr(psum[:, base + 128 * i:base + 128 * (i + 1)],
                                pl(AT2 + 128 * kx, 128),
                                sem="mm" if i == nblk - 1 else None)

                    def evac(t8=t8, nblk=nblk, base=base, tl=tl):
                        p.wait("vector", "mm", tl)
                        return vcp(pl(ATT + 1024 * t8, 128 * nblk),
                                   psum[:, base:base + 128 * nblk], sem="ev")
                    tpend.append(evac)
                    if len(tpend) >= 2:
                        tev_last = tpend.pop(0)()
                for f in tpend:
                    tev_last = f()
                p.wait("tensor", "ev", tev_last)
                p.wait("tensor", "dW", lwo)
                evd = {}
                pend = []
                for i, (t, ncol) in enumerate(
                        (t, ncol) for t in range(TT) for ncol in range(2)):
                    b = i % 2
                    po = psum[:, 512 * b:512 * (b + 1)]
                    p.wait("tensor", "ev", evd.get(b))
                    last = mm(po, pl(ATT + 128 * t, 128),
                              pl(WSL + 1024 + 512 * ncol, 512), True, True,
                              sem="mm")
                    free_wsl = last

                    def evac(t=t, ncol=ncol, po=po, last=last, b=b):
                        p.wait("vector", "mm", last)
                        dst = acc[:, D * t + 512 * ncol:
                                  D * t + 512 * (ncol + 1)]
                        evd[b] = tt2(dst, po, dst, ALU.add, sem="ev")
                    pend.append(evac)
                    if len(pend) >= 2:
                        pend.pop(0)()
                for f in pend:
                    f()
                p.wait("tensor", "ev", evd.get(0))
                p.wait("tensor", "ev", evd.get(1))
        p.barrier()
        if cfg.debug and seq == 0:
            dd = p.dma("sync", dbg["d_acc_l0"], acc, "dD")
            p.wait("sync", "dD", dd)
            p.dma("sync", dbg["d_pool0"][:, 0:128 * TT],
                  pool[:, 8192:8192 + 128 * TT], "dD")
            p.dma("sync", dbg["d_pool0"][:, 128 * TT:257 * TT],
                  pool[:, 10240:10240 + 129 * TT], "dD")
            p.dma("sync", dbg["d_pool0"][:, 257 * TT:385 * TT],
                  pool[:, 2048:2048 + 128 * TT], "dD")
            p.dma("sync", dbg["d_pool0"][:, 385 * TT:513 * TT],
                  pool[:, 6144:6144 + 128 * TT], "dD")
            dp = p.dma("sync", dbg["d_pool0"][:, 513 * TT:513 * TT + 130],
                       pool[:, 13312:13312 + 130], "dD")
            p.wait("sync", "dD", dp)

    def phase_l1attn(seq):
        LAT = 0
        WKVH = 4096
        KVTG = 8192
        WDN = 10240
        latT = pl(LAT, 2 * T)
        ldn = p.dma("sync", pl(WDN, 2048),
                    dr["l1_wdown"].rearrange("(c p) j -> p c j", p=128), "dW")
        p.wait("tensor", "dW", ldn)
        NTG = T // 512
        kv_mm_done = None
        for hkv in range(2):
            cols = slice(512 * hkv, 512 * (hkv + 1))
            p.wait("gpsimd", "mm", kv_mm_done)
            l1 = p.dma("gpsimd", pl(WKVH, 4096),
                       dr["l1_wk"][:, cols].rearrange("(c p) j -> p c j", p=128),
                       "dG")
            p.wait("gpsimd", "dG", l1)
            l2_ = p.emit("gpsimd", lambda e, cols=cols: e.dma_start(
                out=pl(WKVH, 4096),
                in_=dr["l1_wv"][:, cols].rearrange("(c p) j -> p c j", p=128),
                accum_op=ALU.add), "dG", 16)
            p.wait("tensor", "dG", l2_)
            kevd = {}
            levd = {}
            for tg in range(NTG):
                kpend = []
                for kvi in range(4):
                    kvc = 4 * hkv + kvi
                    b = kvi % 2
                    pkv = psum[:, 512 * b:512 * (b + 1)]
                    p.wait("tensor", "ev", kevd.get(b))
                    last = None
                    for c in range(NCH):
                        last = mm(pkv, pl(WKVH + 512 * c + 128 * kvi, 128),
                                  xnc(c)[:, 512 * tg:512 * (tg + 1)],
                                  c == 0, c == NCH - 1,
                                  sem="mm" if c == NCH - 1 else None)
                    kv_mm_done = last

                    def evac(kvi=kvi, pkv=pkv, last=last, b=b):
                        p.wait("vector", "mm", last)
                        kevd[b] = vcp(pl(KVTG + 512 * kvi, 512), pkv, sem="ev")
                    kpend.append(evac)
                    if len(kpend) >= 2:
                        kpend.pop(0)()
                for f in kpend:
                    f()
                p.wait("tensor", "ev", kevd.get(0))
                p.wait("tensor", "ev", kevd.get(1))
                lpend = []
                for mc2 in range(2):
                    b = mc2 % 2
                    plat = psum[:, 1024 + 512 * b:1024 + 512 * (b + 1)]
                    p.wait("tensor", "ev", levd.get(b))
                    last = None
                    for kvi in range(4):
                        kvc = 4 * hkv + kvi
                        last = mm(plat, pl(WDN + 256 * kvc + 128 * mc2, 128),
                                  pl(KVTG + 512 * kvi, 512), kvi == 0, kvi == 3,
                                  sem="mm" if kvi == 3 else None)

                    def evac(tg=tg, mc2=mc2, plat=plat, last=last, b=b,
                             hkv=hkv):
                        p.wait("vector", "mm", last)
                        dstl = latT[:, T * mc2 + 512 * tg:
                                    T * mc2 + 512 * (tg + 1)]
                        if hkv == 0:
                            levd[b] = vcp(dstl, plat, sem="ev")
                        else:
                            levd[b] = tt2(dstl, plat, dstl, ALU.add, sem="ev")
                    lpend.append(evac)
                    if len(lpend) >= 2:
                        lpend.pop(0)()
                for f in lpend:
                    f()
            p.wait("tensor", "ev", levd.get(0))
            p.wait("tensor", "ev", levd.get(1))
        p.barrier()
        if cfg.debug and seq == 0:
            dd = p.dma("sync", dbg["d_latT"], latT, "dD")
            p.wait("sync", "dD", dd)

        RING, E2B, KTMP = 0, 3072, 3584
        Q2, KA, KB = 4096, 6144, 8192
        V2 = 10240
        AT2 = 12304
        WQS = 14352
        exb1 = {0: (1024, 1536), 1: (2048, 2560)}
        scb1 = {0: (0, 512), 1: (3072, 3584)}
        WUK = smallC[:, 0:256]
        WUV = smallC[:, 256:512]
        sl = p.dma("sync", spill_lat, latT, "dS")
        p.wait("sync", "dS", sl)
        p.wait("scalar", "dS", sl)
        p.wait("vector", "dS", sl)
        p.wait("tensor", "dS", sl)
        vzo = memset(pool[:, V2:V2 + 129 * TT].rearrange(
            "p (b c) -> p b c", c=129)[:, :, 64:65], 1.0, sem="v")
        p.wait("scalar", "v", vzo)
        p.wait("tensor", "v", vzo)
        free_wq = None
        for pr in range(NP):
            p.wait("sync", "mm", free_wq)
            lq = p.dma("sync", pl(WQS, 1024),
                       dr["l1_wqg"][pr].rearrange("(c p) m -> p c m", p=128), "dW")
            p.dma("sync", WUK,
                  dr["l1_wupkg"][pr].rearrange("(c p) m -> p c m", p=128), "dW")
            lv = p.dma("sync", WUV,
                       dr["l1_wupvg"][pr].rearrange("(c p) m -> p c m", p=128),
                       "dW")
            p.wait("tensor", "dW", lv)
            vzq = memset(smallB[:, 30:31], 0.0, sem="v")
            p.wait("scalar", "v", vzq)
            qtail = {}
            qpend2 = []
            for t in range(TT):
                b = t % 2
                ptq = psum[:, 512 * b:512 * b + 128]
                qo = 13328 + 128 * b
                trb = 2048 + 512 * b
                p.wait("tensor", "ev", qtail.get(b))
                last = None
                for c in range(NCH):
                    last = mm(ptq, xnc(c)[:, 128 * t:128 * (t + 1)],
                              pl(WQS + 128 * c, 128), c == 0, c == NCH - 1,
                              sem="mm" if c == NCH - 1 else None)
                free_wq = last

                def qchain(t=t, b=b, ptq=ptq, qo=qo, trb=trb, last=last):
                    acol = 36 if b == 0 else 37
                    p.wait("vector", "mm", last)
                    vsq = ts(pl(qo, 128), ptq, escA[:, t:t + 1], None,
                             ALU.mult, sem="v")
                    p.wait("scalar", "v", vsq)
                    sq = act(psum[:, 3072 + 128 * b:3200 + 128 * b],
                             pl(qo, 128), AF.Square,
                             accum=smallA[:, acol:acol + 1], sem="s")
                    p.wait("vector", "s", sq)
                    tt2c(smallB[:, 30:31], smallB[:, 30:31],
                         smallA[:, acol:acol + 1], ALU.max)
                    p.wait("tensor", "v", vsq)
                    tq = tr(psum[:, trb:trb + 128], pl(qo, 128), sem="mm")
                    p.wait("vector", "mm", tq)
                    qtail[b] = vcp(pl(Q2 + 128 * t, 128),
                                   psum[:, trb:trb + 128], sem="ev")
                qpend2.append(qchain)
                if len(qpend2) >= 2:
                    qpend2.pop(0)()
            for f in qpend2:
                f()
            p.wait("tensor", "v", p.cnt["v"])
            tqa = tr(psum[:1, 3328:3456], smallB[:, 30:31], sem="mm")
            p.wait("vector", "mm", tqa)
            vcpc(smallD[:1, 0:128], psum[:1, 3328:3456])
            v9 = rmaxc(smallA[:1, 23:24], smallD[:1, 0:128])
            p.wait("scalar", "v", v9)
            s9 = act(smallA[:1, 38:39], smallA[:1, 23:24], AF.Sqrt, sem="s")
            p.wait("vector", "s", s9)
            vz = memset(smallB[:, 28:30], 0.0, sem="v")
            p.wait("scalar", "v", vz)
            ktail = {}
            kpend2 = []
            lastuse = {}
            ldt = {}

            def lat_load(t):
                if t >= TT:
                    return
                s = t % 4
                p.wait("sync", "mm", lastuse.get(t - 4))
                p.dma("sync", pl(RING + 256 * s, 128),
                      spill_lat[:, 128 * t:128 * (t + 1)], f"dR{s}")
                ldt[t] = p.dma("sync", pl(RING + 256 * s + 128, 128),
                               spill_lat[:, T + 128 * t:T + 128 * (t + 1)],
                               f"dR{s}")
            for t in range(4):
                lat_load(t)
            for t in range(TT):
                b = t % 2
                s = t % 4
                pk_ = psum[:, 1024 + 512 * b:1024 + 512 * b + 128]
                pv_ = psum[:, 512 * b:512 * b + 128]
                ko = KTMP + 256 * b
                trb = 2048 + 512 * b
                p.wait("tensor", f"dR{s}", ldt[t])
                p.wait("tensor", "ev", ktail.get(b))
                mm(pk_, pl(RING + 256 * s, 128), WUK[:, 0:128],
                   True, False)
                mm(pk_, pl(RING + 256 * s + 128, 128),
                   WUK[:, 128:256], False, True, sem="mm")
                mm(pv_, pl(RING + 256 * s, 128), WUV[:, 0:128],
                   True, False)
                lastk = mm(pv_, pl(RING + 256 * s + 128, 128),
                           WUV[:, 128:256], False, True, sem="mm")
                lastuse[t] = lastk
                lat_load(t + 4)

                def chain(t=t, b=b, pk_=pk_, pv_=pv_, ko=ko, trb=trb,
                          lastk=lastk):
                    acol = 28 if b == 0 else 32
                    p.wait("vector", "mm", lastk)
                    ts(pl(ko, 128), pk_, eplus[:, t:t + 1], None, ALU.mult)
                    vb = ts(pl(ko + 128, 128), pk_, eminus[:, t:t + 1], None,
                            ALU.mult, sem="v")
                    vcp(pl(V2 + 129 * t, 64), pv_[:, 0:64])
                    vv = vcp(pl(V2 + 129 * t + 65, 64), pv_[:, 64:128],
                             sem="ev")
                    p.wait("scalar", "v", vb)
                    act(psum[:, 3072 + 256 * b:3200 + 256 * b], pl(ko, 128),
                        AF.Square, accum=smallA[:, acol:acol + 1])
                    sqb = act(psum[:, 3200 + 256 * b:3328 + 256 * b],
                              pl(ko + 128, 128), AF.Square,
                              accum=smallA[:, acol + 1:acol + 2], sem="s")
                    p.wait("vector", "s", sqb)
                    tt2c(smallB[:, 28:29], smallB[:, 28:29],
                         smallA[:, acol:acol + 1], ALU.max)
                    tt2c(smallB[:, 29:30], smallB[:, 29:30],
                         smallA[:, acol + 1:acol + 2], ALU.max)
                    p.wait("tensor", "v", vb)
                    tr(psum[:, trb:trb + 128], pl(ko, 128))
                    tB = tr(psum[:, trb + 128:trb + 256], pl(ko + 128, 128),
                            sem="mm")
                    p.wait("vector", "mm", tB)
                    vcp(pl(KA + 128 * t, 128), psum[:, trb:trb + 128])
                    ktail[b] = vcp(pl(KB + 128 * t, 128),
                                   psum[:, trb + 128:trb + 256], sem="ev")
                kpend2.append(chain)
                if len(kpend2) >= 2:
                    kpend2.pop(0)()
            for f in kpend2:
                f()
            p.wait("tensor", "v", p.cnt["v"])
            tka = tr(psum[:1, 3328:3456], smallB[:, 28:29], sem="mm")
            tkb = tr(psum[:1, 3456:3584], smallB[:, 29:30], sem="mm")
            p.wait("vector", "mm", tkb)
            vcpc(smallC[:1, 0:128], psum[:1, 3328:3456])
            rmaxc(smallA[:1, 24:25], smallC[:1, 0:128])
            vcpc(smallC[:1, 0:128], psum[:1, 3456:3584])
            v6 = rmaxc(smallA[:1, 25:26], smallC[:1, 0:128])
            p.wait("scalar", "v", v6)
            act(smallA[:1, 26:27], smallA[:1, 24:25], AF.Sqrt)
            s7 = act(smallA[:1, 27:28], smallA[:1, 25:26], AF.Sqrt, sem="s")
            p.wait("vector", "s", s7)
            tsc(smallA[:1, 27:28], smallA[:1, 27:28],
                float(np.exp(2.0 * C_DECAY * (T - 1))), None, ALU.mult)
            tt2c(smallA[:1, 19:20], smallA[:1, 26:27], smallA[:1, 27:28],
                 ALU.max)
            tt2c(smallA[:1, 20:21], smallA[:1, 38:39], smallA[:1, 19:20],
                 ALU.mult)
            v8 = tsc(smallA[:1, 21:22], smallA[:1, 20:21], -1.0, None,
                     ALU.mult)
            p.wait("tensor", "v", v8)
            bb8 = mm(psum[:, 3584:3585], onesr, smallA[:1, 21:22], True, True,
                     sem="mm")
            p.wait("vector", "mm", bb8)
            vb8 = vcp(smallB[:, 27:28], psum[:, 3584:3585], sem="v")
            p.wait("scalar", "v", vb8)
            nbnd = smallB[:, 27:28]
            gevd = {}
            p.wait("tensor", "s", p.cnt["s"])
            sexp = {}
            pvdone = {}
            parc = [0]
            gpend = []
            for qtg in range(TT // 4):
                for f in gpend:
                    f()
                gpend = []
                p.wait("tensor", "v", p.cnt["v"])
                p.wait("tensor", "s", p.cnt["s"])
                tq2 = None
                for qi in range(4):
                    qt = 4 * qtg + qi
                    tq2 = tr(psum[:1, 3072 + 128 * qi:
                                  3072 + 128 * (qi + 1)],
                             e2plus[:, qt:qt + 1],
                             sem="mm" if qi == 3 else None)
                p.wait("vector", "mm", tq2)
                ve2 = vcp(smallD[:1, 0:512], psum[:1, 3072:3584], sem="v")
                p.wait("tensor", "v", ve2)
                e2l = None
                for qi in range(4):
                    e2l = mm(psum[:, 1024 + 128 * qi:1024 + 128 * (qi + 1)],
                             onesr, smallD[:1, 128 * qi:128 * (qi + 1)],
                             True, True, sem="mm" if qi == 3 else None)
                p.wait("vector", "mm", e2l)
                ve2b = vcpc(pl(E2B, 512), psum[:, 1024:1536])
                p.wait("tensor", "v", ve2b)
                pvstop = None
                pends = []
                for kc in range(TT):
                    nB = max(0, min(kc - 4 * qtg, 4))
                    pieces = []
                    if nB:
                        pieces.append(("B", 0, nB))
                    if nB < 4:
                        pieces.append(("A", nB, 4))
                    for kind, qi0, qi1 in pieces:
                        par = parc[0] % 2
                        parc[0] += 1
                        w = 128 * (qi1 - qi0)
                        ko = KA if kind == "A" else KB
                        smms = {}
                        for hh in range(2):
                            qrow = 64 * hh
                            SCh = psum[:, scb1[hh][par]:scb1[hh][par] + w]
                            p.wait("tensor", "s", sexp.get((hh, par)))
                            smms[hh] = mm(
                                SCh,
                                pool[qrow:qrow + 64,
                                     ko + 128 * kc:ko + 128 * (kc + 1)],
                                pool[qrow:qrow + 64,
                                     Q2 + 512 * qtg + 128 * qi0:
                                     Q2 + 512 * qtg + 128 * qi1],
                                True, True, sem="mm")
                        s1s = {}
                        for hh in range(2):
                            SCh = psum[:, scb1[hh][par]:scb1[hh][par] + w]
                            vfix = None
                            if kind == "B":
                                p.wait("vector", "mm", smms[hh])
                                for qi in range(qi0, qi1):
                                    off = 128 * (qi - qi0)
                                    vfix = tt2c(
                                        SCh[:, off:off + 128],
                                        SCh[:, off:off + 128],
                                        pl(E2B + 128 * qi, 128), ALU.mult)
                            elif 4 * qtg <= kc < 4 * qtg + 4:
                                qi_d = kc - 4 * qtg
                                off = 128 * (qi_d - qi0)
                                p.wait("vector", "mm", smms[hh])
                                vfix = tt2c(SCh[:, off:off + 128],
                                            SCh[:, off:off + 128], dcorrT,
                                            ALU.mult)
                            if vfix is not None:
                                p.wait("scalar", "v", vfix)
                            p.wait("scalar", "mm", smms[hh])
                            p.wait("scalar", "mm", pvdone.get((hh, par)))
                            s1s[hh] = act(pl(exb1[hh][par], w), SCh, AF.Exp,
                                          bias=nbnd, sem="s")
                            sexp[(hh, par)] = s1s[hh]

                        def pend_pv(kc=kc, par=par, s1s=dict(s1s), qi0=qi0,
                                    qi1=qi1):
                            nonlocal pvstop
                            p.wait("tensor", "s", s1s[1])
                            for qi in range(qi0, qi1):
                                if kc == 0:
                                    p.wait("tensor", "ev", gevd.get(qi))
                                pvl = None
                                for hh in range(2):
                                    pvb = psum[:, 1024 + 512 * qi + 128 * hh:
                                               1024 + 512 * qi +
                                               128 * hh + 65]
                                    pvl = mm(
                                        pvb,
                                        pool[:, exb1[hh][par] +
                                             128 * (qi - qi0):
                                             exb1[hh][par] +
                                             128 * (qi - qi0) + 128],
                                        pl(V2 + 129 * kc + 64 * hh, 65),
                                        kc == 0 and hh == 0,
                                        kc == TT - 1 and hh == 1,
                                        sem="mm" if kc == TT - 1
                                        and hh == 1 else
                                        ("mm" if qi == qi1 - 1 and hh == 1
                                         else None))
                                if kc == TT - 1:
                                    pvstop = pvl
                            pvdone[(0, par)] = pvl
                            pvdone[(1, par)] = pvl
                        pends.append(pend_pv)
                        if len(pends) >= 2:
                            pends.pop(0)()
                for f in pends:
                    f()
                for qi in range(4):
                    def evac(qi=qi, qt=4 * qtg + qi, last=pvstop):
                        p.wait("vector", "mm", last)
                        for hh in range(2):
                            pvb = psum[:, 1024 + 512 * qi + 128 * hh:
                                       1024 + 512 * qi + 128 * hh + 65]
                            zc = pvb[:, 64:65] if hh == 0 else pvb[:, 0:1]
                            vo = 0 if hh == 0 else 1
                            rc_ = 8 + qi if hh == 0 else 40 + qi
                            recipc(smallA[:, rc_:rc_ + 1], zc)
                            gevd[qi] = ts(
                                pl(AT2 + 128 * qt + 64 * hh, 64),
                                pvb[:, vo:vo + 64],
                                smallA[:, rc_:rc_ + 1], None, ALU.mult,
                                sem="ev")
                    gpend.append(evac)
            for f in gpend:
                f()
            for qi in range(4):
                p.wait("tensor", "ev", gevd.get(qi))
            p.wait("sync", "mm", p.cnt["mm"])
            lwo = p.dma("sync", pl(WQS, 1024),
                        dr["l1_wo"][128 * pr:128 * (pr + 1), :], "dW")
            ATT = Q2
            tpend = []
            tev_last = None
            for t8 in range((TT + 7) // 8):
                nblk = min(8, TT - 8 * t8)
                base = 2048 + 1024 * (t8 % 2)
                tl = None
                for i in range(nblk):
                    kx = 8 * t8 + i
                    tl = tr(psum[:, base + 128 * i:base + 128 * (i + 1)],
                            pl(AT2 + 128 * kx, 128),
                            sem="mm" if i == nblk - 1 else None)

                def evac(t8=t8, nblk=nblk, base=base, tl=tl):
                    p.wait("vector", "mm", tl)
                    return vcp(pl(ATT + 1024 * t8, 128 * nblk),
                               psum[:, base:base + 128 * nblk], sem="ev")
                tpend.append(evac)
                if len(tpend) >= 2:
                    tev_last = tpend.pop(0)()
            for f in tpend:
                tev_last = f()
            p.wait("tensor", "ev", tev_last)
            p.wait("tensor", "dW", lwo)
            evd = {}
            pend = []
            for i, (t, ncol) in enumerate(
                    (t, ncol) for t in range(TT) for ncol in range(2)):
                b = i % 2
                po = psum[:, 512 * b:512 * (b + 1)]
                p.wait("tensor", "ev", evd.get(b))
                last = mm(po, pl(ATT + 128 * t, 128),
                          pl(WQS + 512 * ncol, 512), True, True, sem="mm")
                free_wq = last

                def evac(t=t, ncol=ncol, po=po, last=last, b=b):
                    p.wait("vector", "mm", last)
                    dst = acc[:, D * t + 512 * ncol:D * t + 512 * (ncol + 1)]
                    evd[b] = tt2(dst, po, dst, ALU.add, sem="ev")
                pend.append(evac)
                if len(pend) >= 2:
                    pend.pop(0)()
            for f in pend:
                f()
            p.wait("tensor", "ev", evd.get(0))
            p.wait("tensor", "ev", evd.get(1))
        p.barrier()
        if cfg.debug and seq == 0:
            dd = p.dma("sync", dbg["d_acc_at1"], acc, "dD")
            p.wait("sync", "dD", dd)

    def phase_amax_cc():
        t1 = tr(psum[:1, 3072:3200], gmax, sem="mm")
        p.wait("vector", "mm", t1)
        v1 = vcp(smallC[:1, 0:128], psum[:1, 3072:3200], sem="v")
        p.wait("vector", "v", v1)
        rmaxc(smallA[:1, 10:11], smallC[:1, 0:128])
        v2 = memset(amax1, 0.0, sem="v")
        p.wait("scalar", "v", v2)
        s1 = act(amax1[:1, 0:1], smallA[:1, 10:11], AF.Sqrt, sem="s")
        p.wait("gpsimd", "s", s1)
        p.wait("gpsimd", "v", v2)
        p.wait("sync", "s", s1)
        p.wait("sync", "v", v2)
        if cfg.use_cc:
            g1_ = p.dma("gpsimd", cc_in[:, :], amax1, "dG")
            p.wait("gpsimd", "dG", g1_)
            cc = p.emit("gpsimd", lambda e: e.collective_compute(
                "AllReduce", ALU.max,
                replica_groups=[list(range(cfg.n_cores))],
                ins=[cc_in.ap().opt()], outs=[cc_out.ap().opt()]), "dCC", 1)
            p.wait("gpsimd", "dCC", cc)
            g2_ = p.dma("gpsimd", amax1, cc_out[:, :], "dG")
            p.wait("vector", "dG", g2_)
        if cfg.debug:
            dd = p.dma("sync", dbg["d_amax"], amax1, "dD")
            p.wait("sync", "dD", dd)

    def emit_scale_from_amax(a_ap, s_col, r_col, tag):
        i0 = {"x": 16, "w": 24}[tag]
        ts(smallA[:1, i0:i0 + 1], a_ap, 1e-12, None, ALU.max)
        p.barrier()
        recip(smallA[:1, i0 + 1:i0 + 2], smallA[:1, i0:i0 + 1])
        p.barrier()
        ts(smallA[:1, i0 + 2:i0 + 3], smallA[:1, i0 + 1:i0 + 2], 448.0, None,
           ALU.mult)
        p.barrier()
        ts(smallA[:1, i0 + 3:i0 + 4], smallA[:1, i0 + 2:i0 + 3], 448.0, None,
           ALU.min)
        p.barrier()
        recip(smallA[:1, i0 + 4:i0 + 5], smallA[:1, i0 + 3:i0 + 4])
        ts(smallA[:1, i0 + 5:i0 + 6], smallA[:1, i0 + 3:i0 + 4], 0.5, None,
           ALU.mult)
        p.barrier()
        m1 = mm(psum[:, 3584:3585], onesr, smallA[:1, i0 + 5:i0 + 6],
                True, True)
        m2 = mm(psum[:, 3586:3587], onesr, smallA[:1, i0 + 4:i0 + 5],
                True, True, sem="mm")
        p.wait("vector", "mm", m2)
        vcp(s_col, psum[:, 3584:3585])
        vcp(r_col, psum[:, 3586:3587])
        p.barrier()

    BIASB = 0
    WT = 1024

    def phase_wquant():
        lbr = p.dma("sync", pool[:1, WT:WT + D], dr["l1_bias"], "dW")
        p.wait("tensor", "dW", lbr)
        mm(psum[:, 0:512], onesr, pool[:1, WT:WT + 512], True, True)
        m2 = mm(psum[:, 512:1024], onesr, pool[:1, WT + 512:WT + D], True, True,
                sem="mm")
        p.wait("vector", "mm", m2)
        ev = vcp(pl(BIASB, D), psum[:, 0:1024], sem="ev")
        p.wait("sync", "ev", ev)
        memset(smallA[:, 12:13], 0.0)
        for c in range(NCH):
            lw = p.dma("sync", pl(WT, D), dr["l1_wt"][128 * c:128 * (c + 1), :],
                       "dW")
            p.wait("vector", "dW", lw)
            rmaxc(smallA[:, 13:14], pl(WT, D), absval=True)
            va = tt2c(smallA[:, 12:13], smallA[:, 12:13], smallA[:, 13:14],
                      ALU.max)
            p.wait("sync", "v", va)
        p.barrier()
        t1 = tr(psum[:1, 3072:3200], smallA[:, 12:13], sem="mm")
        p.wait("vector", "mm", t1)
        v1 = vcp(smallC[:1, 0:128], psum[:1, 3072:3200], sem="v")
        p.wait("vector", "v", v1)
        rmaxc(smallA[:1, 30:31], smallC[:1, 0:128])
        p.barrier()
        emit_scale_from_amax(smallA[:1, 30:31], smallB[:, 16:17],
                             smallB[:, 17:18], "w")
        for c in range(NCH):
            lw = p.dma("sync", pl(WT, D), dr["l1_wt"][128 * c:128 * (c + 1), :],
                       "dW")
            p.wait("vector", "dW", lw)
            tsc(pl(WT, D), pl(WT, D), smallB[:, 16:17], -224.0, ALU.mult,
                ALU.max)
            vq = ts(w8[:, D * c:D * (c + 1)], pl(WT, D), 224.0, None, ALU.min,
                    sem="v")
            p.wait("sync", "v", vq)
            p.wait("vector", "v", vq)
        p.barrier()

    def phase_l1ffn(seq):
        qfree = {0: None, 1: None}
        fevd = {}
        fpend = []
        for t in range(TT):
            bb = t % 2
            po0 = psum[:, 1024 * bb:1024 * bb + 512]
            po1 = psum[:, 1024 * bb + 512:1024 * bb + 1024]
            p.wait("tensor", "ev", fevd.get(bb))
            last = None
            for c in range(NCH):
                slot = c % 2
                p.wait("vector", "mm", qfree[slot])
                tsc(smallC[:, 0:128], xnc(c)[:, 128 * t:128 * (t + 1)],
                    smallB[:, 18:19], -224.0, ALU.mult, ALU.max)
                vq = tsc(q8t[:, 128 * slot:128 * (slot + 1)], smallC[:, 0:128],
                         224.0, None, ALU.min)
                p.wait("tensor", "v", vq)
                mm(po0, q8t[:, 128 * slot:128 * (slot + 1)],
                   w8[:, D * c:D * c + 512], c == 0, c == NCH - 1)
                last = mm(po1, q8t[:, 128 * slot:128 * (slot + 1)],
                          w8[:, D * c + 512:D * (c + 1)], c == 0, c == NCH - 1,
                          sem="mm")
                qfree[slot] = last

            def evac(t=t, po0=po0, po1=po1, last=last, bb=bb):
                p.wait("vector", "mm", last)
                for ncol in range(2):
                    po = po0 if ncol == 0 else po1
                    sttc(bft, po, smallB[:, 19:20],
                         pl(BIASB + 512 * ncol, 512), ALU.mult, ALU.add)
                    dst = acc[:, D * t + 512 * ncol:D * t + 512 * (ncol + 1)]
                    ea = tt2(dst, dst, bft, ALU.add, sem="ev")
                    p.wait("vector", "ev", ea)
                    fevd[bb] = ea
            fpend.append(evac)
            if len(fpend) >= 2:
                fpend.pop(0)()
        for f in fpend:
            f()
        p.barrier()

    phase_ln(0, False, dbg.get("d_xn0"))
    phase_l0ffn(0)
    phase_l0attn(0)
    phase_ln(1, True, dbg.get("d_xn1"))
    s1 = p.dma("sync", spill_xn, xn, "dS")
    phase_l1attn(0)
    s2 = p.dma("sync", spill_acc, acc, "dS")
    p.wait("sync", "dS", s2)
    l3 = p.dma("sync", acc, x0[1].rearrange("(t p) d -> p t d", p=128), "dX")
    p.wait("vector", "dX", l3)
    p.barrier()
    phase_ln(0, False)
    phase_l0ffn(1)
    phase_l0attn(1)
    phase_ln(1, True)
    phase_amax_cc()
    phase_l1attn(1)
    phase_wquant()
    p.barrier()
    emit_scale_from_amax(amax1[:1, 0:1], smallB[:, 18:19], smallB[:, 20:21], "x")
    tt2c(smallB[:, 19:20], smallB[:, 20:21], smallB[:, 17:18], ALU.mult)
    tsc(smallB[:, 19:20], smallB[:, 19:20], 4.0, None, ALU.mult)
    p.barrier()
    phase_l1ffn(1)
    st_b = p.dma("sync", out[1].rearrange("(t p) d -> p t d", p=128), acc, "dS")
    p.wait("sync", "dS", st_b)
    p.dma("sync", xn, spill_xn, "dX")
    r2 = p.dma("sync", acc, spill_acc, "dX")
    p.wait("vector", "dX", r2)
    p.wait("tensor", "dX", r2)
    p.barrier()
    phase_l1ffn(0)
    st_a = p.dma("sync", out[0].rearrange("(t p) d -> p t d", p=128), acc, "dS")
    p.wait("sync", "dS", st_a)
    p.barrier()
    p.build()
    return nc, p


def kernel(**inputs):
    cfg = Cfg()
    cons = host_constants(cfg)
    wts = host_weights(cfg, inputs)
    x = np.ascontiguousarray(np.asarray(inputs["L0_attn_x"], dtype=np.float32))
    nc, p = build_program(cfg)
    base = dict(cons)
    base.update(wts)
    in_maps = []
    for k in range(cfg.n_cores):
        m = dict(base)
        m["x0"] = np.ascontiguousarray(x[2 * k:2 * k + 2])
        in_maps.append(m)
    res = run_bass_kernel_spmd(nc, in_maps, core_ids=list(range(cfg.n_cores)))
    return np.concatenate([r["out"] for r in res.results], axis=0)


# version 19 (final)
# speedup vs baseline: 1.3025x; 1.0033x over previous
import math
import numpy as np

import concourse.bass as bass
import concourse.mybir as mybir
from concourse.bass_utils import run_bass_kernel_spmd

F32 = mybir.dt.float32
BF16 = mybir.dt.bfloat16
FP8 = mybir.dt.float8e4
AF = mybir.ActivationFunctionType
ALU = mybir.AluOpType
AX = mybir.AxisListType

D = 1024
NCH = 8
DK = 64
DH = 128
DL = 256
E = 16
BIG = 1.0e30
EPS_LN = 1e-5
C_DECAY = 0.01 * 0.01
SC_L0 = 1.0 / 8.0
SC_FFN = 1.0 / math.sqrt(128.0)
SC_L1 = 1.0 / (math.sqrt(1024.0) + 1e-8)
POOL = 15376


class Cfg:
    def __init__(self, T=2048, H=16, n_seq=2, n_cores=8, debug=False):
        self.T = T
        self.TT = T // 128
        self.H = H
        self.n_seq = n_seq
        self.n_cores = n_cores
        self.debug = debug
        self.use_cc = n_cores > 1
        self.detect = False
        assert T % 512 == 0 and H % 4 == 0


def rope_perm(H):
    perm = []
    for g in range(H // 4):
        for par in (0, 1):
            for h in range(4 * g, 4 * g + 4):
                for f in range(32):
                    perm.append(64 * h + 2 * f + par)
    return np.array(perm)


def host_constants(cfg):
    T = cfg.T
    c = {}
    c["ident"] = np.eye(128, dtype=np.float32)
    i = np.arange(128)
    c["causmask"] = np.where(i[None, :] > i[:, None], -BIG, 0.0).astype(np.float32)
    c["causmT"] = np.ascontiguousarray(c["causmask"].T)
    c["dcorr"] = np.exp(-2.0 * C_DECAY *
                        np.maximum(i[None, :] - i[:, None], 0)).astype(np.float32)
    c["dcorrT"] = np.ascontiguousarray(c["dcorr"].T)
    powers = np.arange(0, DK, 2, dtype=np.float32) / np.float32(DK)
    freqs = (np.float32(1.0) / (np.float32(10000.0) ** powers)).astype(np.float32)
    t = np.arange(T, dtype=np.float32)
    ang = np.outer(t, freqs).astype(np.float32)
    c["ctab"] = np.ascontiguousarray(np.tile(np.cos(ang).astype(np.float32).T,
                                             (4, 1)))
    c["stab"] = np.ascontiguousarray(np.tile(np.sin(ang).astype(np.float32).T,
                                             (4, 1)))
    pos = np.arange(T, dtype=np.float64).reshape(cfg.TT, 128).T
    c["eplus"] = np.ascontiguousarray(np.exp(C_DECAY * pos).astype(np.float32))
    c["eminus"] = np.ascontiguousarray(np.exp(-C_DECAY * pos).astype(np.float32))
    c["escA"] = np.ascontiguousarray((SC_L1 * np.exp(-C_DECAY * pos))
                                     .astype(np.float32))
    c["escB"] = np.ascontiguousarray((SC_L1 * np.exp(C_DECAY * pos))
                                     .astype(np.float32))
    c["e2plus"] = np.ascontiguousarray(np.exp(2.0 * C_DECAY * pos)
                                       .astype(np.float32))
    return c


def host_weights(cfg, w):
    H = cfg.H
    perm = rope_perm(H)
    o = {}
    for nm in ("wq", "wk"):
        a = np.ascontiguousarray(np.asarray(w[f"L0_attn_{nm}"]).T[:, perm])
        o[f"l0_{nm}g"] = np.ascontiguousarray(
            a.reshape(D, H // 4, 256).transpose(1, 0, 2))
    a = np.ascontiguousarray(np.asarray(w["L0_attn_wv"]).T)
    o["l0_wvp"] = np.ascontiguousarray(a.reshape(D, H // 2, 128).transpose(1, 0, 2))
    a = np.ascontiguousarray(np.asarray(w["L0_attn_wo"]).T)
    o["l0_wop"] = np.ascontiguousarray(a.reshape(H // 2, 128, D))
    o["l0_snT"] = np.ascontiguousarray(np.asarray(w["L0_ffn_sim_matrix"]).T)
    o["l0_gates"] = np.ascontiguousarray(np.asarray(w["L0_ffn_gates"])
                                         .reshape(1, E))
    o["l0_qkvp"] = np.ascontiguousarray(np.concatenate(
        [np.asarray(w["L0_ffn_q_proj"]), np.asarray(w["L0_ffn_k_proj"]),
         np.asarray(w["L0_ffn_v_proj"])], axis=2))
    o["l0_op"] = np.ascontiguousarray(w["L0_ffn_o_proj"])
    a = np.ascontiguousarray(w["L1_attn_Wq"])
    o["l1_wqg"] = np.ascontiguousarray(a.reshape(D, H // 2, 128).transpose(1, 0, 2))
    o["l1_wk"] = np.ascontiguousarray(w["L1_attn_Wk"])
    o["l1_wv"] = np.ascontiguousarray(w["L1_attn_Wv"])
    o["l1_wdown"] = np.ascontiguousarray(w["L1_attn_W_down_kv"])
    for nm, key in (("wupk", "L1_attn_W_up_k"), ("wupv", "L1_attn_W_up_v")):
        a = np.ascontiguousarray(w[key])
        o[f"l1_{nm}g"] = np.ascontiguousarray(
            a.reshape(DL, H // 2, 128).transpose(1, 0, 2))
    o["l1_wo"] = np.ascontiguousarray(w["L1_attn_Wo"])
    o["l1_wt"] = np.ascontiguousarray(np.asarray(w["L1_ffn_weight"]).T)
    o["l1_bias"] = np.ascontiguousarray(np.asarray(w["L1_ffn_bias"])
                                        .reshape(1, D))
    return o


class P:
    ENGS = ("sync", "tensor", "vector", "scalar", "gpsimd")

    def __init__(self, nc):
        self.nc = nc
        self.q = {e: [] for e in P.ENGS}
        self.sems = {}
        self.cnt = {}
        self.n_instr = 0

    def sem(self, name):
        if name not in self.sems:
            self.sems[name] = self.nc.alloc_semaphore(name)
            self.cnt[name] = 0
        return self.sems[name]

    def on(self, eng, fn):
        self.q[eng].append(fn)
        self.n_instr += 1

    def wait(self, eng, name, val):
        if val is None or val <= 0:
            return
        s = self.sem(name)
        self.on(eng, lambda e, s=s, v=val: e.wait_ge(s, v))

    def emit(self, eng, fn, sem=None, n=1):
        if sem is None:
            self.on(eng, fn)
            return None
        s = self.sem(sem)
        self.cnt[sem] += n
        tot = self.cnt[sem]
        self.on(eng, lambda e, fn=fn, s=s, n=n: fn(e).then_inc(s, n))
        return tot

    def chain(self, eng, fn, sem="v"):
        t = self.emit(eng, fn, sem)
        self.wait(eng, sem, t)
        return t

    def dma(self, eng, out, in_, sem="dW"):
        return self.emit(eng, lambda e, o=out, i=in_: e.dma_start(out=o, in_=i),
                         sem, 16)

    def barrier(self, tag=""):
        s = self.sem("bar")
        self.cnt["bar"] += 5
        tgt = self.cnt["bar"]
        for eng in P.ENGS:
            self.on(eng, lambda e, s=s: e.drain().then_inc(s, 1))
            self.on(eng, lambda e, s=s, t=tgt: e.wait_ge(s, t))

    def build(self):
        with self.nc.Block() as blk:
            for eng in P.ENGS:
                def body(e, eng=eng):
                    for fn in self.q[eng]:
                        fn(e)
                getattr(blk, eng)(body)


def build_program(cfg):
    T, TT, H = cfg.T, cfg.TT, cfg.H
    NG, NP = H // 4, H // 2
    SPAN = 512
    nc = bass.Bass("TRN2", target_bir_lowering=False,
                   detect_race_conditions=cfg.detect)

    dr = {}
    def din(name, shape):
        dr[name] = nc.dram_tensor(name, list(shape), F32, kind="ExternalInput").ap()
        return dr[name]

    x0 = din("x0", (cfg.n_seq, T, D))
    out = nc.dram_tensor("out", [cfg.n_seq, T, D], F32, kind="ExternalOutput").ap()
    for nm, shape in (
        ("ident", (128, 128)), ("causmask", (128, 128)), ("dcorr", (128, 128)),
        ("dcorrT", (128, 128)), ("e2plus", (128, TT)),
        ("causmT", (128, 128)),
        ("ctab", (128, T)), ("stab", (128, T)),
        ("eplus", (128, TT)), ("eminus", (128, TT)),
        ("escA", (128, TT)), ("escB", (128, TT)),
        ("l0_wqg", (NG, D, 256)), ("l0_wkg", (NG, D, 256)),
        ("l0_wvp", (NP, D, 128)), ("l0_wop", (NP, 128, D)),
        ("l0_snT", (16, D)), ("l0_gates", (1, E)),
        ("l0_qkvp", (E, D, 3 * DH)), ("l0_op", (E, DH, D)),
        ("l1_wqg", (NP, D, 128)), ("l1_wk", (D, D)), ("l1_wv", (D, D)),
        ("l1_wdown", (D, DL)), ("l1_wupkg", (NP, DL, 128)),
        ("l1_wupvg", (NP, DL, 128)), ("l1_wo", (D, D)),
        ("l1_wt", (D, D)), ("l1_bias", (1, D)),
    ):
        din(nm, shape)
    spill_xn = nc.dram_tensor("spill_xn", [128, NCH * T], F32).ap()
    spill_lat = nc.dram_tensor("spill_lat", [128, 2 * T], F32).ap()
    spill_acc = nc.dram_tensor("spill_acc", [128, TT * D], F32).ap()
    cc_in = nc.dram_tensor("cc_in", [1, 8], F32)
    cc_out = nc.dram_tensor("cc_out", [1, 8], F32)

    dbg = {}
    if cfg.debug:
        for nm, shape in (
            ("d_xn0", (128, NCH * T)), ("d_w16", (128, E * TT)),
            ("d_accq", (128, TT * DH)), ("d_o", (128, TT * DH)),
            ("d_acc_ffn", (128, TT * D)), ("d_acc_l0", (128, TT * D)),
            ("d_latT", (128, 2 * T)), ("d_amax", (1, 8)),
            ("d_acc_at1", (128, TT * D)), ("d_xn1", (128, NCH * T)),
            ("d_pool0", (128, 513 * TT + 130)),
        ):
            dbg[nm] = nc.dram_tensor(nm, list(shape), F32,
                                     kind="ExternalOutput").ap()

    def sbt(name, shape, dt=F32):
        return nc.alloc_sbuf_tensor(name, list(shape), dt).ap()

    xn = sbt("xn", (128, NCH * T))
    acc = sbt("acc", (128, TT * D))
    pool = sbt("pool", (128, POOL))
    ident = sbt("identS", (128, 128))
    causm = sbt("causmS", (128, 128))
    dcorr = sbt("dcorrS", (128, 128))
    dcorrT = sbt("dcorrTS", (128, 128))
    causmT = sbt("causmTS", (128, 128))
    e2plus = sbt("e2plusS", (128, TT))
    eplus = sbt("eplusS", (128, TT))
    eminus = sbt("eminusS", (128, TT))
    escA = sbt("escAS", (128, TT))
    escB = sbt("escBS", (128, TT))
    w16 = sbt("w16S", (128, E * TT))
    rninv = sbt("rninvS", (128, TT))
    gmax = sbt("gmaxS", (128, 1))
    smallA = sbt("smallAS", (128, 64))
    smallB = sbt("smallBS", (128, 64))
    smallC = sbt("smallCS", (128, 512))
    onesr = sbt("onesrS", (1, 128))
    sg16 = sbt("sg16S", (128, 16))
    neg16 = sbt("neg16S", (128, 16))
    snN = sbt("snNS", (128, 16 * NCH))
    g1 = sbt("g1S", (1, 16))
    amax1 = sbt("amax1S", (1, 8))
    smallD = sbt("smallDS", (128, 512))
    onec = sbt("onecS", (128, 1))
    w8 = sbt("w8S", (128, NCH * D), FP8)
    bft = sbt("bftS", (128, 512), BF16)
    q8t = sbt("q8tS", (128, 256), FP8)
    psum = nc.alloc_psum_tensor("psA", [128, 4096], F32).ap()

    p = P(nc)

    def pl(off, n):
        return pool[:, off:off + n]

    def mm(out_ap, lhsT, rhs, start, stop, sem=None):
        return p.emit("tensor",
                      lambda e: e.matmul(out_ap, lhsT, rhs, start=start, stop=stop),
                      sem)

    def tr(out_ap, in_ap, idn=None, sem=None):
        i_ = ident if idn is None else idn
        return p.emit("tensor", lambda e: e.transpose(out_ap, in_ap, i_), sem)

    def vcp(out_ap, in_ap, sem=None, eng="vector"):
        return p.emit(eng, lambda e: e.tensor_copy(out=out_ap, in_=in_ap), sem)

    def ts(out_ap, in0, s1, s2, op0, op1=None, sem=None, eng="vector"):
        return p.emit(eng, lambda e: e.tensor_scalar(
            out=out_ap, in0=in0, scalar1=s1, scalar2=s2, op0=op0,
            op1=(ALU.bypass if op1 is None else op1)), sem)

    def tsc(out_ap, in0, s1, s2, op0, op1=None, eng="vector"):
        return p.chain(eng, lambda e: e.tensor_scalar(
            out=out_ap, in0=in0, scalar1=s1, scalar2=s2, op0=op0,
            op1=(ALU.bypass if op1 is None else op1)))

    def tt2c(out_ap, in0, in1, op, eng="vector"):
        return p.chain(eng, lambda e: e.tensor_tensor(
            out=out_ap, in0=in0, in1=in1, op=op))

    def rsumc(out_ap, in_ap):
        return p.chain("vector", lambda e: e.reduce_sum(out_ap, in_ap,
                                                        axis=AX.X))

    def rmaxc(out_ap, in_ap, negate=False, absval=False):
        kw = {}
        if negate:
            kw["negate"] = True
        if absval:
            kw["apply_absolute_value"] = True
        return p.chain("vector", lambda e: e.reduce_max(out_ap, in_ap,
                                                        axis=AX.X, **kw))

    def recipc(out_ap, in_ap):
        return p.chain("vector", lambda e: e.reciprocal(out_ap, in_ap))

    def sttc(out_ap, in0, scalar, in1, op0, op1):
        return p.chain("vector", lambda e: e.scalar_tensor_tensor(
            out=out_ap, in0=in0, scalar=scalar, in1=in1, op0=op0, op1=op1))

    def vcpc(out_ap, in_ap, eng="vector"):
        return p.chain(eng, lambda e: e.tensor_copy(out=out_ap, in_=in_ap))

    def tt2(out_ap, in0, in1, op, sem=None, eng="vector"):
        return p.emit(eng, lambda e: e.tensor_tensor(
            out=out_ap, in0=in0, in1=in1, op=op), sem)

    def stt(out_ap, in0, scalar, in1, op0, op1, sem=None):
        return p.emit("vector", lambda e: e.scalar_tensor_tensor(
            out=out_ap, in0=in0, scalar=scalar, in1=in1, op0=op0, op1=op1), sem)

    def act(out_ap, in_ap, func, bias=0.0, scale=1.0, accum=None, sem=None):
        kw = {"accum_out": accum} if accum is not None else {}
        return p.emit("scalar", lambda e: e.activation(
            out=out_ap, in_=in_ap, func=func, bias=bias, scale=scale, **kw), sem)

    def rmax(out_ap, in_ap, negate=False, absval=False, sem=None, ):
        kw = {}
        if negate:
            kw["negate"] = True
        if absval:
            kw["apply_absolute_value"] = True
        return p.emit("vector", lambda e: e.reduce_max(out_ap, in_ap, axis=AX.X,
                                                       **kw), sem)

    def rsum(out_ap, in_ap, sem=None):
        return p.emit("vector", lambda e: e.reduce_sum(out_ap, in_ap, axis=AX.X),
                      sem)

    def recip(out_ap, in_ap, sem=None):
        return p.emit("vector", lambda e: e.reciprocal(out_ap, in_ap), sem)

    def sel(out_ap, mask, on_t, on_f, sem=None):
        return p.emit("vector", lambda e: e.select(
            out=out_ap, mask=mask, on_true=on_t, on_false=on_f), sem)

    def memset(ap_, v, eng="vector", sem=None):
        return p.emit(eng, lambda e: e.memset(ap_, v), sem)

    def xnc(c):
        return xn[:, T * c:T * (c + 1)]

    PVB = psum[:, 3584:3648]
    PVF = psum[:, 3584:3712]

    def pipe2(jobs):
        prev = None
        for f, b in jobs:
            f()
            if prev is not None:
                prev()
            prev = b
        if prev is not None:
            prev()

    def pair_bound(srcs, sq_off, scale_negs, outcols):
        sqO = (pl(sq_off, 512), pl(sq_off + 512, 512))
        pbO = (psum[:1, 3072:3584], psum[:1, 3584:4096])
        memset(smallB[:1, 30:30 + len(srcs)], 0.0, sem="v")
        p.wait("scalar", "v", p.cnt["v"])
        jobs = [(j2, so, ch) for j2, so in enumerate(srcs)
                for ch in range(T // 512)]
        nmmd = {0: None, 1: None}
        rdone = {0: None, 1: None}
        pend = None
        for i, (j2, so, ch) in enumerate(jobs):
            par = i % 2
            p.wait("scalar", "mm", nmmd[par])
            sb = act(sqO[par], pl(so + 512 * ch, 512), AF.Square, sem="s")
            if pend is not None:
                pend()

            def pend(j2=j2, par=par, sb=sb):
                p.wait("tensor", "s", sb)
                p.wait("tensor", "v", rdone[par])
                nmm = mm(pbO[par], onec, sqO[par], True, True, sem="mm")
                nmmd[par] = nmm
                p.wait("vector", "mm", nmm)
                rmaxc(smallA[:1, 16 + par:17 + par], pbO[par])
                rdone[par] = tt2c(smallB[:1, 30 + j2:31 + j2],
                                  smallB[:1, 30 + j2:31 + j2],
                                  smallA[:1, 16 + par:17 + par], ALU.max)
        pend()
        p.wait("scalar", "v", p.cnt["v"])
        s4 = None
        for j2 in range(len(srcs)):
            s4 = act(smallA[:1, 17 + j2:18 + j2], smallB[:1, 30 + j2:31 + j2],
                     AF.Sqrt, sem="s")
        p.wait("vector", "s", s4)
        for scale_neg, outc in zip(scale_negs, outcols):
            if len(srcs) >= 2:
                tt2c(smallA[:1, 20:21], smallA[:1, 17:18], smallA[:1, 18:19],
                     ALU.mult)
            else:
                tt2c(smallA[:1, 20:21], smallA[:1, 17:18], smallA[:1, 19:20],
                     ALU.mult)
            v5 = tsc(smallA[:1, 21:22], smallA[:1, 20:21], scale_neg, None,
                     ALU.mult)
            p.wait("tensor", "v", v5)
            bb = mm(psum[:, 3584:3585], onesr, smallA[:1, 21:22], True, True,
                    sem="mm")
            p.wait("vector", "mm", bb)
            vb = vcp(smallB[:, outc:outc + 1], psum[:, 3584:3585], sem="v")
            p.wait("scalar", "v", vb)
            p.wait("tensor", "v", vb)

    for nm, dst in (("ident", ident), ("causmask", causm), ("dcorr", dcorr),
                    ("dcorrT", dcorrT), ("e2plus", e2plus),
                    ("causmT", causmT),
                    ("eplus", eplus), ("eminus", eminus),
                    ("escA", escA), ("escB", escB), ("l0_gates", g1)):
        p.dma("sync", dst, dr[nm], "dC")
    snT = pool[:16, 0:D]
    p.dma("sync", snT, dr["l0_snT"], "dC")
    vC = p.dma("sync", acc, x0[0].rearrange("(t p) d -> p t d", p=128), "dC")
    p.wait("vector", "dC", vC)
    memset(neg16, -BIG)
    memset(onesr, 1.0)
    memset(onec, 1.0)
    memset(gmax, 0.0)
    p.wait("scalar", "dC", vC)
    act(g1, g1, AF.Sigmoid)
    act(psum[:16, 512:512 + D], snT, AF.Square, accum=smallA[:16, 0:1])
    p.barrier()
    mm(psum[:, 0:16], onesr, g1, True, True)
    act(smallA[:16, 1:2], smallA[:16, 0:1], AF.Sqrt)
    p.barrier()
    vcp(sg16, psum[:, 0:16])
    ts(smallA[:16, 2:3], smallA[:16, 1:2], 1e-12, None, ALU.max)
    p.barrier()
    recip(smallA[:16, 3:4], smallA[:16, 2:3])
    p.barrier()
    ts(snT, snT, smallA[:16, 3:4], None, ALU.mult)
    p.barrier()
    for c in range(NCH):
        tr(psum[:, 1024 + 16 * c:1024 + 16 * (c + 1)],
           snT[:, 128 * c:128 * (c + 1)], ident[:16, :16])
    p.barrier()
    vcp(snN, psum[:, 1024:1024 + 16 * NCH])
    p.barrier()

    def phase_ln(layer, track_amax, dbg_tap=None):
        SQ = psum[:, 2048:2048 + D]
        TR = psum[:, 3072:3072 + 1024]
        xt = pl(POOL - 1024, 1024)
        for t in range(TT):
            a = acc[:, D * t:D * (t + 1)]
            rs, nm_, ssq = smallA[:, 0:1], smallA[:, 1:2], smallA[:, 2:3]
            rstd = smallA[:, 5:6]
            rsumc(rs, a)
            v2 = tsc(nm_, rs, -1.0 / D, None, ALU.mult)
            p.wait("scalar", "v", v2)
            s1 = act(SQ, a, AF.Square, bias=nm_, accum=ssq, sem="s")
            p.wait("vector", "s", s1)
            v3 = tsc(smallA[:, 3:4], ssq, 1.0 / D, EPS_LN, ALU.mult, ALU.add)
            p.wait("scalar", "v", v3)
            s2 = act(smallA[:, 4:5], smallA[:, 3:4], AF.Sqrt, sem="s")
            if layer == 0:
                s2 = act(smallA[:, 8:9], ssq, AF.Sqrt, sem="s")
            p.wait("vector", "s", s2)
            recipc(rstd, smallA[:, 4:5])
            if layer == 0:
                tt2c(smallA[:, 6:7], smallA[:, 8:9], rstd, ALU.mult)
                tsc(smallA[:, 7:8], smallA[:, 6:7], 1e-12, None, ALU.max)
                recipc(rninv[:, t:t + 1], smallA[:, 7:8])
            if track_amax:
                rmaxc(smallB[:, 0:1], SQ)
                tt2c(smallB[:, 1:2], rstd, rstd, ALU.mult)
                tt2c(smallB[:, 2:3], smallB[:, 0:1], smallB[:, 1:2], ALU.mult)
                tt2c(gmax, gmax, smallB[:, 2:3], ALU.max)
            v4 = ts(xt, a, nm_, rstd, ALU.add, ALU.mult, sem="v")
            p.wait("tensor", "v", v4)
            tms = None
            for c in range(NCH):
                tms = tr(TR[:, 128 * c:128 * (c + 1)], xt[:, 128 * c:128 * (c + 1)],
                         sem="mm" if c == NCH - 1 else None)
            p.wait("vector", "mm", tms)
            src = psum[:, 3072:3072 + 1024].rearrange("p (c r) -> p c r", r=128)
            dst = xn.rearrange("p (c tk) -> p c tk", c=NCH)[:, :,
                                                            128 * t:128 * (t + 1)]
            v5 = vcp(dst, src, sem="v")
            p.wait("tensor", "v", v5)
        p.barrier()
        if dbg_tap is not None:
            dd = p.dma("sync", dbg_tap, xn, "dD")
            p.wait("sync", "dD", dd)

    def phase_l0ffn(seq):
        ACCQ, ACCK = 0, TT * DH
        WQKV = 3 * TT * DH
        WSZ = 3 * NCH * DH
        KT = 2 * TT * DH
        QTf = KT + T
        OBUF = QTf + T
        EXPB = OBUF + TT * DH
        ACCV = WQKV + 2 * WSZ
        GS = WQKV
        vz0 = memset(pool[:, ACCV:ACCV + 129 * TT].rearrange(
            "p (b c) -> p b c", c=129)[:, :, 0:1], 1.0, sem="v")
        p.wait("scalar", "v", vz0)
        p.wait("tensor", "v", vz0)
        for t in range(TT):
            lg = psum[:16, 0:128]
            mt = None
            for c in range(NCH):
                mt = mm(lg, snN[:, 16 * c:16 * (c + 1)],
                        xnc(c)[:, 128 * t:128 * (t + 1)], c == 0, c == NCH - 1,
                        sem="mm" if c == NCH - 1 else None)
            p.wait("vector", "mm", mt)
            v1 = vcp(smallC[:16, 0:128], lg, sem="v")
            p.wait("tensor", "v", v1)
            t1 = tr(psum[:, 512:512 + 16], smallC[:16, 0:128], ident[:16, :16],
                    sem="mm")
            p.wait("vector", "mm", t1)
            lt = pl(GS, 16)
            g_ = pl(GS + 16, 16)
            hc = pl(GS + 32, 16)
            sl = pl(GS + 48, 16)
            e16 = pl(GS + 64, 16)
            fb = pl(GS + 80, 16)
            l2 = pl(GS + 96, 16)
            tsc(lt, psum[:, 512:528], rninv[:, t:t + 1], None, ALU.mult)
            tt2c(lt, lt, sg16, ALU.subtract)
            tsc(g_, lt, 0.0, None, ALU.max)
            tsc(hc, lt, 0.0, None, ALU.is_gt)
            rsumc(smallB[:, 3:4], hc)
            tsc(smallB[:, 4:5], smallB[:, 3:4], 0.5, None, ALU.is_lt)
            rmaxc(smallB[:, 5:6], lt)
            tsc(fb, lt, smallB[:, 5:6], None, ALU.is_equal)
            sttc(l2, fb, -BIG, lt, ALU.mult, ALU.add)
            rmaxc(smallB[:, 6:7], l2)
            tsc(fb, lt, smallB[:, 6:7], None, ALU.is_ge)
            sttc(sl, fb, smallB[:, 4:5], hc, ALU.mult, ALU.add)
            tt2c(l2, g_, sl, ALU.mult)
            tsc(fb, sl, 1.0, BIG, ALU.subtract, ALU.mult)
            tt2c(e16, l2, fb, ALU.add)
            v2 = rmaxc(smallB[:, 7:8], e16, negate=True)
            p.wait("scalar", "v", v2)
            s1 = act(e16, e16, AF.Exp, bias=smallB[:, 7:8], accum=smallB[:, 8:9],
                     sem="s")
            p.wait("vector", "s", s1)
            recipc(smallB[:, 9:10], smallB[:, 8:9])
            vlast = ts(w16[:, E * t:E * (t + 1)], e16, smallB[:, 9:10], None,
                       ALU.mult, sem="v")
            p.wait("tensor", "v", vlast)
        p.barrier()
        if cfg.debug and seq == 0:
            dd = p.dma("sync", dbg["d_w16"], w16, "dD")
            p.wait("sync", "dD", dd)

        pqkvO = (psum[:, 0:384], psum[:, 512:896])
        mm_done = {}
        eevd = {}
        epend = []
        for ei in range(E):
            buf = WQKV + (ei % 2) * WSZ
            sname = "dWa" if ei % 2 == 0 else "dWb"
            if ei >= 2:
                p.wait("sync", "mm", mm_done[ei - 2])
            ldt = p.dma("sync", pl(buf, WSZ),
                        dr["l0_qkvp"][ei].rearrange("(c p) h -> p c h", p=128),
                        sname)
            p.wait("tensor", sname, ldt)
            for t in range(TT):
                b = (ei * TT + t) % 2
                pqkv = pqkvO[b]
                p.wait("tensor", "ev", eevd.get(b))
                last = None
                for c in range(NCH):
                    lhs = xnc(c)[:, 128 * t:128 * (t + 1)]
                    last = mm(pqkv, lhs, pl(buf + 3 * DH * c, 3 * DH),
                              c == 0, c == NCH - 1,
                              sem="mm" if c == NCH - 1 else None)
                mm_done[ei] = last

                def evac(ei=ei, t=t, pqkv=pqkv, last=last, b=b):
                    p.wait("vector", "mm", last)
                    wcol = w16[:, E * t + ei:E * t + ei + 1]
                    for j, accb in enumerate((ACCQ, ACCK, ACCV)):
                        ps_ = pqkv[:, 128 * j:128 * (j + 1)]
                        if j == 2:
                            dst = pl(accb + 129 * t + 1, DH)
                        else:
                            dst = pl(accb + DH * t, DH)
                        if ei == 0:
                            eevd[b] = ts(dst, ps_, wcol, None, ALU.mult,
                                         sem="ev" if j == 2 else None)
                        else:
                            eevd[b] = stt(dst, ps_, wcol, dst, ALU.mult,
                                          ALU.add, sem="ev" if j == 2 else None)
                epend.append(evac)
                if len(epend) >= 2:
                    epend.pop(0)()
        for f in epend:
            f()
        p.barrier()
        if cfg.debug and seq == 0:
            dd = p.dma("sync", dbg["d_accq"], pl(ACCQ, TT * DH), "dD")
            p.wait("sync", "dD", dd)

        for srcb, dstb in ((ACCK, KT), (ACCQ, QTf)):
            tpend = []
            tev_last = None
            for t8 in range((TT + 7) // 8):
                nblk = min(8, TT - 8 * t8)
                base = 2048 + 1024 * (t8 % 2)
                tl = None
                for i in range(nblk):
                    kx = 8 * t8 + i
                    tl = tr(psum[:, base + 128 * i:base + 128 * (i + 1)],
                            pl(srcb + DH * kx, DH),
                            sem="mm" if i == nblk - 1 else None)

                def evac(t8=t8, nblk=nblk, base=base, tl=tl, dstb=dstb):
                    p.wait("vector", "mm", tl)
                    return vcp(pl(dstb + 1024 * t8, 128 * nblk),
                               psum[:, base:base + 128 * nblk], sem="ev")
                tpend.append(evac)
                if len(tpend) >= 2:
                    tev_last = tpend.pop(0)()
            for f in tpend:
                tev_last = f()
            p.wait("tensor", "ev", tev_last)
        p.barrier()
        SQB = psum[:, 3072:3200]
        memset(smallB[:, 24:26], 0.0, sem="v")
        p.wait("scalar", "v", p.cnt["v"])
        for j2, srcb in enumerate((ACCQ, ACCK)):
            for t in range(TT):
                sb = act(SQB, pl(srcb + DH * t, DH), AF.Square,
                         accum=smallB[:, 28:29], sem="s")
                p.wait("vector", "s", sb)
                tt2c(smallB[:, 24 + j2:25 + j2], smallB[:, 24 + j2:25 + j2],
                     smallB[:, 28:29], ALU.max)
                p.wait("scalar", "v", p.cnt["v"])
        vq = p.cnt["v"]
        p.wait("tensor", "v", vq)
        t1 = tr(psum[:1, 3200:3328], smallB[:, 24:25], sem="mm")
        t2 = tr(psum[:1, 3328:3456], smallB[:, 25:26], sem="mm")
        p.wait("vector", "mm", t2)
        vcpc(smallC[:1, 0:128], psum[:1, 3200:3328])
        rmaxc(smallA[:1, 16:17], smallC[:1, 0:128])
        vcpc(smallC[:1, 0:128], psum[:1, 3328:3456])
        v3 = rmaxc(smallA[:1, 17:18], smallC[:1, 0:128])
        p.wait("scalar", "v", v3)
        act(smallA[:1, 18:19], smallA[:1, 16:17], AF.Sqrt)
        s4 = act(smallA[:1, 19:20], smallA[:1, 17:18], AF.Sqrt, sem="s")
        p.wait("vector", "s", s4)
        tt2c(smallA[:1, 20:21], smallA[:1, 18:19], smallA[:1, 19:20], ALU.mult)
        v5 = tsc(smallA[:1, 21:22], smallA[:1, 20:21], -SC_FFN, None, ALU.mult)
        p.wait("tensor", "v", v5)
        bb = mm(psum[:, 3456:3457], onesr, smallA[:1, 21:22], True, True,
                sem="mm")
        p.wait("vector", "mm", bb)
        vb = vcp(smallB[:, 27:28], psum[:, 3456:3457], sem="v")
        p.wait("scalar", "v", vb)
        nbnd = smallB[:, 27:28]
        p.barrier()

        sexp = {0: None, 1: None, 2: None}
        pvdone = {0: None, 1: None, 2: None}
        scb = (0, 512, 3072)
        gevd = {}
        gpend = []
        pvlast = {}
        for qtg in range(TT // 4):
            for f in gpend:
                f()
            gpend = []
            pends = []
            for kc in range(TT):
                par = (qtg * TT + kc) % 3
                SCh = psum[:, scb[par]:scb[par] + 512]
                p.wait("tensor", "s", sexp[par])
                smm = mm(SCh, pl(KT + 128 * kc, 128),
                         pl(QTf + 512 * qtg, 512), True, True, sem="mm")
                p.wait("scalar", "mm", smm)
                p.wait("scalar", "mm", pvdone[par])
                s1 = act(pl(EXPB + 512 * par, 512), SCh, AF.Exp, bias=nbnd,
                         scale=SC_FFN, sem="s")
                sexp[par] = s1

                def pend_pv(kc=kc, par=par, s1=s1):
                    p.wait("tensor", "s", s1)
                    for qi in range(4):
                        pvb = psum[:, 1024 + 512 * qi:1024 + 512 * qi + 129]
                        if kc == 0:
                            p.wait("tensor", "ev", gevd.get(qi))
                        pvlast[qi] = mm(
                            pvb, pool[:, EXPB + 512 * par + 128 * qi:
                                      EXPB + 512 * par + 128 * (qi + 1)],
                            pl(ACCV + 129 * kc, 129), kc == 0, kc == TT - 1,
                            sem="mm" if qi == 3 else None)
                    pvdone[par] = pvlast[3]
                pends.append(pend_pv)
                if len(pends) >= 2:
                    pends.pop(0)()
            for f in pends:
                f()
            for qi in range(4):
                def evac(qi=qi, qt=4 * qtg + qi, last=pvlast[3],
                         pvb=psum[:, 1024 + 512 * qi:1024 + 512 * qi + 129]):
                    p.wait("vector", "mm", last)
                    recipc(smallA[:, 8 + qi:9 + qi], pvb[:, 0:1])
                    gevd[qi] = ts(pl(OBUF + DH * qt, DH), pvb[:, 1:129],
                                  smallA[:, 8 + qi:9 + qi], None, ALU.mult,
                                  sem="ev")
                gpend.append(evac)
        for f in gpend:
            f()
        for qi in range(4):
            p.wait("tensor", "ev", gevd.get(qi))
        p.barrier()
        if cfg.debug and seq == 0:
            dd = p.dma("sync", dbg["d_o"], pl(OBUF, TT * DH), "dD")
            p.wait("sync", "dD", dd)

        ZET = pl(EXPB, 256)
        OPW = 0
        OPSZ = 2 * D
        ztmp = pl(EXPB + 256, 128)
        mmo_done = {}
        opb = [0]
        opevd = {}
        oppend = []
        dblw = OBUF >= 2 * OPSZ
        for es in range(E // 2):
            ob = OPW + OPSZ * (es % 2) if dblw else OPW
            sname = "dOa" if (es % 2 == 0 or not dblw) else "dOb"
            gate = es - 2 if dblw else es - 1
            if gate >= 0:
                p.wait("sync", "mm", mmo_done[gate])
            ld2 = p.dma("sync", pl(ob, OPSZ),
                        dr["l0_op"][2 * es:2 * es + 2].rearrange("e p j -> p e j"),
                        sname)
            p.wait("tensor", sname, ld2)
            for t in range(TT):
                for k in range(2):
                    ei = 2 * es + k
                    wcol = w16[:, E * t + ei:E * t + ei + 1]
                    vz = ts(ztmp, pl(OBUF + DH * t, DH), wcol, None, ALU.mult,
                            sem="v")
                    p.wait("tensor", "v", vz)
                    tz = tr(psum[:, 2048 + 128 * k:2048 + 128 * (k + 1)], ztmp,
                            sem="mm")
                    p.wait("vector", "mm", tz)
                    ez = vcp(ZET[:, 128 * k:128 * (k + 1)],
                             psum[:, 2048 + 128 * k:2048 + 128 * (k + 1)],
                             sem="ev")
                    p.wait("tensor", "ev", ez)
                for ncol in range(2):
                    b = opb[0] % 2
                    opb[0] += 1
                    po = psum[:, 512 * b:512 * (b + 1)]
                    p.wait("tensor", "ev", opevd.get(b))
                    last = None
                    for k in range(2):
                        last = mm(po, ZET[:, 128 * k:128 * (k + 1)],
                                  pl(ob + D * k + 512 * ncol, 512),
                                  k == 0, k == 1, sem="mm" if k == 1 else None)
                    if t == TT - 1 and ncol == 1:
                        mmo_done[es] = last

                    def evac(t=t, ncol=ncol, po=po, last=last, b=b):
                        p.wait("vector", "mm", last)
                        dst = acc[:, D * t + 512 * ncol:
                                  D * t + 512 * (ncol + 1)]
                        opevd[b] = tt2(dst, po, dst, ALU.add, sem="ev")
                    oppend.append(evac)
                    if len(oppend) >= 2:
                        oppend.pop(0)()
        for f in oppend:
            f()
        p.barrier()
        if cfg.debug and seq == 0:
            dd = p.dma("sync", dbg["d_acc_ffn"], acc, "dD")
            p.wait("sync", "dD", dd)

    def phase_l0attn(seq):
        QF, KF = 0, 4096
        AT2 = 8192
        VP = 10240
        PB = 12304
        WSL = 13312
        EXPB0 = WSL
        TBC, TBS = 12288, 12800
        S0, S1_, S2, S3 = 8192, 8704, 9216, 9728
        free_wsl = None
        for g in range(NG):
            for wsrc, dstoff in ((dr["l0_wqg"], QF), (dr["l0_wkg"], KF)):
                p.wait("sync", "mm", free_wsl)
                lw = p.dma("sync", pl(WSL, 2048),
                           wsrc[g].rearrange("(c p) m -> p c m", p=128), "dW")
                p.wait("tensor", "dW", lw)
                qkevd = {}
                for sp in range(0, T, SPAN):
                    p.dma("sync", pl(TBC, SPAN), dr["ctab"][:, sp:sp + SPAN], "dT")
                    lt_ = p.dma("sync", pl(TBS, SPAN), dr["stab"][:, sp:sp + SPAN],
                                "dT")
                    qkpend = []
                    for mc in range(2):
                        last = None
                        p.wait("tensor", "ev", qkevd.get(mc))
                        for j in range(0, SPAN, 512):
                            for c in range(NCH):
                                last = mm(psum[:, 512 * mc:512 * mc + 512],
                                          pl(WSL + 256 * c + 128 * mc, 128),
                                          xnc(c)[:, sp + j:sp + j + 512],
                                          c == 0, c == NCH - 1,
                                          sem="mm" if c == NCH - 1 else None)
                            free_wsl = last

                        def qkevac(mc=mc, last=last):
                            p.wait("vector", "mm", last)
                            dst = pl((S0 if mc == 0 else S1_), 512)
                            qkevd[mc] = vcp(
                                dst, psum[:, 512 * mc:512 * mc + 512],
                                sem="ev")
                        qkpend.append(qkevac)
                        if len(qkpend) >= 2:
                            qkpend.pop(0)()
                    for f in qkpend:
                        f()
                    p.wait("vector", "dT", lt_)
                    p.wait("vector", "ev", p.cnt["ev"])
                    PE_, PO_ = pl(S0, SPAN), pl(S1_, SPAN)
                    tbc, tbs = pl(TBC, SPAN), pl(TBS, SPAN)
                    tt2(pl(S2, SPAN), PE_, tbc, ALU.mult)
                    tt2c(pl(S3, SPAN), PO_, tbs, ALU.mult)
                    tt2c(pl(S2, SPAN), pl(S2, SPAN), pl(S3, SPAN), ALU.subtract)
                    tt2(pl(S3, SPAN), PE_, tbs, ALU.mult)
                    tt2c(PO_, PO_, tbc, ALU.mult)
                    vr = p.chain("vector", lambda e, a_=pl(S3, SPAN), b_=PO_:
                                 e.tensor_tensor(out=b_, in0=a_, in1=b_,
                                                 op=ALU.add))
                    p.wait("sync", "v", vr)
                    rr = None
                    for hp in range(4):
                        pair_off = dstoff + 2048 * (hp // 2)
                        row = 64 * (hp % 2)
                        for half, srcoff in ((0, S2), (32, S1_)):
                            srcap = pool[32 * hp:32 * hp + 32,
                                         srcoff:srcoff + SPAN]
                            dstap = pool[row + half:row + half + 32,
                                         pair_off + sp:pair_off + sp + SPAN]
                            rr = p.dma("sync", dstap, srcap, "dR")
                    p.wait("vector", "dR", rr)
                    p.wait("tensor", "dR", rr)
            for pr2 in range(2):
                pr = 2 * g + pr2
                p.wait("sync", "mm", free_wsl)
                lv = p.dma("sync", pl(WSL, 1024),
                           dr["l0_wvp"][pr].rearrange("(c p) m -> p c m", p=128),
                           "dW")
                p.wait("tensor", "dW", lv)
                p.wait("vector", "mm", free_wsl)
                vzp = memset(pool[:, VP:VP + 129 * TT].rearrange(
                    "p (b c) -> p b c", c=129)[:, :, 64:65], 1.0, sem="v")
                p.wait("scalar", "v", vzp)
                p.wait("tensor", "v", vzp)
                pvvO = (psum[:, 1024:1152], psum[:, 1536:1664])
                vevd = {}
                vpend = []
                for t in range(TT):
                    b = t % 2
                    pvv = pvvO[b]
                    p.wait("tensor", "ev", vevd.get(b))
                    last = None
                    for c in range(NCH):
                        last = mm(pvv, xnc(c)[:, 128 * t:128 * (t + 1)],
                                  pl(WSL + 128 * c, 128), c == 0, c == NCH - 1,
                                  sem="mm" if c == NCH - 1 else None)
                    free_wsl = last

                    def evac(t=t, pvv=pvv, last=last, b=b):
                        p.wait("vector", "mm", last)
                        vcp(pl(VP + 129 * t, 64), pvv[:, 0:64])
                        vevd[b] = vcp(pl(VP + 129 * t + 65, 64),
                                      pvv[:, 64:128], sem="ev")
                    vpend.append(evac)
                    if len(vpend) >= 2:
                        vpend.pop(0)()
                for f in vpend:
                    f()
                p.wait("tensor", "ev", vevd.get(0))
                p.wait("tensor", "ev", vevd.get(1))
                pair_bound((QF + 2048 * pr2, KF + 2048 * pr2), PB,
                           (-SC_L0,), (27,))
                nbnd = smallB[:, 27:28]
                p.wait("tensor", "s", p.cnt["s"])
                gevd = {}
                qoff = QF + 2048 * pr2
                koff = KF + 2048 * pr2
                sexp = {}
                pvdone = {}
                scb0 = {0: (0, 512), 1: (3072, 3584)}
                exb0 = {0: (EXPB0, EXPB0 + 512), 1: (14336, 14848)}
                parc = [0]
                gpend = []
                for qtg in range(TT // 4):
                    for f in gpend:
                        f()
                    gpend = []
                    pvstop = {}
                    pends = []
                    for kc in range(4 * qtg + 4):
                        qi0 = max(0, kc - 4 * qtg)
                        par = parc[0] % 2
                        parc[0] += 1
                        w = 128 * (4 - qi0)
                        smms = {}
                        for hh in range(2):
                            qrow = 64 * hh
                            SCh = psum[:, scb0[hh][par]:scb0[hh][par] + w]
                            p.wait("tensor", "s", sexp.get((hh, par)))
                            smms[hh] = mm(
                                SCh,
                                pool[qrow:qrow + 64,
                                     koff + 128 * kc:koff + 128 * (kc + 1)],
                                pool[qrow:qrow + 64,
                                     qoff + 512 * qtg + 128 * qi0:
                                     qoff + 512 * (qtg + 1)],
                                True, True, sem="mm")
                        s1s = {}
                        for hh in range(2):
                            SCh = psum[:, scb0[hh][par]:scb0[hh][par] + w]
                            if kc >= 4 * qtg:
                                p.wait("vector", "mm", smms[hh])
                                vdg = tt2c(SCh[:, 0:128], SCh[:, 0:128],
                                           causmT, ALU.add)
                                p.wait("scalar", "v", vdg)
                            p.wait("scalar", "mm", smms[hh])
                            p.wait("scalar", "mm", pvdone.get((hh, par)))
                            s1s[hh] = act(pl(exb0[hh][par], w), SCh, AF.Exp,
                                          bias=nbnd, scale=SC_L0, sem="s")
                            sexp[(hh, par)] = s1s[hh]

                        def pend_pv(kc=kc, par=par, s1s=dict(s1s), qi0=qi0,
                                    qtg=qtg):
                            p.wait("tensor", "s", s1s[1])
                            for qi in range(qi0, 4):
                                qt = 4 * qtg + qi
                                if kc == 0:
                                    p.wait("tensor", "ev", gevd.get(qi))
                                stop = kc == qt
                                pvl = None
                                for hh in range(2):
                                    pvb = psum[:, 1024 + 512 * qi + 128 * hh:
                                               1024 + 512 * qi + 128 * hh + 65]
                                    pvl = mm(
                                        pvb,
                                        pool[:, exb0[hh][par] +
                                             128 * (qi - qi0):
                                             exb0[hh][par] +
                                             128 * (qi - qi0) + 128],
                                        pl(VP + 129 * kc + 64 * hh, 65),
                                        kc == 0 and hh == 0,
                                        stop and hh == 1,
                                        sem="mm" if (stop or qi == 3)
                                        and hh == 1 else None)
                                if stop:
                                    pvstop[qi] = pvl
                            pvdone[(0, par)] = pvl
                            pvdone[(1, par)] = pvl
                        pends.append(pend_pv)
                        if len(pends) >= 2:
                            pends.pop(0)()
                    for f in pends:
                        f()
                    for qi in range(4):
                        def evac(qi=qi, qt=4 * qtg + qi, last=pvstop[qi]):
                            p.wait("vector", "mm", last)
                            for hh in range(2):
                                pvb = psum[:, 1024 + 512 * qi + 128 * hh:
                                           1024 + 512 * qi + 128 * hh + 65]
                                zc = pvb[:, 64:65] if hh == 0 else pvb[:, 0:1]
                                vo = 0 if hh == 0 else 1
                                rc_ = 8 + qi if hh == 0 else 40 + qi
                                recipc(smallA[:, rc_:rc_ + 1], zc)
                                gevd[qi] = ts(
                                    pl(AT2 + 128 * qt + 64 * hh, 64),
                                    pvb[:, vo:vo + 64],
                                    smallA[:, rc_:rc_ + 1], None, ALU.mult,
                                    sem="ev")
                        gpend.append(evac)
                for f in gpend:
                    f()
                for qi in range(4):
                    p.wait("tensor", "ev", gevd.get(qi))
                p.wait("sync", "mm", p.cnt["mm"])
                lwo = p.dma("sync", pl(WSL + 1024, 1024), dr["l0_wop"][pr], "dW")
                ATT = VP
                tpend = []
                tev_last = None
                for t8 in range((TT + 7) // 8):
                    nblk = min(8, TT - 8 * t8)
                    base = 2048 + 1024 * (t8 % 2)
                    tl = None
                    for i in range(nblk):
                        kx = 8 * t8 + i
                        tl = tr(psum[:, base + 128 * i:base + 128 * (i + 1)],
                                pl(AT2 + 128 * kx, 128),
                                sem="mm" if i == nblk - 1 else None)

                    def evac(t8=t8, nblk=nblk, base=base, tl=tl):
                        p.wait("vector", "mm", tl)
                        return vcp(pl(ATT + 1024 * t8, 128 * nblk),
                                   psum[:, base:base + 128 * nblk], sem="ev")
                    tpend.append(evac)
                    if len(tpend) >= 2:
                        tev_last = tpend.pop(0)()
                for f in tpend:
                    tev_last = f()
                p.wait("tensor", "ev", tev_last)
                p.wait("tensor", "dW", lwo)
                evd = {}
                pend = []
                for i, (t, ncol) in enumerate(
                        (t, ncol) for t in range(TT) for ncol in range(2)):
                    b = i % 2
                    po = psum[:, 512 * b:512 * (b + 1)]
                    p.wait("tensor", "ev", evd.get(b))
                    last = mm(po, pl(ATT + 128 * t, 128),
                              pl(WSL + 1024 + 512 * ncol, 512), True, True,
                              sem="mm")
                    free_wsl = last

                    def evac(t=t, ncol=ncol, po=po, last=last, b=b):
                        p.wait("vector", "mm", last)
                        dst = acc[:, D * t + 512 * ncol:
                                  D * t + 512 * (ncol + 1)]
                        evd[b] = tt2(dst, po, dst, ALU.add, sem="ev")
                    pend.append(evac)
                    if len(pend) >= 2:
                        pend.pop(0)()
                for f in pend:
                    f()
                p.wait("tensor", "ev", evd.get(0))
                p.wait("tensor", "ev", evd.get(1))
        p.barrier()
        if cfg.debug and seq == 0:
            dd = p.dma("sync", dbg["d_acc_l0"], acc, "dD")
            p.wait("sync", "dD", dd)
            p.dma("sync", dbg["d_pool0"][:, 0:128 * TT],
                  pool[:, 8192:8192 + 128 * TT], "dD")
            p.dma("sync", dbg["d_pool0"][:, 128 * TT:257 * TT],
                  pool[:, 10240:10240 + 129 * TT], "dD")
            p.dma("sync", dbg["d_pool0"][:, 257 * TT:385 * TT],
                  pool[:, 2048:2048 + 128 * TT], "dD")
            p.dma("sync", dbg["d_pool0"][:, 385 * TT:513 * TT],
                  pool[:, 6144:6144 + 128 * TT], "dD")
            dp = p.dma("sync", dbg["d_pool0"][:, 513 * TT:513 * TT + 130],
                       pool[:, 13312:13312 + 130], "dD")
            p.wait("sync", "dD", dp)

    def phase_l1attn(seq):
        LAT = 0
        WKVH = 4096
        KVTG = 8192
        WDN = 10240
        latT = pl(LAT, 2 * T)
        ldn = p.dma("sync", pl(WDN, 2048),
                    dr["l1_wdown"].rearrange("(c p) j -> p c j", p=128), "dW")
        p.wait("tensor", "dW", ldn)
        NTG = T // 512
        kv_mm_done = None
        for hkv in range(2):
            cols = slice(512 * hkv, 512 * (hkv + 1))
            p.wait("gpsimd", "mm", kv_mm_done)
            l1 = p.dma("gpsimd", pl(WKVH, 4096),
                       dr["l1_wk"][:, cols].rearrange("(c p) j -> p c j", p=128),
                       "dG")
            p.wait("gpsimd", "dG", l1)
            l2_ = p.emit("gpsimd", lambda e, cols=cols: e.dma_start(
                out=pl(WKVH, 4096),
                in_=dr["l1_wv"][:, cols].rearrange("(c p) j -> p c j", p=128),
                accum_op=ALU.add), "dG", 16)
            p.wait("tensor", "dG", l2_)
            kevd = {}
            levd = {}
            for tg in range(NTG):
                kpend = []
                for kvi in range(4):
                    kvc = 4 * hkv + kvi
                    b = kvi % 2
                    pkv = psum[:, 512 * b:512 * (b + 1)]
                    p.wait("tensor", "ev", kevd.get(b))
                    last = None
                    for c in range(NCH):
                        last = mm(pkv, pl(WKVH + 512 * c + 128 * kvi, 128),
                                  xnc(c)[:, 512 * tg:512 * (tg + 1)],
                                  c == 0, c == NCH - 1,
                                  sem="mm" if c == NCH - 1 else None)
                    kv_mm_done = last

                    def evac(kvi=kvi, pkv=pkv, last=last, b=b):
                        p.wait("vector", "mm", last)
                        kevd[b] = vcp(pl(KVTG + 512 * kvi, 512), pkv, sem="ev")
                    kpend.append(evac)
                    if len(kpend) >= 2:
                        kpend.pop(0)()
                for f in kpend:
                    f()
                p.wait("tensor", "ev", kevd.get(0))
                p.wait("tensor", "ev", kevd.get(1))
                lpend = []
                for mc2 in range(2):
                    b = mc2 % 2
                    plat = psum[:, 1024 + 512 * b:1024 + 512 * (b + 1)]
                    p.wait("tensor", "ev", levd.get(b))
                    last = None
                    for kvi in range(4):
                        kvc = 4 * hkv + kvi
                        last = mm(plat, pl(WDN + 256 * kvc + 128 * mc2, 128),
                                  pl(KVTG + 512 * kvi, 512), kvi == 0, kvi == 3,
                                  sem="mm" if kvi == 3 else None)

                    def evac(tg=tg, mc2=mc2, plat=plat, last=last, b=b,
                             hkv=hkv):
                        p.wait("vector", "mm", last)
                        dstl = latT[:, T * mc2 + 512 * tg:
                                    T * mc2 + 512 * (tg + 1)]
                        if hkv == 0:
                            levd[b] = vcp(dstl, plat, sem="ev")
                        else:
                            levd[b] = tt2(dstl, plat, dstl, ALU.add, sem="ev")
                    lpend.append(evac)
                    if len(lpend) >= 2:
                        lpend.pop(0)()
                for f in lpend:
                    f()
            p.wait("tensor", "ev", levd.get(0))
            p.wait("tensor", "ev", levd.get(1))
        p.barrier()
        if cfg.debug and seq == 0:
            dd = p.dma("sync", dbg["d_latT"], latT, "dD")
            p.wait("sync", "dD", dd)

        RING, E2B, KTMP = 0, 3072, 3584
        Q2, KA, KB = 4096, 6144, 8192
        V2 = 10240
        AT2 = 12304
        WQS = 14352
        exb1 = {0: (1024, 1536), 1: (2048, 2560)}
        scb1 = {0: (0, 512), 1: (3072, 3584)}
        WUK = smallC[:, 0:256]
        WUV = smallC[:, 256:512]
        sl = p.dma("sync", spill_lat, latT, "dS")
        p.wait("sync", "dS", sl)
        p.wait("scalar", "dS", sl)
        p.wait("vector", "dS", sl)
        p.wait("tensor", "dS", sl)
        vzo = memset(pool[:, V2:V2 + 129 * TT].rearrange(
            "p (b c) -> p b c", c=129)[:, :, 64:65], 1.0, sem="v")
        p.wait("scalar", "v", vzo)
        p.wait("tensor", "v", vzo)
        free_wq = None
        for pr in range(NP):
            p.wait("sync", "mm", free_wq)
            lq = p.dma("sync", pl(WQS, 1024),
                       dr["l1_wqg"][pr].rearrange("(c p) m -> p c m", p=128), "dW")
            p.dma("sync", WUK,
                  dr["l1_wupkg"][pr].rearrange("(c p) m -> p c m", p=128), "dW")
            lv = p.dma("sync", WUV,
                       dr["l1_wupvg"][pr].rearrange("(c p) m -> p c m", p=128),
                       "dW")
            p.wait("tensor", "dW", lv)
            vzq = memset(smallB[:, 30:31], 0.0, sem="v")
            p.wait("scalar", "v", vzq)
            qtail = {}
            qpend2 = []
            for t in range(TT):
                b = t % 2
                ptq = psum[:, 512 * b:512 * b + 128]
                qo = 13328 + 128 * b
                trb = 2048 + 512 * b
                p.wait("tensor", "ev", qtail.get(b))
                last = None
                for c in range(NCH):
                    last = mm(ptq, xnc(c)[:, 128 * t:128 * (t + 1)],
                              pl(WQS + 128 * c, 128), c == 0, c == NCH - 1,
                              sem="mm" if c == NCH - 1 else None)
                free_wq = last

                def qchain(t=t, b=b, ptq=ptq, qo=qo, trb=trb, last=last):
                    acol = 36 if b == 0 else 37
                    p.wait("vector", "mm", last)
                    vsq = ts(pl(qo, 128), ptq, escA[:, t:t + 1], None,
                             ALU.mult, sem="v")
                    p.wait("scalar", "v", vsq)
                    sq = act(psum[:, 3072 + 128 * b:3200 + 128 * b],
                             pl(qo, 128), AF.Square,
                             accum=smallA[:, acol:acol + 1], sem="s")
                    p.wait("vector", "s", sq)
                    tt2c(smallB[:, 30:31], smallB[:, 30:31],
                         smallA[:, acol:acol + 1], ALU.max)
                    p.wait("tensor", "v", vsq)
                    tq = tr(psum[:, trb:trb + 128], pl(qo, 128), sem="mm")
                    p.wait("vector", "mm", tq)
                    qtail[b] = vcp(pl(Q2 + 128 * t, 128),
                                   psum[:, trb:trb + 128], sem="ev")
                qpend2.append(qchain)
                if len(qpend2) >= 2:
                    qpend2.pop(0)()
            for f in qpend2:
                f()
            p.wait("tensor", "v", p.cnt["v"])
            tqa = tr(psum[:1, 3328:3456], smallB[:, 30:31], sem="mm")
            p.wait("vector", "mm", tqa)
            vcpc(smallD[:1, 0:128], psum[:1, 3328:3456])
            v9 = rmaxc(smallA[:1, 23:24], smallD[:1, 0:128])
            p.wait("scalar", "v", v9)
            s9 = act(smallA[:1, 38:39], smallA[:1, 23:24], AF.Sqrt, sem="s")
            p.wait("vector", "s", s9)
            vz = memset(smallB[:, 28:30], 0.0, sem="v")
            p.wait("scalar", "v", vz)
            ktail = {}
            kpend2 = []
            lastuse = {}
            ldt = {}

            def lat_load(t):
                if t >= TT:
                    return
                s = t % 4
                p.wait("sync", "mm", lastuse.get(t - 4))
                p.dma("sync", pl(RING + 256 * s, 128),
                      spill_lat[:, 128 * t:128 * (t + 1)], f"dR{s}")
                ldt[t] = p.dma("sync", pl(RING + 256 * s + 128, 128),
                               spill_lat[:, T + 128 * t:T + 128 * (t + 1)],
                               f"dR{s}")
            for t in range(4):
                lat_load(t)
            for t in range(TT):
                b = t % 2
                s = t % 4
                pk_ = psum[:, 1024 + 512 * b:1024 + 512 * b + 128]
                pv_ = psum[:, 512 * b:512 * b + 128]
                ko = KTMP + 256 * b
                trb = 2048 + 512 * b
                p.wait("tensor", f"dR{s}", ldt[t])
                p.wait("tensor", "ev", ktail.get(b))
                mm(pk_, pl(RING + 256 * s, 128), WUK[:, 0:128],
                   True, False)
                mm(pk_, pl(RING + 256 * s + 128, 128),
                   WUK[:, 128:256], False, True, sem="mm")
                mm(pv_, pl(RING + 256 * s, 128), WUV[:, 0:128],
                   True, False)
                lastk = mm(pv_, pl(RING + 256 * s + 128, 128),
                           WUV[:, 128:256], False, True, sem="mm")
                lastuse[t] = lastk
                lat_load(t + 4)

                def chain(t=t, b=b, pk_=pk_, pv_=pv_, ko=ko, trb=trb,
                          lastk=lastk):
                    acol = 28 if b == 0 else 32
                    p.wait("vector", "mm", lastk)
                    ts(pl(ko, 128), pk_, eplus[:, t:t + 1], None, ALU.mult)
                    vb = ts(pl(ko + 128, 128), pk_, eminus[:, t:t + 1], None,
                            ALU.mult, sem="v")
                    vcp(pl(V2 + 129 * t, 64), pv_[:, 0:64])
                    vv = vcp(pl(V2 + 129 * t + 65, 64), pv_[:, 64:128],
                             sem="ev")
                    p.wait("scalar", "v", vb)
                    act(psum[:, 3072 + 256 * b:3200 + 256 * b], pl(ko, 128),
                        AF.Square, accum=smallA[:, acol:acol + 1])
                    sqb = act(psum[:, 3200 + 256 * b:3328 + 256 * b],
                              pl(ko + 128, 128), AF.Square,
                              accum=smallA[:, acol + 1:acol + 2], sem="s")
                    p.wait("vector", "s", sqb)
                    tt2c(smallB[:, 28:29], smallB[:, 28:29],
                         smallA[:, acol:acol + 1], ALU.max)
                    tt2c(smallB[:, 29:30], smallB[:, 29:30],
                         smallA[:, acol + 1:acol + 2], ALU.max)
                    p.wait("tensor", "v", vb)
                    tr(psum[:, trb:trb + 128], pl(ko, 128))
                    tB = tr(psum[:, trb + 128:trb + 256], pl(ko + 128, 128),
                            sem="mm")
                    p.wait("vector", "mm", tB)
                    vcp(pl(KA + 128 * t, 128), psum[:, trb:trb + 128])
                    ktail[b] = vcp(pl(KB + 128 * t, 128),
                                   psum[:, trb + 128:trb + 256], sem="ev")
                kpend2.append(chain)
                if len(kpend2) >= 2:
                    kpend2.pop(0)()
            for f in kpend2:
                f()
            p.wait("tensor", "v", p.cnt["v"])
            tka = tr(psum[:1, 3328:3456], smallB[:, 28:29], sem="mm")
            tkb = tr(psum[:1, 3456:3584], smallB[:, 29:30], sem="mm")
            p.wait("vector", "mm", tkb)
            vcpc(smallC[:1, 0:128], psum[:1, 3328:3456])
            rmaxc(smallA[:1, 24:25], smallC[:1, 0:128])
            vcpc(smallC[:1, 0:128], psum[:1, 3456:3584])
            v6 = rmaxc(smallA[:1, 25:26], smallC[:1, 0:128])
            p.wait("scalar", "v", v6)
            act(smallA[:1, 26:27], smallA[:1, 24:25], AF.Sqrt)
            s7 = act(smallA[:1, 27:28], smallA[:1, 25:26], AF.Sqrt, sem="s")
            p.wait("vector", "s", s7)
            tsc(smallA[:1, 27:28], smallA[:1, 27:28],
                float(np.exp(2.0 * C_DECAY * (T - 1))), None, ALU.mult)
            tt2c(smallA[:1, 19:20], smallA[:1, 26:27], smallA[:1, 27:28],
                 ALU.max)
            tt2c(smallA[:1, 20:21], smallA[:1, 38:39], smallA[:1, 19:20],
                 ALU.mult)
            v8 = tsc(smallA[:1, 21:22], smallA[:1, 20:21], -1.0, None,
                     ALU.mult)
            p.wait("tensor", "v", v8)
            bb8 = mm(psum[:, 3584:3585], onesr, smallA[:1, 21:22], True, True,
                     sem="mm")
            p.wait("vector", "mm", bb8)
            vb8 = vcp(smallB[:, 27:28], psum[:, 3584:3585], sem="v")
            p.wait("scalar", "v", vb8)
            nbnd = smallB[:, 27:28]
            gevd = {}
            p.wait("tensor", "s", p.cnt["s"])
            sexp = {}
            pvdone = {}
            parc = [0]
            gpend = []
            for qtg in range(TT // 4):
                for f in gpend:
                    f()
                gpend = []
                p.wait("tensor", "v", p.cnt["v"])
                p.wait("tensor", "s", p.cnt["s"])
                tq2 = None
                for qi in range(4):
                    qt = 4 * qtg + qi
                    tq2 = tr(psum[:1, 3072 + 128 * qi:
                                  3072 + 128 * (qi + 1)],
                             e2plus[:, qt:qt + 1],
                             sem="mm" if qi == 3 else None)
                p.wait("vector", "mm", tq2)
                ve2 = vcp(smallD[:1, 0:512], psum[:1, 3072:3584], sem="v")
                p.wait("tensor", "v", ve2)
                e2l = None
                for qi in range(4):
                    e2l = mm(psum[:, 1024 + 128 * qi:1024 + 128 * (qi + 1)],
                             onesr, smallD[:1, 128 * qi:128 * (qi + 1)],
                             True, True, sem="mm" if qi == 3 else None)
                p.wait("vector", "mm", e2l)
                ve2b = vcpc(pl(E2B, 512), psum[:, 1024:1536])
                p.wait("tensor", "v", ve2b)
                pvstop = None
                pends = []
                for kc in range(TT):
                    nB = max(0, min(kc - 4 * qtg, 4))
                    pieces = []
                    if nB:
                        pieces.append(("B", 0, nB))
                    if nB < 4:
                        pieces.append(("A", nB, 4))
                    for kind, qi0, qi1 in pieces:
                        par = parc[0] % 2
                        parc[0] += 1
                        w = 128 * (qi1 - qi0)
                        ko = KA if kind == "A" else KB
                        smms = {}
                        for hh in range(2):
                            qrow = 64 * hh
                            SCh = psum[:, scb1[hh][par]:scb1[hh][par] + w]
                            p.wait("tensor", "s", sexp.get((hh, par)))
                            smms[hh] = mm(
                                SCh,
                                pool[qrow:qrow + 64,
                                     ko + 128 * kc:ko + 128 * (kc + 1)],
                                pool[qrow:qrow + 64,
                                     Q2 + 512 * qtg + 128 * qi0:
                                     Q2 + 512 * qtg + 128 * qi1],
                                True, True, sem="mm")
                        s1s = {}
                        for hh in range(2):
                            SCh = psum[:, scb1[hh][par]:scb1[hh][par] + w]
                            vfix = None
                            if kind == "B":
                                p.wait("vector", "mm", smms[hh])
                                for qi in range(qi0, qi1):
                                    off = 128 * (qi - qi0)
                                    vfix = tt2c(
                                        SCh[:, off:off + 128],
                                        SCh[:, off:off + 128],
                                        pl(E2B + 128 * qi, 128), ALU.mult)
                            elif 4 * qtg <= kc < 4 * qtg + 4:
                                qi_d = kc - 4 * qtg
                                off = 128 * (qi_d - qi0)
                                p.wait("vector", "mm", smms[hh])
                                vfix = tt2c(SCh[:, off:off + 128],
                                            SCh[:, off:off + 128], dcorrT,
                                            ALU.mult)
                            if vfix is not None:
                                p.wait("scalar", "v", vfix)
                            p.wait("scalar", "mm", smms[hh])
                            p.wait("scalar", "mm", pvdone.get((hh, par)))
                            s1s[hh] = act(pl(exb1[hh][par], w), SCh, AF.Exp,
                                          bias=nbnd, sem="s")
                            sexp[(hh, par)] = s1s[hh]

                        def pend_pv(kc=kc, par=par, s1s=dict(s1s), qi0=qi0,
                                    qi1=qi1):
                            nonlocal pvstop
                            p.wait("tensor", "s", s1s[1])
                            for qi in range(qi0, qi1):
                                if kc == 0:
                                    p.wait("tensor", "ev", gevd.get(qi))
                                pvl = None
                                for hh in range(2):
                                    pvb = psum[:, 1024 + 512 * qi + 128 * hh:
                                               1024 + 512 * qi +
                                               128 * hh + 65]
                                    pvl = mm(
                                        pvb,
                                        pool[:, exb1[hh][par] +
                                             128 * (qi - qi0):
                                             exb1[hh][par] +
                                             128 * (qi - qi0) + 128],
                                        pl(V2 + 129 * kc + 64 * hh, 65),
                                        kc == 0 and hh == 0,
                                        kc == TT - 1 and hh == 1,
                                        sem="mm" if kc == TT - 1
                                        and hh == 1 else
                                        ("mm" if qi == qi1 - 1 and hh == 1
                                         else None))
                                if kc == TT - 1:
                                    pvstop = pvl
                            pvdone[(0, par)] = pvl
                            pvdone[(1, par)] = pvl
                        pends.append(pend_pv)
                        if len(pends) >= 2:
                            pends.pop(0)()
                for f in pends:
                    f()
                for qi in range(4):
                    def evac(qi=qi, qt=4 * qtg + qi, last=pvstop):
                        p.wait("vector", "mm", last)
                        for hh in range(2):
                            pvb = psum[:, 1024 + 512 * qi + 128 * hh:
                                       1024 + 512 * qi + 128 * hh + 65]
                            zc = pvb[:, 64:65] if hh == 0 else pvb[:, 0:1]
                            vo = 0 if hh == 0 else 1
                            rc_ = 8 + qi if hh == 0 else 40 + qi
                            recipc(smallA[:, rc_:rc_ + 1], zc)
                            gevd[qi] = ts(
                                pl(AT2 + 128 * qt + 64 * hh, 64),
                                pvb[:, vo:vo + 64],
                                smallA[:, rc_:rc_ + 1], None, ALU.mult,
                                sem="ev")
                    gpend.append(evac)
            for f in gpend:
                f()
            for qi in range(4):
                p.wait("tensor", "ev", gevd.get(qi))
            p.wait("sync", "mm", p.cnt["mm"])
            lwo = p.dma("sync", pl(WQS, 1024),
                        dr["l1_wo"][128 * pr:128 * (pr + 1), :], "dW")
            ATT = Q2
            tpend = []
            tev_last = None
            for t8 in range((TT + 7) // 8):
                nblk = min(8, TT - 8 * t8)
                base = 2048 + 1024 * (t8 % 2)
                tl = None
                for i in range(nblk):
                    kx = 8 * t8 + i
                    tl = tr(psum[:, base + 128 * i:base + 128 * (i + 1)],
                            pl(AT2 + 128 * kx, 128),
                            sem="mm" if i == nblk - 1 else None)

                def evac(t8=t8, nblk=nblk, base=base, tl=tl):
                    p.wait("vector", "mm", tl)
                    return vcp(pl(ATT + 1024 * t8, 128 * nblk),
                               psum[:, base:base + 128 * nblk], sem="ev")
                tpend.append(evac)
                if len(tpend) >= 2:
                    tev_last = tpend.pop(0)()
            for f in tpend:
                tev_last = f()
            p.wait("tensor", "ev", tev_last)
            p.wait("tensor", "dW", lwo)
            evd = {}
            pend = []
            for i, (t, ncol) in enumerate(
                    (t, ncol) for t in range(TT) for ncol in range(2)):
                b = i % 2
                po = psum[:, 512 * b:512 * (b + 1)]
                p.wait("tensor", "ev", evd.get(b))
                last = mm(po, pl(ATT + 128 * t, 128),
                          pl(WQS + 512 * ncol, 512), True, True, sem="mm")
                free_wq = last

                def evac(t=t, ncol=ncol, po=po, last=last, b=b):
                    p.wait("vector", "mm", last)
                    dst = acc[:, D * t + 512 * ncol:D * t + 512 * (ncol + 1)]
                    evd[b] = tt2(dst, po, dst, ALU.add, sem="ev")
                pend.append(evac)
                if len(pend) >= 2:
                    pend.pop(0)()
            for f in pend:
                f()
            p.wait("tensor", "ev", evd.get(0))
            p.wait("tensor", "ev", evd.get(1))
        p.barrier()
        if cfg.debug and seq == 0:
            dd = p.dma("sync", dbg["d_acc_at1"], acc, "dD")
            p.wait("sync", "dD", dd)

    def phase_amax_cc():
        t1 = tr(psum[:1, 3072:3200], gmax, sem="mm")
        p.wait("vector", "mm", t1)
        v1 = vcp(smallC[:1, 0:128], psum[:1, 3072:3200], sem="v")
        p.wait("vector", "v", v1)
        rmaxc(smallA[:1, 10:11], smallC[:1, 0:128])
        v2 = memset(amax1, 0.0, sem="v")
        p.wait("scalar", "v", v2)
        s1 = act(amax1[:1, 0:1], smallA[:1, 10:11], AF.Sqrt, sem="s")
        p.wait("gpsimd", "s", s1)
        p.wait("gpsimd", "v", v2)
        p.wait("sync", "s", s1)
        p.wait("sync", "v", v2)
        if cfg.use_cc:
            g1_ = p.dma("gpsimd", cc_in[:, :], amax1, "dG")
            p.wait("gpsimd", "dG", g1_)
            cc = p.emit("gpsimd", lambda e: e.collective_compute(
                "AllReduce", ALU.max,
                replica_groups=[list(range(cfg.n_cores))],
                ins=[cc_in.ap().opt()], outs=[cc_out.ap().opt()]), "dCC", 1)
            p.wait("gpsimd", "dCC", cc)
            g2_ = p.dma("gpsimd", amax1, cc_out[:, :], "dG")
            p.wait("vector", "dG", g2_)
        if cfg.debug:
            dd = p.dma("sync", dbg["d_amax"], amax1, "dD")
            p.wait("sync", "dD", dd)

    def emit_scale_from_amax(a_ap, s_col, r_col, tag):
        i0 = {"x": 16, "w": 24}[tag]
        ts(smallA[:1, i0:i0 + 1], a_ap, 1e-12, None, ALU.max)
        p.barrier()
        recip(smallA[:1, i0 + 1:i0 + 2], smallA[:1, i0:i0 + 1])
        p.barrier()
        ts(smallA[:1, i0 + 2:i0 + 3], smallA[:1, i0 + 1:i0 + 2], 448.0, None,
           ALU.mult)
        p.barrier()
        ts(smallA[:1, i0 + 3:i0 + 4], smallA[:1, i0 + 2:i0 + 3], 448.0, None,
           ALU.min)
        p.barrier()
        recip(smallA[:1, i0 + 4:i0 + 5], smallA[:1, i0 + 3:i0 + 4])
        ts(smallA[:1, i0 + 5:i0 + 6], smallA[:1, i0 + 3:i0 + 4], 0.5, None,
           ALU.mult)
        p.barrier()
        m1 = mm(psum[:, 3584:3585], onesr, smallA[:1, i0 + 5:i0 + 6],
                True, True)
        m2 = mm(psum[:, 3586:3587], onesr, smallA[:1, i0 + 4:i0 + 5],
                True, True, sem="mm")
        p.wait("vector", "mm", m2)
        vcp(s_col, psum[:, 3584:3585])
        vcp(r_col, psum[:, 3586:3587])
        p.barrier()

    BIASB = 0
    WT = 1024

    def phase_wquant():
        lbr = p.dma("sync", pool[:1, WT:WT + D], dr["l1_bias"], "dW")
        p.wait("tensor", "dW", lbr)
        mm(psum[:, 0:512], onesr, pool[:1, WT:WT + 512], True, True)
        m2 = mm(psum[:, 512:1024], onesr, pool[:1, WT + 512:WT + D], True, True,
                sem="mm")
        p.wait("vector", "mm", m2)
        ev = vcp(pl(BIASB, D), psum[:, 0:1024], sem="ev")
        p.wait("sync", "ev", ev)
        memset(smallA[:, 12:13], 0.0)
        for c in range(NCH):
            lw = p.dma("sync", pl(WT, D), dr["l1_wt"][128 * c:128 * (c + 1), :],
                       "dW")
            p.wait("vector", "dW", lw)
            rmaxc(smallA[:, 13:14], pl(WT, D), absval=True)
            va = tt2c(smallA[:, 12:13], smallA[:, 12:13], smallA[:, 13:14],
                      ALU.max)
            p.wait("sync", "v", va)
        p.barrier()
        t1 = tr(psum[:1, 3072:3200], smallA[:, 12:13], sem="mm")
        p.wait("vector", "mm", t1)
        v1 = vcp(smallC[:1, 0:128], psum[:1, 3072:3200], sem="v")
        p.wait("vector", "v", v1)
        rmaxc(smallA[:1, 30:31], smallC[:1, 0:128])
        p.barrier()
        emit_scale_from_amax(smallA[:1, 30:31], smallB[:, 16:17],
                             smallB[:, 17:18], "w")
        for c in range(NCH):
            lw = p.dma("sync", pl(WT, D), dr["l1_wt"][128 * c:128 * (c + 1), :],
                       "dW")
            p.wait("vector", "dW", lw)
            tsc(pl(WT, D), pl(WT, D), smallB[:, 16:17], -224.0, ALU.mult,
                ALU.max)
            vq = ts(w8[:, D * c:D * (c + 1)], pl(WT, D), 224.0, None, ALU.min,
                    sem="v")
            p.wait("sync", "v", vq)
            p.wait("vector", "v", vq)
        p.barrier()

    def phase_l1ffn(seq):
        qfree = {0: None, 1: None}
        fevd = {}
        fpend = []
        for t in range(TT):
            bb = t % 2
            po0 = psum[:, 1024 * bb:1024 * bb + 512]
            po1 = psum[:, 1024 * bb + 512:1024 * bb + 1024]
            p.wait("tensor", "ev", fevd.get(bb))
            last = None
            for c in range(NCH):
                slot = c % 2
                p.wait("vector", "mm", qfree[slot])
                tsc(smallC[:, 0:128], xnc(c)[:, 128 * t:128 * (t + 1)],
                    smallB[:, 18:19], -224.0, ALU.mult, ALU.max)
                vq = tsc(q8t[:, 128 * slot:128 * (slot + 1)], smallC[:, 0:128],
                         224.0, None, ALU.min)
                p.wait("tensor", "v", vq)
                mm(po0, q8t[:, 128 * slot:128 * (slot + 1)],
                   w8[:, D * c:D * c + 512], c == 0, c == NCH - 1)
                last = mm(po1, q8t[:, 128 * slot:128 * (slot + 1)],
                          w8[:, D * c + 512:D * (c + 1)], c == 0, c == NCH - 1,
                          sem="mm")
                qfree[slot] = last

            def evac(t=t, po0=po0, po1=po1, last=last, bb=bb):
                p.wait("vector", "mm", last)
                for ncol in range(2):
                    po = po0 if ncol == 0 else po1
                    sttc(bft, po, smallB[:, 19:20],
                         pl(BIASB + 512 * ncol, 512), ALU.mult, ALU.add)
                    dst = acc[:, D * t + 512 * ncol:D * t + 512 * (ncol + 1)]
                    ea = tt2(dst, dst, bft, ALU.add, sem="ev")
                    p.wait("vector", "ev", ea)
                    fevd[bb] = ea
            fpend.append(evac)
            if len(fpend) >= 2:
                fpend.pop(0)()
        for f in fpend:
            f()
        p.barrier()

    phase_ln(0, False, dbg.get("d_xn0"))
    phase_l0ffn(0)
    phase_l0attn(0)
    phase_ln(1, True, dbg.get("d_xn1"))
    s1 = p.dma("sync", spill_xn, xn, "dS")
    phase_l1attn(0)
    p.dma("sync", spill_acc[:, :TT * D // 2], acc[:, :TT * D // 2], "dS")
    s2 = p.dma("sync", spill_acc[:, TT * D // 2:], acc[:, TT * D // 2:], "dS")
    p.wait("sync", "dS", s2)
    x1r = x0[1].rearrange("(t p) d -> p t d", p=128)
    p.dma("sync", acc[:, :TT * D // 2], x1r[:, :TT // 2, :], "dX")
    l3 = p.dma("sync", acc[:, TT * D // 2:], x1r[:, TT // 2:, :], "dX")
    p.wait("vector", "dX", l3)
    p.barrier()
    phase_ln(0, False)
    phase_l0ffn(1)
    phase_l0attn(1)
    phase_ln(1, True)
    phase_amax_cc()
    phase_l1attn(1)
    phase_wquant()
    p.barrier()
    emit_scale_from_amax(amax1[:1, 0:1], smallB[:, 18:19], smallB[:, 20:21], "x")
    tt2c(smallB[:, 19:20], smallB[:, 20:21], smallB[:, 17:18], ALU.mult)
    tsc(smallB[:, 19:20], smallB[:, 19:20], 4.0, None, ALU.mult)
    p.barrier()
    phase_l1ffn(1)
    st_b = p.dma("sync", out[1].rearrange("(t p) d -> p t d", p=128), acc, "dS")
    p.wait("sync", "dS", st_b)
    p.dma("sync", xn[:, :NCH * T // 2], spill_xn[:, :NCH * T // 2], "dX")
    p.dma("sync", xn[:, NCH * T // 2:], spill_xn[:, NCH * T // 2:], "dX")
    p.dma("sync", acc[:, :TT * D // 2], spill_acc[:, :TT * D // 2], "dX")
    r2 = p.dma("sync", acc[:, TT * D // 2:], spill_acc[:, TT * D // 2:], "dX")
    p.wait("vector", "dX", r2)
    p.wait("tensor", "dX", r2)
    p.barrier()
    phase_l1ffn(0)
    st_a = p.dma("sync", out[0].rearrange("(t p) d -> p t d", p=128), acc, "dS")
    p.wait("sync", "dS", st_a)
    p.barrier()
    p.build()
    return nc, p


def kernel(**inputs):
    cfg = Cfg()
    cons = host_constants(cfg)
    wts = host_weights(cfg, inputs)
    x = np.ascontiguousarray(np.asarray(inputs["L0_attn_x"], dtype=np.float32))
    nc, p = build_program(cfg)
    base = dict(cons)
    base.update(wts)
    in_maps = []
    for k in range(cfg.n_cores):
        m = dict(base)
        m["x0"] = np.ascontiguousarray(x[2 * k:2 * k + 2])
        in_maps.append(m)
    res = run_bass_kernel_spmd(nc, in_maps, core_ids=list(range(cfg.n_cores)))
    return np.concatenate([r["out"] for r in res.results], axis=0)
